# version 07
# baseline (speedup 1.0000x reference)
import numpy as np
from contextlib import ExitStack

import concourse.bass as bass
import concourse.mybir as mybir
from concourse.alu_op_type import AluOpType as ALU
from concourse.bass_utils import run_bass_kernel_spmd

F32 = mybir.dt.float32
BF16 = mybir.dt.bfloat16
FP8 = mybir.dt.float8e4
AF = mybir.ActivationFunctionType
AX = mybir.AxisListType.X

B, T, D = 8, 2048, 1024
NH, HD = 16, 64
NT = T // 128
FF = 4096
NE, NI = 16, 512
NEG = -1.0e30
M32 = 12582912.0
M64 = 6755399441055744.0

ENGINES = ("tensor", "vector", "scalar", "sync", "gpsimd")


def _bucket_onehot():
    rel = np.arange(4095, dtype=np.int64) - 2047
    nb = 16
    buckets = (rel > 0).astype(np.int64) * nb
    rp = np.abs(rel)
    max_exact = nb // 2
    is_small = rp < max_exact
    large = max_exact + (
        np.log(np.maximum(rp, 1) / max_exact) / np.log(128 / max_exact) * (nb - max_exact)
    ).astype(np.int64)
    large = np.minimum(large, nb - 1)
    buckets = buckets + np.where(is_small, rp, large)
    oh = np.zeros((32, 4095), dtype=np.float32)
    oh[buckets, np.arange(4095)] = 1.0
    return oh


def _l0_masks():
    i = np.arange(128)[:, None]
    j384 = np.arange(384)[None, :]
    j256 = np.arange(256)[None, :]
    inter = np.where((j384 >= i) & (j384 <= i + 256), 0.0, NEG).astype(np.float32)
    first = np.where(j256 <= i + 128, 0.0, NEG).astype(np.float32)
    last = np.where(j256 >= i, 0.0, NEG).astype(np.float32)
    return inter, first, last


class _MockInst:
    def then_inc(self, *a, **k):
        return self


class _MockEng:
    def __init__(self):
        self.calls = []

    def __getattr__(self, name):
        def f(*args, **kwargs):
            self.calls.append((name, args, kwargs))
            return _MockInst()
        return f


def _ap_is_psum(ap):
    try:
        return "psum" in str(getattr(ap, "space", "")).lower()
    except Exception:
        return False


def _ap_keys(ap, gran_sb=512, gran_ps=2048):
    try:
        t = ap.tensor
        name = t.name
        esz = {"float32": 4, "bfloat16": 2, "float16": 2, "int32": 4,
               "float8e4": 1, "int8": 1, "uint8": 1}.get(str(ap.dtype).split(".")[-1], 4)
        dims = [list(d) for d in ap.ap]
        space = str(getattr(ap, "space", ""))
        onchip = ("SB" in space) or ("PSUM" in space) or ("Psum" in space) or \
                 ("State" in space) or ("sb" in space.lower() and "dram" not in space.lower())
        is_ps = "psum" in space.lower()
        if not onchip and "dram" not in space.lower():
            onchip = True
        use = dims[1:] if (onchip and len(dims) > 1) else dims
        off = int(ap.offset)
        if onchip and len(dims) > 1 and dims[0][0] > 0:
            off = off % int(dims[0][0])
        lo = hi = off
        for st, ct in use:
            ext = int(st) * (int(ct) - 1)
            if ext >= 0:
                hi += ext
            else:
                lo += ext
        gran = gran_ps if is_ps else gran_sb
        b0 = (lo * esz) // gran
        b1 = (hi * esz) // gran
        return [f"{name}:{g}" for g in range(b0, b1 + 1)]
    except Exception:
        return []


def _is_ap(x):
    return hasattr(x, "tensor") and hasattr(x, "offset")


class Plan:

    def __init__(self):
        self.ops = []
        self.last_w = {}
        self.readers = {}
        self.eng_pos = {e: [] for e in ENGINES}

    def op(self, engine, fn, reads=(), writes=(), extra_deps=(), tag="", dma=False):
        idx = len(self.ops)
        mock = _MockEng()
        fn(mock)
        rkeys = list(reads)
        wkeys = list(writes)
        pkeys = []
        defer = False
        for name, args, kwargs in mock.calls:
            if name in ("bn_stats", "bn_aggr", "tensor_reduce", "reduce_max",
                        "reduce_sum", "max", "max_index", "tensor_tensor_reduce",
                        "reciprocal"):
                defer = True
                for a in args[1:]:
                    if _is_ap(a):
                        pkeys += _ap_keys(a)
            if "accum_out" in kwargs and kwargs["accum_out"] is not None:
                defer = True
            aps = [a for a in args if _is_ap(a)]
            if aps:
                wkeys += _ap_keys(aps[0])
                for a in aps[1:]:
                    rkeys += _ap_keys(a)
            if name == "tensor_scalar":
                for a in args[2:4]:
                    if _is_ap(a):
                        pkeys += _ap_keys(a)
            elif name == "scalar_tensor_tensor" and len(args) > 2 and _is_ap(args[2]):
                pkeys += _ap_keys(args[2])
            for kw, v in kwargs.items():
                if _is_ap(v):
                    if kw == "accum_out":
                        wkeys += _ap_keys(v)
                    else:
                        rkeys += _ap_keys(v)
                        if kw in ("bias", "scale", "alpha"):
                            pkeys += _ap_keys(v)
        deps = set(extra_deps)
        hard = set()
        for k in rkeys:
            if k in self.last_w:
                d = self.last_w[k]
                deps.add(d)
                if self.ops[d]["defer"]:
                    hard.add(d)
        for k in pkeys:
            if k in self.last_w:
                hard.add(self.last_w[k])
        for k in wkeys:
            if k in self.last_w:
                d = self.last_w[k]
                deps.add(d)
                if self.ops[d]["defer"]:
                    hard.add(d)
            for r in self.readers.get(k, ()):
                deps.add(r)
        deps.discard(idx)
        hard.discard(idx)
        self.ops.append(dict(engine=engine, fn=fn, deps=deps, hard=hard, idx=idx,
                             inc=0, tag=tag, dma=dma, defer=defer))
        self.eng_pos[engine].append(idx)
        for k in wkeys:
            self.last_w[k] = idx
            self.readers[k] = []
        for k in rkeys:
            self.readers.setdefault(k, []).append(idx)
        return idx

    def dma(self, engine, fn, reads=(), writes=(), extra_deps=(), tag=""):
        return self.op(engine, fn, reads, writes, extra_deps, tag, dma=True)

    def materialize(self, nc, ctx):
        ops = self.ops
        need_inc = [False] * len(ops)
        for o in ops:
            for d in o["deps"]:
                if (ops[d]["engine"] != o["engine"] or ops[d]["dma"]
                        or o["engine"] != "tensor"):
                    need_inc[d] = True
            for d in o["hard"]:
                need_inc[d] = True
        sems = {e: ctx.enter_context(nc.semaphore(f"sem_{e}")) for e in ENGINES}
        cum = {e: 0 for e in ENGINES}
        val_after = [0] * len(ops)
        for o in ops:
            e = o["engine"]
            if need_inc[o["idx"]]:
                amt = 16 if o["dma"] else 1
                cum[e] += amt
                o["inc"] = amt
            val_after[o["idx"]] = cum[e]
        hwm = {e: {e2: 0 for e2 in ENGINES} for e in ENGINES}

        def emit_engine(eng_name, e):
            my = hwm[eng_name]
            for idx in self.eng_pos[eng_name]:
                o = ops[idx]
                waits = {}
                for d in o["deps"]:
                    pe = ops[d]["engine"]
                    if (pe == eng_name and not ops[d]["dma"] and d not in o["hard"]
                            and eng_name == "tensor"):
                        continue
                    v = val_after[d]
                    if v > waits.get(pe, 0):
                        waits[pe] = v
                for d in o["hard"]:
                    pe = ops[d]["engine"]
                    v = val_after[d]
                    if v > waits.get(pe, 0):
                        waits[pe] = v
                for pe in sorted(waits):
                    if waits[pe] > my[pe]:
                        e.wait_ge(sems[pe], waits[pe])
                        my[pe] = waits[pe]
                inst = o["fn"](e)
                if o["inc"]:
                    inst.then_inc(sems[eng_name], o["inc"])

        blk = ctx.enter_context(nc.Block())
        for name in ENGINES:
            getattr(blk, name)(lambda e, _n=name: emit_engine(_n, e))
        return sems


def build_program(upto=99, debug=False):
    nc = bass.Bass("TRN2", target_bir_lowering=False, detect_race_conditions=False)
    P = Plan()
    ctx = ExitStack()

    di = {}
    def din(name, shape, dt=F32):
        di[name] = nc.dram_tensor(name, list(shape), dt, kind="ExternalInput").ap()
        return di[name]

    x_in = din("x", (T, D))
    w_in = din("w_in", (3 * D, D)); b_in = din("b_in", (3 * D,))
    w_out = din("w_out", (D, D)); b_out = din("b_out", (D,))
    f_w1 = din("f_w1", (FF, D)); f_b1 = din("f_b1", (FF,))
    f_w2 = din("f_w2", (D, FF)); f_b2 = din("f_b2", (D,))
    a_w = {s: din(f"a_w{s}", (D, D)) for s in "qkvo"}
    a_b = {s: din(f"a_b{s}", (D,)) for s in "qkvo"}
    rel_e = din("rel_e", (32, 16))
    gru_w = din("gru_w", (64, 8)); gru_b = din("gru_b", (8,))
    gru_c = din("gru_c", (16,))
    gru_c2 = din("gru_c2", (2, 8))
    r_w = din("r_w", (D, NE))
    m_gu = din("m_gu", (NE, D, 2 * NI)); m_gub = din("m_gub", (NE, 2 * NI))
    m_dn = din("m_dn", (NE, NI, D)); m_dnb = din("m_dnb", (NE, D))
    c_oh = din("c_oh", (32, 4095))
    c_mi = din("c_mi", (128, 384), BF16); c_mf = din("c_mf", (128, 256), BF16); c_ml = din("c_ml", (128, 256), BF16)
    c_id32 = din("c_id32", (128, 128))
    c_idbf = din("c_idbf", (128, 128), BF16)
    c_es2 = din("c_es2", (128, 2), BF16)
    c_p4 = din("c_p4", (8, 2))

    y_out = nc.dram_tensor("y", [T, D], F32, kind="ExternalOutput").ap()
    pb_dram = nc.dram_tensor("pb_scratch", [NH, 4608], F32, kind="Internal").ap()

    def sbt(name, shape, dt=F32):
        return ctx.enter_context(nc.sbuf_tensor(name, list(shape), dt))

    XRES = sbt("XRES", (128, NT * D))
    XLNF = [sbt(f"XLNF{c}", (128, T), BF16) for c in range(8)]
    CTXF = [sbt(f"CTXF{c}", (128, T), BF16) for c in range(8)]
    ABIG = sbt("ABIG", (128, 4096))
    PT = [sbt(f"PT{c}", (128, 256), BF16) for c in range(16)]
    QKV = sbt("QKV", (128, 3 * T), BF16)
    VT = sbt("VT", (128, T), BF16)
    WSL = sbt("WSL", (128, 3072), BF16)
    SCR = sbt("SCR", (128, 2048))
    PBF = sbt("PBF", (128, 2048), BF16)
    ROWS = sbt("ROWS", (1, 6144), BF16)
    ID32 = sbt("ID32", (128, 128))
    IDBF = sbt("IDBF", (128, 128), BF16)
    ES2 = sbt("ES2", (128, 2), BF16)
    STAT = sbt("STAT", (128, 64))
    SMALL = sbt("SMALL", (128, 128))
    BT = sbt("BT", (128, 256))
    GT = sbt("GT", (128, 256))
    MK = sbt("MK", (128, 896), BF16)
    CTM = sbt("CTM", (128, 256), BF16)
    ONER = sbt("ONER", (1, 512), BF16)
    U1R = sbt("U1R", (1, 256), BF16)
    P75 = sbt("P75", (128, 1)); PDET = sbt("PDET", (128, 1)); PM1 = sbt("PM1", (128, 1))
    FROW = sbt("FROW", (1, 128))
    GAB = sbt("GAB", (128, 4), BF16)
    CEPS = sbt("CEPS", (128, 4))

    S_ps = ctx.enter_context(nc.psum_tensor("S_ps", [128, 2048], F32))
    TR_ps = ctx.enter_context(nc.psum_tensor("TR_ps", [128, 512], F32))
    AV_ps = ctx.enter_context(nc.psum_tensor("AV_ps", [128, 512], F32))
    AX_ps = ctx.enter_context(nc.psum_tensor("AX_ps", [128, 512], F32))
    AY_ps = ctx.enter_context(nc.psum_tensor("AY_ps", [128, 512], F32))
    TRbf = TR_ps[:].bitcast(BF16)
    AXbf = AX_ps[:].bitcast(BF16)
    AVbf = AV_ps[:].bitcast(BF16)

    SKEY = ["S0", "S1", "S2", "S3"]

    P.dma("sync", lambda e: e.dma_start(ID32[:], c_id32), writes=["ID32"])
    P.dma("sync", lambda e: e.dma_start(IDBF[:], c_idbf), writes=["IDBF"])
    P.dma("sync", lambda e: e.dma_start(ES2[:], c_es2), writes=["ES2"])
    P.dma("sync", lambda e: e.dma_start(MK[:, 0:384], c_mi), writes=["MK"])
    P.dma("sync", lambda e: e.dma_start(MK[:, 384:640], c_mf), writes=["MK"])
    P.dma("sync", lambda e: e.dma_start(MK[:, 640:896], c_ml), writes=["MK"])
    for t in range(NT):
        P.dma("sync", lambda e, t=t: e.dma_start(
            XRES[:, t * D:(t + 1) * D], x_in[t * 128:(t + 1) * 128, :]),
            writes=[f"xr{t}"])
    P.op("vector", lambda e: e.memset(P75[:], 0.75), writes=["P75"])
    P.op("vector", lambda e: e.memset(CEPS[:, 0:1], 1e-5), writes=["ceps"])
    P.op("vector", lambda e: e.memset(CEPS[:, 1:2], 1e-30), writes=["ceps"])
    P.op("vector", lambda e: e.memset(ONER[:], 1.0), writes=["ONER"])
    P.op("vector", lambda e: e.tensor_scalar(PDET[:], P75[:], M64, M64, ALU.add, ALU.subtract),
         reads=["P75"], writes=["PDET"])
    P.op("vector", lambda e: e.tensor_scalar(PM1[:], PDET[:], -1.0, 1.0, ALU.mult, ALU.add),
         reads=["PDET"], writes=["PM1"])

    def emit_ln(blk, fm=True):
        for t in range(NT):
            xa = XRES[:, t * D:(t + 1) * D]
            mcol = STAT[:, t:t + 1]
            rcol = STAT[:, 16 + t:17 + t]
            bcol = STAT[:, 32 + t:33 + t]
            P.op("vector", lambda e, xa=xa: e.bn_stats(SMALL[:, 0:6], xa[:, 0:512]),
                 reads=[f"xr{t}"], writes=["bnA"])
            P.op("vector", lambda e, xa=xa: e.bn_stats(SMALL[:, 6:12], xa[:, 512:1024]),
                 reads=[f"xr{t}"], writes=["bnB"])
            P.op("vector", lambda e: e.bn_aggr(SMALL[:, 16:18], SMALL[:, 0:12]),
                 reads=["bnA", "bnB"], writes=["mv"])
            P.op("vector", lambda e, mcol=mcol: e.tensor_copy(mcol, SMALL[:, 16:17]),
                 reads=["mv"], writes=[f"stm{t}"])
            P.op("scalar", lambda e: e.activation(SMALL[:, 18:19], SMALL[:, 17:18],
                                                  AF.Sqrt, bias=CEPS[:, 0:1]),
                 reads=["mv", "ceps"], writes=["sd"])
            P.op("vector", lambda e, rcol=rcol: e.reciprocal(rcol, SMALL[:, 18:19]),
                 reads=["sd"], writes=[f"str{t}"])
            P.op("vector", lambda e, mcol=mcol, rcol=rcol, bcol=bcol: e.tensor_scalar(
                bcol, mcol, rcol, -1.0, ALU.mult, ALU.mult),
                 reads=[f"stm{t}", f"str{t}"], writes=[f"stb{t}"])
            if fm:
                P.op("scalar", lambda e, xa=xa, rcol=rcol, bcol=bcol: e.activation(
                    PBF[:, 0:1024], xa, AF.Identity, bias=bcol, scale=rcol),
                     reads=[f"xr{t}", f"str{t}", f"stb{t}"], writes=["ps0", "ps1"])
                for c in range(8):
                    P.op("tensor", lambda e, c=c: e.transpose(
                        TRbf[:, c * 128:(c + 1) * 128], PBF[:, c * 128:(c + 1) * 128], IDBF[:]),
                         reads=["ps0", "ps1", "IDBF"], writes=["bkTR"])
                for c in range(8):
                    P.op("vector", lambda e, c=c, t=t: e.tensor_copy(
                        XLNF[c][:, t * 128:(t + 1) * 128], TRbf[:, c * 128:(c + 1) * 128]),
                         reads=["bkTR"], writes=[f"xlnf{c}"])

    def tr_f32_to(dst_ap_fn, src_key, dst_key):
        for c in range(8):
            P.op("tensor", lambda e, c=c: e.transpose(
                TR_ps[:, (c % 4) * 128:(c % 4 + 1) * 128],
                ABIG[:, c * 128:(c + 1) * 128], ID32[:]),
                 reads=[src_key, "ID32"], writes=["bkTR"])
            P.op("scalar", lambda e, c=c: e.copy(
                dst_ap_fn(c), TR_ps[:, (c % 4) * 128:(c % 4 + 1) * 128]),
                 reads=["bkTR"], writes=[dst_key])

    def emit_l0_attn():
        P.dma("gpsimd", lambda e: e.dma_start(ROWS[0:1, 0:3072], b_in.unsqueeze(0)),
              writes=["rows_bin"])
        P.dma("gpsimd", lambda e: e.dma_start(ROWS[0:1, 3072:4096], b_out.unsqueeze(0)),
              writes=["rows_bout"])

        def qt_mm_phase(hp, qt):
            hA = 2 * hp
            lo = max(0, qt * 128 - 128)
            hi = min(T, qt * 128 + 256)
            w = hi - lo
            for h01 in range(2):
                sbank = h01 * 2 + qt % 2
                scol = sbank * 512
                P.op("tensor", lambda e, h01=h01, qt=qt, lo=lo, w=w, scol=scol: e.matmul(
                    S_ps[:, scol:scol + w],
                    QKV[h01 * 64:(h01 + 1) * 64, qt * 128:(qt + 1) * 128],
                    QKV[h01 * 64:(h01 + 1) * 64, T + lo: T + lo + w],
                    start=True, stop=True),
                     reads=["qkvA", "qkvB"], writes=[SKEY[sbank]], tag=f"s{hp}_{qt}_{h01}")

        def qt_tail_phase(hp, qt):
            hA = 2 * hp
            lo = max(0, qt * 128 - 128)
            hi = min(T, qt * 128 + 256)
            w = hi - lo
            nb = w // 128
            mofs = 384 if qt == 0 else (640 if qt == NT - 1 else 0)
            TB = TRbf if qt % 2 == 0 else AXbf
            tbk = "bkTR" if qt % 2 == 0 else "bkAX"
            for h01 in range(2):
                h = hA + h01
                sbank = h01 * 2 + qt % 2
                scol = sbank * 512
                pslot = sbank * 512
                P.op("vector", lambda e, scol=scol, w=w, mofs=mofs: e.tensor_tensor(
                    S_ps[:, scol:scol + w], S_ps[:, scol:scol + w], MK[:, mofs:mofs + w],
                    ALU.add),
                     reads=[SKEY[sbank], "MK"], writes=[SKEY[sbank]])
                zc = 24 + sbank
                P.op("scalar", lambda e, scol=scol, w=w, pslot=pslot, qt=qt, h=h, zc=zc:
                     e.activation(PBF[:, pslot:pslot + w], S_ps[:, scol:scol + w], AF.Exp,
                                  bias=BT[:, qt * 16 + h: qt * 16 + h + 1],
                                  accum_out=SMALL[:, zc:zc + 1]),
                     reads=[SKEY[sbank], f"bt{qt}"], writes=[f"ps{sbank}", f"z{sbank}"])
                P.op("vector", lambda e, zc=zc: e.reciprocal(
                    SMALL[:, zc + 4:zc + 5], SMALL[:, zc:zc + 1]),
                     reads=[f"z{sbank}"], writes=[f"zr{sbank}"])
                for bkt in range(nb):
                    P.op("tensor", lambda e, pslot=pslot, bkt=bkt, h01=h01, TB=TB: e.transpose(
                        TB[:, (h01 * 3 + bkt) * 128:(h01 * 3 + bkt + 1) * 128],
                        PBF[:, pslot + bkt * 128: pslot + (bkt + 1) * 128], IDBF[:]),
                         reads=[f"ps{sbank}", "IDBF"], writes=[tbk])
                for bkt in range(nb):
                    P.op("vector", lambda e, bkt=bkt, h01=h01, TB=TB: e.tensor_copy(
                        PT[h01 * 3 + bkt][:, 0:128],
                        TB[:, (h01 * 3 + bkt) * 128:(h01 * 3 + bkt + 1) * 128]),
                         reads=[tbk], writes=[f"ptb{h01}_{bkt}"])
                avk = "AV" if h01 == 0 else "AY"
                avp = AV_ps if h01 == 0 else AY_ps
                for bkt in range(nb):
                    ktg = lo // 128 + bkt
                    P.op("tensor", lambda e, bkt=bkt, h01=h01, ktg=ktg, avp=avp, nb=nb: e.matmul(
                        avp[:, 0:64], PT[h01 * 3 + bkt][:, 0:128],
                        VT[:, ktg * 128 + h01 * 64: ktg * 128 + h01 * 64 + 64],
                        start=(bkt == 0), stop=(bkt == nb - 1)),
                         reads=[f"ptb{h01}_{bkt}", "vt"], writes=[avk])
                cslot = sbank * 64
                P.op("vector", lambda e, avp=avp, zc=zc, cslot=cslot: e.tensor_scalar(
                    CTM[:, cslot:cslot + 64], avp[:, 0:64],
                    SMALL[:, zc + 4:zc + 5], None, ALU.mult),
                     reads=[avk, f"zr{sbank}"], writes=[f"ctm{sbank}"])
                tcol = 768 if h01 == 0 else 896
                P.op("tensor", lambda e, cslot=cslot, TB=TB, tcol=tcol: e.transpose(
                    TB[0:64, tcol:tcol + 128], CTM[:, cslot:cslot + 64], IDBF[:]),
                     reads=[f"ctm{sbank}", "IDBF"], writes=[tbk])
                P.op("vector", lambda e, h01=h01, qt=qt, hp=hp, TB=TB, tcol=tcol: e.tensor_copy(
                    CTXF[hp][h01 * 64:(h01 + 1) * 64, qt * 128:(qt + 1) * 128],
                    TB[0:64, tcol:tcol + 128]),
                     reads=[tbk], writes=[f"ctxf{hp}"])

        for hp in range(8):
            hA = 2 * hp
            for si, s in enumerate("qkv"):
                P.dma("sync", lambda e, si=si, hp=hp: e.dma_start(
                    ABIG[:, 0:1024], w_in[si * D + hp * 128: si * D + hp * 128 + 128, :]),
                      writes=["abig"])
                tr_f32_to(lambda c, si=si: WSL[:, si * 1024 + c * 128: si * 1024 + (c + 1) * 128],
                          "abig", f"wsl{s}")
            for si, s in enumerate("qkv"):
                for tc4 in range(4):
                    for c in range(8):
                        P.op("tensor", lambda e, si=si, c=c, tc4=tc4: e.matmul(
                            AX_ps[:, 0:512],
                            WSL[:, si * 1024 + c * 128: si * 1024 + (c + 1) * 128],
                            XLNF[c][:, tc4 * 512:(tc4 + 1) * 512],
                            start=(c == 0), stop=False),
                             reads=[f"wsl{s}", f"xlnf{c}"], writes=["bkAX"])
                    P.op("tensor", lambda e, si=si, hp=hp: e.matmul(
                        AX_ps[:, 0:512],
                        ROWS[0:1, si * 1024 + hp * 128: si * 1024 + (hp + 1) * 128],
                        ONER[0:1, 0:512], start=False, stop=True),
                         reads=["rows_bin", "ONER"], writes=["bkAX"])
                    P.op("scalar", lambda e, si=si, tc4=tc4, s=s: e.activation(
                        QKV[:, si * T + tc4 * 512: si * T + (tc4 + 1) * 512],
                        AX_ps[:, 0:512], AF.Identity, scale=(0.125 if si == 0 else 1.0)),
                         reads=["bkAX"], writes=[["qkvA", "qkvB", "qkvC"][si]])
            for kt in range(16):
                P.op("tensor", lambda e, kt=kt: e.transpose(
                    TRbf[:, (kt % 8) * 128:(kt % 8 + 1) * 128],
                    QKV[:, 2 * T + kt * 128: 2 * T + (kt + 1) * 128], IDBF[:]),
                     reads=["qkvC", "IDBF"], writes=["bkTR"])
                P.op("vector", lambda e, kt=kt: e.tensor_copy(
                    VT[:, kt * 128:(kt + 1) * 128], TRbf[:, (kt % 8) * 128:(kt % 8 + 1) * 128]),
                     reads=["bkTR"], writes=["vt"])
            P.op("scalar", lambda e: e.activation(PBF[:, 0:2048], QKV[:, 0:T], AF.Square),
                 reads=["qkvA"], writes=["ps0", "ps1", "ps2", "ps3"])
            for tc4 in range(4):
                P.op("tensor", lambda e, tc4=tc4: e.matmul(
                    S_ps[0:2, tc4 * 512:(tc4 + 1) * 512], ES2[:],
                    PBF[:, tc4 * 512:(tc4 + 1) * 512], start=True, stop=True),
                     reads=["ps0", "ps1", "ps2", "ps3", "ES2"], writes=[SKEY[tc4]])
            P.op("scalar", lambda e: e.activation(SCR[0:2, 0:2048], S_ps[0:2, 0:2048],
                                                  AF.Sqrt, bias=CEPS[0:2, 1:2]),
                 reads=SKEY + ["ceps"], writes=["scrA", "scrB"])
            P.op("scalar", lambda e: e.activation(PBF[:, 0:2048], QKV[:, T:2 * T], AF.Square),
                 reads=["qkvB"], writes=["ps0", "ps1", "ps2", "ps3"])
            for tc4 in range(4):
                P.op("tensor", lambda e, tc4=tc4: e.matmul(
                    S_ps[0:2, tc4 * 512:(tc4 + 1) * 512], ES2[:],
                    PBF[:, tc4 * 512:(tc4 + 1) * 512], start=True, stop=True),
                     reads=["ps0", "ps1", "ps2", "ps3", "ES2"], writes=[SKEY[tc4]])
            P.op("vector", lambda e: e.tensor_reduce(SMALL[0:2, 20:21], S_ps[0:2, 0:2048],
                                                     AX, ALU.max),
                 reads=SKEY, writes=["kmax2"])
            P.op("scalar", lambda e: e.activation(SMALL[0:2, 21:22], SMALL[0:2, 20:21],
                                                  AF.Sqrt, bias=CEPS[0:2, 1:2]),
                 reads=["kmax2", "ceps"], writes=["kmax"])
            P.op("vector", lambda e: e.tensor_scalar(
                SCR[0:2, 0:2048], SCR[0:2, 0:2048], SMALL[0:2, 21:22], -1.0,
                ALU.mult, ALU.mult),
                 reads=["scrA", "scrB", "kmax"], writes=["negB", "scrA", "scrB"])
            for qt in range(NT):
                P.op("tensor", lambda e, qt=qt: e.transpose(
                    TR_ps[:, 0:2], SCR[0:2, qt * 128:(qt + 1) * 128], ID32[0:2, 0:2]),
                     reads=["negB", "scrA", "scrB", "ID32"], writes=["bkTR"])
                P.op("vector", lambda e, qt=qt, hA=hA: e.tensor_copy(
                    BT[:, qt * 16 + hA: qt * 16 + hA + 2], TR_ps[:, 0:2]),
                     reads=["bkTR"], writes=[f"bt{qt}"])
            for qt in range(NT + 1):
                if qt < NT:
                    qt_mm_phase(hp, qt)
                if qt > 0:
                    qt_tail_phase(hp, qt - 1)

        def wot_ap(cc):
            if cc < 6:
                return QKV[:, cc * 1024:(cc + 1) * 1024]
            return WSL[:, (cc - 6) * 1024:(cc - 5) * 1024]

        def wot_alias(cc):
            return ["qkvA", "qkvA", "qkvB", "qkvB", "qkvC", "qkvC", "wslq", "wslk"][cc]
        for c in range(8):
            P.dma("sync", lambda e, c=c: e.dma_start(
                ABIG[:, 0:1024], w_out[c * 128:(c + 1) * 128, :]), writes=["abig"])
            for cc in range(8):
                P.op("tensor", lambda e, cc=cc: e.transpose(
                    TR_ps[:, (cc % 4) * 128:(cc % 4 + 1) * 128],
                    ABIG[:, cc * 128:(cc + 1) * 128], ID32[:]),
                     reads=["abig", "ID32"], writes=["bkTR"])
                P.op("scalar", lambda e, c=c, cc=cc: e.copy(
                    wot_ap(cc)[:, c * 128:(c + 1) * 128],
                    TR_ps[:, (cc % 4) * 128:(cc % 4 + 1) * 128]),
                     reads=["bkTR"], writes=["wotx", wot_alias(cc)])
        for t in range(NT):
            pk = ["S0", "S1"]
            for c in range(8):
                for half in range(2):
                    P.op("tensor", lambda e, c=c, t=t, half=half: e.matmul(
                        S_ps[:, half * 512:(half + 1) * 512],
                        CTXF[c][:, t * 128:(t + 1) * 128],
                        wot_ap(c)[:, half * 512:(half + 1) * 512],
                        start=(c == 0), stop=False),
                         reads=[f"ctxf{c}", "wotx"], writes=pk)
            for half in range(2):
                P.op("tensor", lambda e, half=half: e.matmul(
                    S_ps[:, half * 512:(half + 1) * 512], ONER[0:1, 0:128],
                    ROWS[0:1, 3072 + half * 512: 3072 + (half + 1) * 512],
                    start=False, stop=True),
                     reads=["ONER", "rows_bout"], writes=pk)
            P.op("vector", lambda e, t=t: e.tensor_scalar(
                SMALL[:, 62:63], STAT[:, 16 + t:17 + t], 1.0, None, ALU.add),
                 reads=[f"str{t}"], writes=["pr1"])
            P.op("scalar", lambda e, t=t: e.activation(
                SCR[:, 0:1024], XRES[:, t * D:(t + 1) * D], AF.Identity,
                bias=STAT[:, 32 + t:33 + t], scale=SMALL[:, 62:63]),
                 reads=[f"xr{t}", f"stb{t}", "pr1"], writes=["scrA"])
            P.op("vector", lambda e, t=t: e.tensor_tensor(
                XRES[:, t * D:(t + 1) * D], SCR[:, 0:1024], S_ps[:, 0:1024], ALU.add),
                 reads=["scrA"] + pk, writes=[f"xr{t}"])

    def emit_l0_ffn():
        P.dma("gpsimd", lambda e: e.dma_start(ROWS[0:1, 0:4096], f_b1.unsqueeze(0)),
              writes=["rows_bin", "rows_bout"])
        P.dma("gpsimd", lambda e: e.dma_start(ROWS[0:1, 4096:5120], f_b2.unsqueeze(0)),
              writes=["rows_bout"])

        P.op("vector", lambda e: e.memset(SMALL[:, 120:122], 0.0), writes=["fac"])
        P.op("vector", lambda e: e.memset(CEPS[:, 2:3], 1.0), writes=["ceps1"])
        for wi, (wap, nch) in enumerate(((f_w1, 8), (f_w2, 8))):
            for ch in range(nch):
                if wi == 0:
                    src_ap = wap[ch * 512:(ch + 1) * 512, :].rearrange(
                        "(a p) d -> p a d", p=128)
                    dst_ap = ABIG[:].rearrange("p (a d) -> p a d", a=4)
                else:
                    src_ap = wap[ch * 128:(ch + 1) * 128, :]
                    dst_ap = ABIG[:]
                P.dma("sync", lambda e, src_ap=src_ap, dst_ap=dst_ap: e.dma_start(
                    dst_ap, src_ap), writes=["abig"])
                P.op("vector", lambda e: e.tensor_reduce(
                    SMALL[:, 122:123], ABIG[:], AX, ALU.add, apply_absolute_value=True),
                     reads=["abig"], writes=["fred"])
                P.op("vector", lambda e, wi=wi: e.tensor_tensor(
                    SMALL[:, 120 + wi:121 + wi], SMALL[:, 120 + wi:121 + wi],
                    SMALL[:, 122:123], ALU.add),
                     reads=["fred", "fac"], writes=["fac"])
        P.op("tensor", lambda e: e.matmul(TR_ps[0:1, 0:2], CEPS[:, 2:3], SMALL[:, 120:122],
                                          start=True, stop=True),
             reads=["fac", "ceps1"], writes=["bkTR"])
        P.op("vector", lambda e: e.tensor_scalar(
            SCR[0:1, 0:2], TR_ps[0:1, 0:2], 1.0 / (FF * D), 1e-5, ALU.mult, ALU.max),
             reads=["bkTR"], writes=["fc12", "scrA"])
        P.op("vector", lambda e: e.reciprocal(SCR[0:1, 2:4], SCR[0:1, 0:2]),
             reads=["fc12"], writes=["fsw12", "scrA"])
        P.op("vector", lambda e: e.memset(FROW[:], 1.0), writes=["fonesr"])
        P.op("tensor", lambda e: e.matmul(TR_ps[:, 4:8], FROW[:], SCR[0:1, 0:4],
                                          start=True, stop=True),
             reads=["fc12", "fsw12", "fonesr", "scrA"], writes=["bkTR"])
        P.op("vector", lambda e: e.tensor_copy(SMALL[:, 123:127], TR_ps[:, 4:8]),
             reads=["bkTR"], writes=["fswb"])
        C1C, C2C = SMALL[:, 123:124], SMALL[:, 124:125]
        SW1C, SW2C = SMALL[:, 125:126], SMALL[:, 126:127]

        def tern_quarter(load_dma, swc, dst_fn, dst_key_fn):
            load_dma()
            P.op("vector", lambda e, swc=swc: e.tensor_scalar(
                SCR[:, 0:1024], ABIG[:, 0:1024], swc, 0.5, ALU.mult, ALU.is_ge),
                 reads=["abig", "fswb"], writes=["scrA"])
            P.op("vector", lambda e, swc=swc: e.tensor_scalar(
                SCR[:, 1024:2048], ABIG[:, 0:1024], swc, -0.5, ALU.mult, ALU.is_le),
                 reads=["abig", "fswb"], writes=["scrB"])
            P.op("vector", lambda e: e.tensor_tensor(
                PBF[:, 0:1024], SCR[:, 0:1024], SCR[:, 1024:2048], ALU.subtract),
                 reads=["scrA", "scrB"], writes=["ps0", "ps1"])
            for c in range(8):
                P.op("tensor", lambda e, c=c: e.transpose(
                    TRbf[:, c * 128:(c + 1) * 128], PBF[:, c * 128:(c + 1) * 128], IDBF[:]),
                     reads=["ps0", "ps1", "IDBF"], writes=["bkTR"])
            for c in range(8):
                P.op("scalar", lambda e, c=c: e.copy(
                    dst_fn(c), TRbf[:, c * 128:(c + 1) * 128]),
                     reads=["bkTR"], writes=[dst_key_fn(c)])

        for ch in range(32):
            tern_quarter(
                lambda ch=ch: P.dma("sync", lambda e: e.dma_start(
                    ABIG[:, 0:1024], f_w1[ch * 128:(ch + 1) * 128, :]), writes=["abig"]),
                SW1C,
                lambda c, ch=ch: XLNF[c][:].bitcast(FP8)[:, ch * 128:(ch + 1) * 128],
                lambda c: f"xlnf{c}")
        for ch in range(8):
            for qr in range(4):
                def dst(c, ch=ch, qr=qr):
                    fc = qr * 8 + c
                    return CTXF[fc // 4][:].bitcast(FP8)[
                        :, (fc % 4) * 1024 + ch * 128: (fc % 4) * 1024 + (ch + 1) * 128]
                tern_quarter(
                    lambda ch=ch, qr=qr: P.dma("sync", lambda e: e.dma_start(
                        ABIG[:, 0:1024],
                        f_w2[ch * 128:(ch + 1) * 128, qr * 1024:(qr + 1) * 1024]),
                        writes=["abig"]),
                    SW2C, dst, lambda c, qr=qr: f"ctxf{(qr * 8 + c) // 4}")

        def W1T(c):
            return XLNF[c][:].bitcast(FP8)
        def W2T(fc):
            return CTXF[fc // 4][:].bitcast(FP8)[:, (fc % 4) * 1024:(fc % 4 + 1) * 1024]

        for tc in range(NT if upto >= 2 else 0):
            rcol = STAT[:, 16 + tc:17 + tc]
            mcol = STAT[:, tc:tc + 1]
            xa = XRES[:, tc * D:(tc + 1) * D]
            P.op("vector", lambda e, rcol=rcol: e.tensor_tensor(
                SMALL[:, 40:41], rcol, rcol, ALU.mult), reads=[f"str{tc}"], writes=["f40"])
            P.op("vector", lambda e: e.tensor_scalar(
                SMALL[:, 41:42], SMALL[:, 40:41], -1e-5, 1.0 + 1e-5, ALU.mult, ALU.add),
                 reads=["f40"], writes=["f41"])
            P.op("scalar", lambda e: e.activation(SMALL[:, 42:43], SMALL[:, 41:42], AF.Sqrt),
                 reads=["f41"], writes=["f42"])
            P.op("vector", lambda e: e.reciprocal(SMALL[:, 43:44], SMALL[:, 42:43]),
                 reads=["f42"], writes=["f43"])
            P.op("vector", lambda e, xa=xa: e.tensor_reduce(
                SMALL[:, 44:45], xa, AX, ALU.max), reads=[f"xr{tc}"], writes=["f44"])
            P.op("vector", lambda e, xa=xa: e.tensor_reduce(
                SMALL[:, 45:46], xa, AX, ALU.min), reads=[f"xr{tc}"], writes=["f45"])
            P.op("vector", lambda e, mcol=mcol: e.tensor_tensor(
                SMALL[:, 44:45], SMALL[:, 44:45], mcol, ALU.subtract),
                 reads=["f44", f"stm{tc}"], writes=["f44"])
            P.op("vector", lambda e, mcol=mcol: e.tensor_tensor(
                SMALL[:, 45:46], mcol, SMALL[:, 45:46], ALU.subtract),
                 reads=["f45", f"stm{tc}"], writes=["f45"])
            P.op("vector", lambda e: e.tensor_tensor(
                SMALL[:, 44:45], SMALL[:, 44:45], SMALL[:, 45:46], ALU.max),
                 reads=["f44", "f45"], writes=["f44"])
            P.op("vector", lambda e, rcol=rcol: e.tensor_tensor(
                SMALL[:, 45:46], SMALL[:, 44:45], rcol, ALU.mult),
                 reads=["f44", f"str{tc}"], writes=["f45"])
            P.op("vector", lambda e: e.tensor_scalar(
                SMALL[:, 46:47], SMALL[:, 45:46], SMALL[:, 43:44], 1e-5, ALU.mult, ALU.max),
                 reads=["f45", "f43"], writes=["f46"])
            P.op("vector", lambda e: e.reciprocal(SMALL[:, 47:48], SMALL[:, 46:47]),
                 reads=["f46"], writes=["f47"])
            P.op("vector", lambda e: e.tensor_scalar(
                SMALL[:, 47:48], SMALL[:, 47:48], 127.0, None, ALU.mult),
                 reads=["f47"], writes=["f47"])
            P.op("vector", lambda e, rcol=rcol: e.tensor_tensor(
                SMALL[:, 48:49], SMALL[:, 43:44], rcol, ALU.mult),
                 reads=["f43", f"str{tc}"], writes=["f48"])
            P.op("vector", lambda e: e.tensor_tensor(
                SMALL[:, 48:49], SMALL[:, 48:49], SMALL[:, 47:48], ALU.mult),
                 reads=["f48", "f47"], writes=["f48"])
            P.op("vector", lambda e, mcol=mcol: e.tensor_scalar(
                SMALL[:, 49:50], mcol, SMALL[:, 48:49], -1.0, ALU.mult, ALU.mult),
                 reads=[f"stm{tc}", "f48"], writes=["f49"])
            P.op("vector", lambda e: e.tensor_scalar(
                SMALL[:, 50:51], SMALL[:, 46:47], 1.0 / 127.0, C1C, ALU.mult, ALU.mult),
                 reads=["f46", "fswb"], writes=["f50"])
            P.op("vector", lambda e: e.tensor_scalar(
                SMALL[:, 51:52], SMALL[:, 47:48], SW1C, None, ALU.mult),
                 reads=["f47", "fswb"], writes=["f51"])
            P.op("tensor", lambda e: e.transpose(TR_ps[0:1, 16:144], SMALL[:, 51:52], ID32[:]),
                 reads=["f51", "ID32"], writes=["bkTR"])
            P.op("vector", lambda e: e.tensor_copy(U1R[0:1, 0:128], TR_ps[0:1, 16:144]),
                 reads=["bkTR"], writes=["u1r"])
            P.op("scalar", lambda e, xa=xa: e.activation(
                SCR[:, 0:1024], xa, AF.Identity, bias=SMALL[:, 49:50], scale=SMALL[:, 48:49]),
                 reads=[f"xr{tc}", "f48", "f49"], writes=["scrA"])
            P.op("vector", lambda e: e.tensor_scalar(
                SCR[:, 1024:2048], SCR[:, 0:1024], M32, M32, ALU.add, ALU.subtract),
                 reads=["scrA"], writes=["scrB"])
            P.op("vector", lambda e: e.tensor_scalar(
                SCR[:, 0:1024], SCR[:, 0:1024], M64, M64, ALU.add, ALU.subtract),
                 reads=["scrA"], writes=["scrA"])
            P.op("vector", lambda e: e.tensor_tensor(
                SCR[:, 0:1024], SCR[:, 0:1024], SCR[:, 1024:2048], ALU.subtract),
                 reads=["scrA", "scrB"], writes=["scrA"])
            P.op("vector", lambda e: e.scalar_tensor_tensor(
                PBF[:, 0:1024], SCR[:, 0:1024], PDET[:, 0:1], SCR[:, 1024:2048],
                ALU.mult, ALU.add),
                 reads=["scrA", "scrB", "PDET"], writes=["ps0", "ps1"])
            for c in range(8):
                P.op("tensor", lambda e, c=c: e.transpose(
                    TRbf[:, c * 128:(c + 1) * 128], PBF[:, c * 128:(c + 1) * 128], IDBF[:]),
                     reads=["ps0", "ps1", "IDBF"], writes=["bkTR"])
            for c in range(8):
                P.op("vector", lambda e, c=c: e.tensor_copy(
                    QKV[:, c * 128:(c + 1) * 128], TRbf[:, c * 128:(c + 1) * 128]),
                     reads=["bkTR"], writes=["qkvA"])
            P.op("vector", lambda e: e.memset(SMALL[:, 114:115], 0.0), writes=["fmx"])
            for fs in range(8):
                bk = fs % 4
                for c in range(8):
                    P.op("tensor", lambda e, c=c, fs=fs, bk=bk: e.matmul(
                        S_ps[:, bk * 512:bk * 512 + 512],
                        QKV[:, c * 128:(c + 1) * 128],
                        W1T(c)[:, fs * 512:(fs + 1) * 512],
                        start=(c == 0), stop=False),
                         reads=["qkvA", f"xlnf{c}"], writes=[SKEY[bk]])
                P.op("tensor", lambda e, fs=fs, bk=bk: e.matmul(
                    S_ps[:, bk * 512:bk * 512 + 512], U1R[0:1, 0:128],
                    ROWS[0:1, fs * 512:(fs + 1) * 512], start=False, stop=True),
                     reads=["u1r", "rows_bin"], writes=[SKEY[bk]])
                P.op("scalar", lambda e, fs=fs, bk=bk: e.activation(
                    ABIG[:, fs * 512:(fs + 1) * 512], S_ps[:, bk * 512:bk * 512 + 512],
                    AF.Gelu, scale=SMALL[:, 50:51]),
                     reads=[SKEY[bk], "f50"], writes=["abig"])
                P.op("vector", lambda e, fs=fs: e.bn_stats(
                    SMALL[:, 64 + 6 * fs:70 + 6 * fs], ABIG[:, fs * 512:(fs + 1) * 512]),
                     reads=["abig"], writes=[f"fbn{fs}"])
                P.op("vector", lambda e, fs=fs: e.tensor_reduce(
                    SMALL[:, 115:116], ABIG[:, fs * 512:(fs + 1) * 512], AX, ALU.max,
                    apply_absolute_value=True),
                     reads=["abig"], writes=["fred2"])
                P.op("vector", lambda e: e.tensor_tensor(
                    SMALL[:, 114:115], SMALL[:, 114:115], SMALL[:, 115:116], ALU.max),
                     reads=["fred2", "fmx"], writes=["fmx"])
            P.op("vector", lambda e: e.bn_aggr(SMALL[:, 112:114], SMALL[:, 64:112]),
                 reads=[f"fbn{fs}" for fs in range(8)], writes=["fmv2"])
            P.op("vector", lambda e: e.tensor_tensor(
                SMALL[:, 52:53], SMALL[:, 112:113], SMALL[:, 112:113], ALU.mult),
                 reads=["fmv2"], writes=["f52"])
            P.op("vector", lambda e: e.tensor_tensor(
                SMALL[:, 52:53], SMALL[:, 52:53], SMALL[:, 113:114], ALU.add),
                 reads=["f52", "fmv2"], writes=["f52"])
            P.op("scalar", lambda e: e.activation(SMALL[:, 53:54], SMALL[:, 52:53],
                                                  AF.Sqrt, bias=CEPS[:, 0:1]),
                 reads=["f52", "ceps"], writes=["f53"])
            P.op("vector", lambda e: e.reciprocal(SMALL[:, 54:55], SMALL[:, 53:54]),
                 reads=["f53"], writes=["f54"])
            P.op("vector", lambda e: e.tensor_tensor(
                SMALL[:, 55:56], SMALL[:, 54:55], SMALL[:, 114:115], ALU.mult),
                 reads=["f54", "fmx"], writes=["f55"])
            P.op("vector", lambda e: e.tensor_scalar(
                SMALL[:, 56:57], SMALL[:, 55:56], 1.0, 1e-5, ALU.mult, ALU.max),
                 reads=["f55"], writes=["f56"])
            P.op("vector", lambda e: e.reciprocal(SMALL[:, 57:58], SMALL[:, 56:57]),
                 reads=["f56"], writes=["f57"])
            P.op("vector", lambda e: e.tensor_scalar(
                SMALL[:, 57:58], SMALL[:, 57:58], 127.0, None, ALU.mult),
                 reads=["f57"], writes=["f57"])
            P.op("vector", lambda e: e.tensor_tensor(
                SMALL[:, 58:59], SMALL[:, 54:55], SMALL[:, 57:58], ALU.mult),
                 reads=["f54", "f57"], writes=["f58"])
            P.op("vector", lambda e: e.tensor_scalar(
                SMALL[:, 59:60], SMALL[:, 56:57], 1.0 / 127.0, C2C, ALU.mult, ALU.mult),
                 reads=["f56", "fswb"], writes=["f59"])
            P.op("vector", lambda e: e.tensor_scalar(
                SMALL[:, 60:61], SMALL[:, 57:58], SW2C, None, ALU.mult),
                 reads=["f57", "fswb"], writes=["f60"])
            P.op("tensor", lambda e: e.transpose(TR_ps[0:1, 16:144], SMALL[:, 60:61], ID32[:]),
                 reads=["f60", "ID32"], writes=["bkTR"])
            P.op("vector", lambda e: e.tensor_copy(U1R[0:1, 128:256], TR_ps[0:1, 16:144]),
                 reads=["bkTR"], writes=["u2r"])
            for fs in range(8):
                P.op("vector", lambda e, fs=fs: e.tensor_scalar(
                    SCR[:, 0:512], ABIG[:, fs * 512:(fs + 1) * 512], SMALL[:, 58:59],
                    None, ALU.mult),
                     reads=["abig", "f58"], writes=["scrA"])
                P.op("vector", lambda e: e.tensor_scalar(
                    SCR[:, 512:1024], SCR[:, 0:512], M32, M32, ALU.add, ALU.subtract),
                     reads=["scrA"], writes=["scrB"])
                P.op("vector", lambda e: e.tensor_scalar(
                    SCR[:, 0:512], SCR[:, 0:512], M64, M64, ALU.add, ALU.subtract),
                     reads=["scrA"], writes=["scrA"])
                P.op("vector", lambda e: e.tensor_tensor(
                    SCR[:, 0:512], SCR[:, 0:512], SCR[:, 512:1024], ALU.subtract),
                     reads=["scrA", "scrB"], writes=["scrA"])
                P.op("vector", lambda e: e.scalar_tensor_tensor(
                    PBF[:, 0:512], SCR[:, 0:512], PDET[:, 0:1], SCR[:, 512:1024],
                    ALU.mult, ALU.add),
                     reads=["scrA", "scrB", "PDET"], writes=["ps0"])
                for c in range(4):
                    P.op("tensor", lambda e, c=c: e.transpose(
                        TRbf[:, c * 128:(c + 1) * 128], PBF[:, c * 128:(c + 1) * 128],
                        IDBF[:]),
                         reads=["ps0", "IDBF"], writes=["bkTR"])
                for c in range(4):
                    fc = fs * 4 + c
                    P.op("vector", lambda e, c=c, fc=fc: e.tensor_copy(
                        QKV[:, 1024 + fc * 128: 1024 + (fc + 1) * 128],
                        TRbf[:, c * 128:(c + 1) * 128]),
                         reads=["bkTR"], writes=["qkvA", "qkvB", "qkvC"])
            pk = ["S0", "S1"]
            for half in range(2):
                for fc in range(32):
                    P.op("tensor", lambda e, fc=fc, half=half: e.matmul(
                        S_ps[:, half * 512:(half + 1) * 512],
                        QKV[:, 1024 + fc * 128: 1024 + (fc + 1) * 128],
                        W2T(fc)[:, half * 512:(half + 1) * 512],
                        start=(fc == 0), stop=False),
                         reads=["qkvA", "qkvB", "qkvC", f"ctxf{fc // 4}"], writes=[SKEY[half]])
                P.op("tensor", lambda e, half=half: e.matmul(
                    S_ps[:, half * 512:(half + 1) * 512], U1R[0:1, 128:256],
                    ROWS[0:1, 4096 + half * 512: 4096 + (half + 1) * 512],
                    start=False, stop=True),
                     reads=["u2r", "rows_bout"], writes=[SKEY[half]])
            P.op("vector", lambda e, xa=xa: e.scalar_tensor_tensor(
                xa, S_ps[:, 0:1024], SMALL[:, 59:60], xa, ALU.mult, ALU.add),
                 reads=pk + ["f59", f"xr{tc}"], writes=[f"xr{tc}"])

    STRIPBF = ABIG[:].bitcast(BF16)

    def emit_l1_attn():
        for si, s in enumerate("qkvo"):
            P.dma("gpsimd", lambda e, si=si, s=s: e.dma_start(
                ROWS[0:1, si * 1024:(si + 1) * 1024], a_b[s].unsqueeze(0)),
                writes=["rows_bin", "rows_bout"])
        P.dma("sync", lambda e: e.dma_start(SCR[0:32, 1024:1040], rel_e),
              writes=["scrB"])
        P.op("vector", lambda e: e.memset(SMALL[0:16, 94:95], 0.0), writes=["pbm"])
        for ch in range(8):
            w = 512 if ch < 7 else 511
            P.dma("sync", lambda e, ch=ch, w=w: e.dma_start(
                SCR[0:32, 0:w], c_oh[:, ch * 512:ch * 512 + w]), writes=["scrA"])
            P.op("tensor", lambda e, w=w: e.matmul(
                AX_ps[0:16, 0:w], SCR[0:32, 1024:1040], SCR[0:32, 0:w],
                start=True, stop=True),
                 reads=["scrA", "scrB"], writes=["bkAX"])
            P.op("vector", lambda e, w=w: e.tensor_copy(
                SCR[0:16, 1040:1040 + w],
                bass.AP(AX_ps[0:1, 0:1].tensor, w - 1, [[512, 16], [-1, w]])),
                 reads=["bkAX"], writes=["pbev"])
            P.dma("gpsimd", lambda e, ch=ch, w=w: e.dma_start(
                pb_dram[:, 4223 - ch * 512 - w: 4223 - ch * 512], SCR[0:16, 1040:1040 + w]),
                reads=["pbev"], writes=["pbd"])
            P.op("vector", lambda e, w=w: e.tensor_reduce(
                SMALL[0:16, 95:96], AX_ps[0:16, 0:w], AX, ALU.max,
                apply_absolute_value=True),
                 reads=["bkAX"], writes=["pbr"])
            P.op("vector", lambda e: e.tensor_tensor(
                SMALL[0:16, 94:95], SMALL[0:16, 94:95], SMALL[0:16, 95:96], ALU.max),
                 reads=["pbr", "pbm"], writes=["pbm"])
        P.op("vector", lambda e: e.memset(SCR[0:16, 1040:1553], 0.0), writes=["pbev"])
        P.dma("gpsimd", lambda e: e.dma_start(pb_dram[:, 0:128], SCR[0:16, 1040:1168]),
              reads=["pbev"], writes=["pbd"])
        P.dma("gpsimd", lambda e: e.dma_start(pb_dram[:, 4223:4608], SCR[0:16, 1040:1425]),
              reads=["pbev"], writes=["pbd"])
        P.op("tensor", lambda e: e.transpose(TR_ps[0:1, 0:16], SMALL[0:16, 94:95],
                                             ID32[0:16, 0:16]),
             reads=["pbm", "ID32"], writes=["bkTR"])
        P.op("vector", lambda e: e.tensor_copy(SMALL[0:1, 96:112], TR_ps[0:1, 0:16]),
             reads=["bkTR"], writes=["pbrow"])
        P.op("vector", lambda e: e.tensor_reduce(SMALL[0:1, 113:114], SMALL[0:1, 96:112],
                                                 AX, ALU.max),
             reads=["pbrow"], writes=["pbrow"])
        for j in range(2):
            P.dma("gpsimd", lambda e, j=j: e.dma_start(
                SMALL[j:j + 1, 112:113],
                bass.AP(SMALL[0:1, 0:1].tensor, 113, [[128, 1], [1, 1]])),
                reads=["pbrow"], writes=["pbm2"])
        P.dma("sync", lambda e: e.dma_start(SMALL[0:2, 104:112], gru_c2), writes=["cc2"])
        P.dma("sync", lambda e: e.dma_start(SCR[0:64, 1600:1608], gru_w), writes=["gruw"])
        P.dma("sync", lambda e: e.dma_start(SCR[0:8, 1608:1610], c_p4), writes=["gruw"])
        P.dma("sync", lambda e: e.dma_start(SCR[0:8, 1610:1611], gru_b.unsqueeze(1)),
              writes=["gruw"])
        P.op("tensor", lambda e: e.transpose(TR_ps[0:8, 64:128], SCR[0:64, 1600:1608],
                                             ID32[0:64, 0:64]),
             reads=["gruw", "ID32"], writes=["bkTR"])
        P.op("vector", lambda e: e.tensor_copy(SCR[0:8, 1612:1676], TR_ps[0:8, 64:128]),
             reads=["bkTR"], writes=["gruw2"])
        P.op("tensor", lambda e: e.matmul(TR_ps[0:64, 100:102], SCR[0:8, 1612:1676],
                                          SCR[0:8, 1608:1610], start=True, stop=True),
             reads=["gruw2", "gruw"], writes=["bkTR"])
        P.op("tensor", lambda e: e.matmul(TR_ps[64:128, 102:104], SCR[0:8, 1612:1676],
                                          SCR[0:8, 1608:1610], start=True, stop=True),
             reads=["gruw2", "gruw"], writes=["bkTR"])
        P.op("vector", lambda e: e.memset(GAB[:], 0.0), writes=["gab"])
        P.op("vector", lambda e: e.tensor_copy(GAB[0:64, 0:2], TR_ps[0:64, 100:102]),
             reads=["bkTR", "gab"], writes=["gab"])
        P.op("vector", lambda e: e.tensor_copy(GAB[64:128, 2:4], TR_ps[64:128, 102:104]),
             reads=["bkTR", "gab"], writes=["gab"])
        P.op("tensor", lambda e: e.matmul(TR_ps[0:1, 104:106], SCR[0:8, 1610:1611],
                                          SCR[0:8, 1608:1610], start=True, stop=True),
             reads=["gruw"], writes=["bkTR"])
        P.op("vector", lambda e: e.tensor_copy(SMALL[0:1, 88:90], TR_ps[0:1, 104:106]),
             reads=["bkTR"], writes=["bg2r"])
        for j in range(2):
            P.dma("gpsimd", lambda e, j=j: e.dma_start(
                SMALL[j:j + 1, 90:91], bass.AP(SMALL[0:1, 0:1].tensor, 88, [[128, 1], [1, 1]])),
                reads=["bg2r"], writes=["bga"])
            P.dma("gpsimd", lambda e, j=j: e.dma_start(
                SMALL[j:j + 1, 91:92], bass.AP(SMALL[0:1, 0:1].tensor, 89, [[128, 1], [1, 1]])),
                reads=["bg2r"], writes=["bgb"])

        for hp in range(8):
            hA = 2 * hp
            for si, s in enumerate("qkv"):
                for c in range(8):
                    P.dma("gpsimd", lambda e, s=s, si=si, hp=hp, c=c: e.dma_start(
                        WSL[:, si * 1024 + c * 128: si * 1024 + (c + 1) * 128],
                        a_w[s][c * 128:(c + 1) * 128, hp * 128:(hp + 1) * 128]),
                        writes=[f"wsl{s}"])
            for h01 in range(2):
                h = hA + h01
                P.dma("gpsimd", lambda e, h=h, h01=h01: e.dma_start(
                    STRIPBF[:, h01 * 4096:(h01 + 1) * 4096],
                    bass.AP(pb_dram.tensor, h * 4608, [[1, 128], [1, 4096]])),
                    reads=["pbd"], writes=[f"strip{h01}", "abig"])
            for si, s in enumerate("qkv"):
                for tc4 in range(4):
                    for c in range(8):
                        P.op("tensor", lambda e, si=si, c=c, tc4=tc4: e.matmul(
                            AX_ps[:, 0:512],
                            WSL[:, si * 1024 + c * 128: si * 1024 + (c + 1) * 128],
                            XLNF[c][:, tc4 * 512:(tc4 + 1) * 512],
                            start=(c == 0), stop=False),
                             reads=[f"wsl{s}", f"xlnf{c}"], writes=["bkAX"])
                    P.op("tensor", lambda e, si=si, hp=hp: e.matmul(
                        AX_ps[:, 0:512],
                        ROWS[0:1, si * 1024 + hp * 128: si * 1024 + (hp + 1) * 128],
                        ONER[0:1, 0:512], start=False, stop=True),
                         reads=["rows_bin", "ONER"], writes=["bkAX"])
                    P.op("scalar", lambda e, si=si, tc4=tc4: e.activation(
                        QKV[:, si * T + tc4 * 512: si * T + (tc4 + 1) * 512],
                        AX_ps[:, 0:512], AF.Identity, scale=(0.125 if si == 0 else 1.0)),
                         reads=["bkAX"], writes=[["qkvA", "qkvB", "qkvC"][si]])
            for kt in range(16):
                P.op("tensor", lambda e, kt=kt: e.transpose(
                    TRbf[:, (kt % 8) * 128:(kt % 8 + 1) * 128],
                    QKV[:, 2 * T + kt * 128: 2 * T + (kt + 1) * 128], IDBF[:]),
                     reads=["qkvC", "IDBF"], writes=["bkTR"])
                P.op("vector", lambda e, kt=kt: e.tensor_copy(
                    VT[:, kt * 128:(kt + 1) * 128], TRbf[:, (kt % 8) * 128:(kt % 8 + 1) * 128]),
                     reads=["bkTR"], writes=["vt"])
            P.op("scalar", lambda e: e.activation(PBF[:, 0:2048], QKV[:, 0:T], AF.Square),
                 reads=["qkvA"], writes=["ps0", "ps1", "ps2", "ps3"])
            for tc4 in range(4):
                P.op("tensor", lambda e, tc4=tc4: e.matmul(
                    S_ps[0:2, tc4 * 512:(tc4 + 1) * 512], ES2[:],
                    PBF[:, tc4 * 512:(tc4 + 1) * 512], start=True, stop=True),
                     reads=["ps0", "ps1", "ps2", "ps3", "ES2"], writes=[SKEY[tc4]])
            P.op("scalar", lambda e: e.activation(SCR[0:2, 0:2048], S_ps[0:2, 0:2048],
                                                  AF.Sqrt, bias=CEPS[0:2, 1:2]),
                 reads=SKEY + ["ceps"], writes=["negB", "scrA", "scrB"])
            P.op("scalar", lambda e: e.activation(PBF[:, 0:2048], QKV[:, T:2 * T], AF.Square),
                 reads=["qkvB"], writes=["ps0", "ps1", "ps2", "ps3"])
            for tc4 in range(4):
                P.op("tensor", lambda e, tc4=tc4: e.matmul(
                    S_ps[0:2, tc4 * 512:(tc4 + 1) * 512], ES2[:],
                    PBF[:, tc4 * 512:(tc4 + 1) * 512], start=True, stop=True),
                     reads=["ps0", "ps1", "ps2", "ps3", "ES2"], writes=[SKEY[tc4]])
            P.op("vector", lambda e: e.tensor_reduce(SMALL[0:2, 20:21], S_ps[0:2, 0:2048],
                                                     AX, ALU.max),
                 reads=SKEY, writes=["kmax2"])
            P.op("scalar", lambda e: e.activation(SMALL[0:2, 21:22], SMALL[0:2, 20:21],
                                                  AF.Sqrt, bias=CEPS[0:2, 1:2]),
                 reads=["kmax2", "ceps"], writes=["kmax"])
            P.op("vector", lambda e: e.tensor_scalar(
                SCR[0:2, 0:2048], SCR[0:2, 0:2048], SMALL[0:2, 21:22], None, ALU.mult),
                 reads=["negB", "scrA", "scrB", "kmax"], writes=["negB", "scrA", "scrB"])
            pkeys = ["ps0", "ps1", "ps2", "ps3"]
            for tc4 in range(4):
                P.op("tensor", lambda e, tc4=tc4, hp=hp: e.matmul(
                    S_ps[0:2, tc4 * 512:(tc4 + 1) * 512], GAB[:, 1:4:2],
                    XLNF[hp][:, tc4 * 512:(tc4 + 1) * 512], start=True, stop=True),
                     reads=["gab", f"xlnf{hp}"], writes=[SKEY[tc4]])
            P.op("scalar", lambda e: e.activation(S_ps[0:2, 0:2048], S_ps[0:2, 0:2048],
                                                  AF.Sigmoid, bias=SMALL[0:2, 91:92]),
                 reads=SKEY + ["bgb"], writes=SKEY)
            P.op("vector", lambda e, hp=hp: e.tensor_scalar(
                PBF[0:2, 0:2048], S_ps[0:2, 0:2048], SMALL[0:2, 104 + hp:105 + hp], 1.0,
                ALU.mult, ALU.subtract),
                 reads=SKEY + ["cc2"], writes=pkeys)
            for tc4 in range(4):
                P.op("tensor", lambda e, tc4=tc4, hp=hp: e.matmul(
                    S_ps[0:2, tc4 * 512:(tc4 + 1) * 512], GAB[:, 0:3:2],
                    XLNF[hp][:, tc4 * 512:(tc4 + 1) * 512], start=True, stop=True),
                     reads=["gab", f"xlnf{hp}"], writes=[SKEY[tc4]])
            P.op("scalar", lambda e: e.activation(S_ps[0:2, 0:2048], S_ps[0:2, 0:2048],
                                                  AF.Sigmoid, bias=SMALL[0:2, 90:91]),
                 reads=SKEY + ["bga"], writes=SKEY)
            P.op("vector", lambda e: e.scalar_tensor_tensor(
                PBF[0:2, 0:2048], S_ps[0:2, 0:2048], 1.0, PBF[0:2, 0:2048],
                ALU.mult, ALU.mult),
                 reads=SKEY + pkeys, writes=pkeys)
            P.op("vector", lambda e: e.tensor_scalar(
                PBF[0:2, 0:2048], PBF[0:2, 0:2048], 2.0, None, ALU.add),
                 reads=pkeys, writes=pkeys)
            for qt in range(NT):
                P.op("tensor", lambda e, qt=qt: e.transpose(
                    TRbf[:, 0:2], PBF[0:2, qt * 128:(qt + 1) * 128], IDBF[0:2, 0:2]),
                     reads=pkeys + ["IDBF"], writes=["bkTR"])
                P.op("vector", lambda e, qt=qt, hA=hA: e.tensor_copy(
                    GT[:, qt * 16 + hA: qt * 16 + hA + 2], TRbf[:, 0:2]),
                     reads=["bkTR"], writes=[f"gt{qt}"])
            P.op("scalar", lambda e: e.activation(
                PBF[0:2, 0:2048], PBF[0:2, 0:2048], AF.Abs, scale=SMALL[0:2, 112:113]),
                 reads=pkeys + ["pbm2"], writes=pkeys)
            P.op("vector", lambda e: e.tensor_tensor(
                SCR[0:2, 0:2048], SCR[0:2, 0:2048], PBF[0:2, 0:2048], ALU.add),
                 reads=["negB", "scrA", "scrB"] + pkeys, writes=["negB", "scrA", "scrB"])
            for qt in range(NT):
                P.op("tensor", lambda e, qt=qt: e.transpose(
                    TR_ps[:, 0:2], SCR[0:2, qt * 128:(qt + 1) * 128], ID32[0:2, 0:2]),
                     reads=["negB", "scrA", "scrB", "ID32"], writes=["bkTR"])
                P.op("vector", lambda e, qt=qt, hA=hA: e.tensor_scalar(
                    BT[:, qt * 16 + hA: qt * 16 + hA + 2], TR_ps[:, 0:2], -1.0, None,
                    ALU.mult),
                     reads=["bkTR"], writes=[f"bt{qt}"])

            def l1_mm(qt, kc4):
                for h01 in range(2):
                    sbank = h01 * 2 + kc4 % 2
                    P.op("tensor", lambda e, h01=h01, qt=qt, kc4=kc4, sbank=sbank:
                         e.matmul(
                             S_ps[:, sbank * 512:(sbank + 1) * 512],
                             QKV[h01 * 64:(h01 + 1) * 64, qt * 128:(qt + 1) * 128],
                             QKV[h01 * 64:(h01 + 1) * 64, T + kc4 * 512: T + (kc4 + 1) * 512],
                             start=True, stop=True),
                         reads=["qkvA", "qkvB"], writes=[SKEY[sbank]])

            def l1_tail(qt, kc4):
                TB = TRbf if qt % 2 == 0 else AVbf
                tbk = "bkTR" if qt % 2 == 0 else "bkAV"
                for h01 in range(2):
                    h = hA + h01
                    sbank = h01 * 2 + kc4 % 2
                    scol = sbank * 512
                    pslot = h01 * 1024 + (kc4 % 2) * 512
                    pkey = f"ps{h01 * 2 + kc4 % 2}"
                    stro = h01 * 4096 + 2175 + 128 * qt - kc4 * 512
                    P.op("vector", lambda e, scol=scol, qt=qt, h=h, stro=stro:
                         e.scalar_tensor_tensor(
                             S_ps[:, scol:scol + 512],
                             bass.AP(STRIPBF.tensor, stro, [[8192, 128], [-1, 512]]),
                             GT[:, qt * 16 + h: qt * 16 + h + 1],
                             S_ps[:, scol:scol + 512], ALU.mult, ALU.add),
                         reads=[SKEY[sbank], f"strip{h01}", "abig", f"gt{qt}"],
                         writes=[SKEY[sbank]])
                    zc = 24 + h01 * 2
                    P.op("scalar", lambda e, scol=scol, pslot=pslot, qt=qt, h=h, zc=zc,
                         kc4=kc4: e.activation(
                             PBF[:, pslot:pslot + 512], S_ps[:, scol:scol + 512], AF.Exp,
                             bias=BT[:, qt * 16 + h: qt * 16 + h + 1],
                             accum_out=SMALL[:, zc + kc4 % 2: zc + kc4 % 2 + 1]),
                         reads=[SKEY[sbank], f"bt{qt}"],
                         writes=[pkey, f"zz{h01}_{kc4}"])
                    if kc4 % 2 == 1:
                        dst = 32 + h01 + (kc4 // 2) * 4
                        P.op("vector", lambda e, zc=zc, dst=dst: e.tensor_tensor(
                            SMALL[:, dst:dst + 1],
                            SMALL[:, zc:zc + 1], SMALL[:, zc + 1:zc + 2], ALU.add),
                             reads=[f"zz{h01}_{kc4 - 1}", f"zz{h01}_{kc4}"],
                             writes=[f"zp{h01}_{kc4 // 2}"])
                    for bb in range(4):
                        kcg = kc4 * 4 + bb
                        P.op("tensor", lambda e, pslot=pslot, bb=bb, h01=h01, TB=TB,
                             kcg=kcg: e.transpose(
                                 TB[:, (h01 * 4 + bb) * 128:(h01 * 4 + bb + 1) * 128],
                                 PBF[:, pslot + bb * 128: pslot + (bb + 1) * 128], IDBF[:]),
                             reads=[pkey, "IDBF"], writes=[tbk])
                    for bb in range(4):
                        kcg = kc4 * 4 + bb
                        P.op("vector", lambda e, bb=bb, h01=h01, TB=TB, kcg=kcg:
                             e.tensor_copy(
                                 PT[kcg][:, h01 * 128:h01 * 128 + 128],
                                 TB[:, (h01 * 4 + bb) * 128:(h01 * 4 + bb + 1) * 128]),
                             reads=[tbk], writes=[f"pt{kcg}_{h01}"])
                    if kc4 == 3:
                        P.op("vector", lambda e, h01=h01: e.tensor_tensor(
                            SMALL[:, 40 + h01:41 + h01], SMALL[:, 32 + h01:33 + h01],
                            SMALL[:, 36 + h01:37 + h01], ALU.add),
                             reads=[f"zp{h01}_0", f"zp{h01}_1"], writes=[f"zt{h01}"])
                        P.op("vector", lambda e, h01=h01: e.reciprocal(
                            SMALL[:, 42 + h01:43 + h01], SMALL[:, 40 + h01:41 + h01]),
                             reads=[f"zt{h01}"], writes=[f"zr{h01}"])
                        for kcg in range(16):
                            P.op("tensor", lambda e, kcg=kcg, h01=h01: e.matmul(
                                AY_ps[:, 0:64], PT[kcg][:, h01 * 128:h01 * 128 + 128],
                                VT[:, kcg * 128 + h01 * 64: kcg * 128 + h01 * 64 + 64],
                                start=(kcg == 0), stop=(kcg == 15)),
                                 reads=[f"pt{kcg}_{h01}", "vt"], writes=["AY"])
                        P.op("vector", lambda e, h01=h01, qt=qt: e.tensor_scalar(
                            CTM[:, (h01 * 2 + qt % 2) * 64:(h01 * 2 + qt % 2) * 64 + 64],
                            AY_ps[:, 0:64], SMALL[:, 42 + h01:43 + h01], None, ALU.mult),
                             reads=["AY", f"zr{h01}"], writes=[f"ctm{h01}_{qt % 2}"])
                        tcol = 768 if h01 == 0 else 896
                        P.op("tensor", lambda e, h01=h01, qt=qt, TB=TB, tcol=tcol:
                             e.transpose(
                                 TB[0:64, tcol:tcol + 128],
                                 CTM[:, (h01 * 2 + qt % 2) * 64:(h01 * 2 + qt % 2) * 64 + 64],
                                 IDBF[:]),
                             reads=[f"ctm{h01}_{qt % 2}", "IDBF"], writes=[tbk])
                        P.op("vector", lambda e, h01=h01, qt=qt, hp=hp, TB=TB, tcol=tcol:
                             e.tensor_copy(
                                 CTXF[hp][h01 * 64:(h01 + 1) * 64, qt * 128:(qt + 1) * 128],
                                 TB[0:64, tcol:tcol + 128]),
                             reads=[tbk], writes=[f"ctxf{hp}"])

            stages = [(qt, kc4) for qt in range(NT) for kc4 in range(4)]
            for i in range(len(stages) + 1):
                if i < len(stages):
                    l1_mm(*stages[i])
                if i > 0:
                    l1_tail(*stages[i - 1])

        def wot_ap(cc):
            if cc < 6:
                return QKV[:, cc * 1024:(cc + 1) * 1024]
            return WSL[:, (cc - 6) * 1024:(cc - 5) * 1024]

        def wot_alias(cc):
            return ["qkvA", "qkvA", "qkvB", "qkvB", "qkvC", "qkvC", "wslq", "wslk"][cc]
        for cc in range(8):
            P.dma("gpsimd", lambda e, cc=cc: e.dma_start(
                wot_ap(cc), a_w["o"][cc * 128:(cc + 1) * 128, :]),
                writes=["wotx", wot_alias(cc)])
        for t in range(NT):
            pk = ["S0", "S1"]
            for c in range(8):
                for half in range(2):
                    P.op("tensor", lambda e, c=c, t=t, half=half: e.matmul(
                        S_ps[:, half * 512:(half + 1) * 512],
                        CTXF[c][:, t * 128:(t + 1) * 128],
                        wot_ap(c)[:, half * 512:(half + 1) * 512],
                        start=(c == 0), stop=False),
                         reads=[f"ctxf{c}", "wotx"], writes=pk)
            for half in range(2):
                P.op("tensor", lambda e, half=half: e.matmul(
                    S_ps[:, half * 512:(half + 1) * 512], ONER[0:1, 0:128],
                    ROWS[0:1, 3072 + half * 512: 3072 + (half + 1) * 512],
                    start=False, stop=True),
                     reads=["ONER", "rows_bout"], writes=pk)
            P.op("vector", lambda e, t=t: e.tensor_tensor(
                XRES[:, t * D:(t + 1) * D], XRES[:, t * D:(t + 1) * D],
                S_ps[:, 0:1024], ALU.add),
                 reads=pk + [f"xr{t}"], writes=[f"xr{t}"])

    SIG7 = 1.0 / (1.0 + float(np.exp(-1.702 * 7.0)))

    def emit_l1_moe(nexp=NE, router_only=False, stage=4):
        ABF32 = ABIG
        DNB = None
        if not router_only and stage >= 4:
            P.dma("gpsimd", lambda e: e.dma_start(QKV[0:16, 5120:6144], m_dnb),
                  writes=["dnball"])
        for c in range(8):
            P.dma("gpsimd", lambda e, c=c: e.dma_start(
                WSL[:, c * 16:(c + 1) * 16], r_w[c * 128:(c + 1) * 128, :]),
                writes=["rw"])
        for tc4 in range(4):
            for c in range(8):
                P.op("tensor", lambda e, c=c, tc4=tc4: e.matmul(
                    AX_ps[0:16, 0:512], WSL[:, c * 16:(c + 1) * 16],
                    XLNF[c][:, tc4 * 512:(tc4 + 1) * 512],
                    start=(c == 0), stop=(c == 7)),
                     reads=["rw", f"xlnf{c}"], writes=["bkAX"])
            P.op("vector", lambda e, tc4=tc4: e.tensor_copy(
                ABIG[0:16, tc4 * 512:(tc4 + 1) * 512], AX_ps[0:16, 0:512]),
                 reads=["bkAX"], writes=["logit"])
        for tc in range(NT):
            P.op("tensor", lambda e, tc=tc: e.transpose(
                TR_ps[:, 0:16], ABIG[0:16, tc * 128:(tc + 1) * 128], ID32[0:16, 0:16]),
                 reads=["logit", "ID32"], writes=["bkTR"])
            P.op("vector", lambda e: e.tensor_copy(SCR[:, 0:16], TR_ps[:, 0:16]),
                 reads=["bkTR"], writes=["lt"])
            P.op("vector", lambda e: e.tensor_reduce(SMALL[:, 44:45], SCR[:, 0:16],
                                                     AX, ALU.max),
                 reads=["lt"], writes=["m1"])
            P.op("vector", lambda e: e.tensor_scalar(
                SCR[:, 16:32], SCR[:, 0:16], SMALL[:, 44:45], None, ALU.is_ge),
                 reads=["lt", "m1"], writes=["eq1"])
            P.op("vector", lambda e: e.scalar_tensor_tensor(
                SCR[:, 32:48], SCR[:, 16:32], -1.0e30, SCR[:, 0:16], ALU.mult, ALU.add),
                 reads=["eq1", "lt"], writes=["l2"])
            P.op("vector", lambda e: e.tensor_reduce(SMALL[:, 45:46], SCR[:, 32:48],
                                                     AX, ALU.max),
                 reads=["l2"], writes=["m2"])
            P.op("vector", lambda e: e.tensor_scalar(
                SCR[:, 48:64], SCR[:, 32:48], SMALL[:, 45:46], None, ALU.is_ge),
                 reads=["l2", "m2"], writes=["eq2"])
            P.op("vector", lambda e: e.tensor_tensor(
                SMALL[:, 46:47], SMALL[:, 45:46], SMALL[:, 44:45], ALU.subtract),
                 reads=["m1", "m2"], writes=["md"])
            P.op("scalar", lambda e: e.activation(SMALL[:, 47:48], SMALL[:, 46:47], AF.Exp),
                 reads=["md"], writes=["dd"])
            P.op("vector", lambda e: e.tensor_scalar(
                SMALL[:, 48:49], SMALL[:, 47:48], 1.0, None, ALU.add),
                 reads=["dd"], writes=["dn"])
            P.op("vector", lambda e: e.reciprocal(SMALL[:, 49:50], SMALL[:, 48:49]),
                 reads=["dn"], writes=["w1"])
            P.op("vector", lambda e: e.tensor_scalar(
                SMALL[:, 50:51], SMALL[:, 49:50], -1.0, 1.0, ALU.mult, ALU.add),
                 reads=["w1"], writes=["w2"])
            P.op("vector", lambda e: e.tensor_scalar(
                SCR[:, 48:64], SCR[:, 48:64], SMALL[:, 50:51], None, ALU.mult),
                 reads=["eq2", "w2"], writes=["eq2"])
            P.op("vector", lambda e, tc=tc: e.scalar_tensor_tensor(
                GT[:, tc * 16:(tc + 1) * 16], SCR[:, 16:32], SMALL[:, 49:50],
                SCR[:, 48:64], ALU.mult, ALU.add),
                 reads=["eq1", "w1", "eq2"], writes=[f"sc{tc}"])
            P.op("tensor", lambda e, tc=tc: e.transpose(
                TR_ps[0:16, 0:128], GT[:, tc * 16:(tc + 1) * 16], ID32[:]),
                 reads=[f"sc{tc}", "ID32"], writes=["bkTR"])
            P.op("vector", lambda e, tc=tc: e.tensor_copy(
                VT[0:16, tc * 128:(tc + 1) * 128], TR_ps[0:16, 0:128]),
                 reads=["bkTR"], writes=["sbt"])

        def gu_ap(par, c):
            return CTXF[par * 4 + c // 2][:, (c % 2) * 1024:(c % 2 + 1) * 1024]

        if router_only:
            return
        for ei in range(nexp):
            par = ei % 2
            rbase = par * 1024
            for c in range(8):
                P.dma("gpsimd", lambda e, ei=ei, c=c, par=par: e.dma_start(
                    gu_ap(par, c), m_gu[ei, c * 128:(c + 1) * 128, :]),
                    writes=[f"guw{par}_{c}"])
            if stage >= 4:
                for c4 in range(4):
                    P.dma("gpsimd", lambda e, ei=ei, c4=c4: e.dma_start(
                        QKV[:, c4 * 1024:(c4 + 1) * 1024],
                        m_dn[ei, c4 * 128:(c4 + 1) * 128, :]),
                        writes=[f"dnw{c4}"])
            P.dma("gpsimd", lambda e, ei=ei, rbase=rbase: e.dma_start(
                ROWS[0:1, rbase:rbase + 1024], m_gub[ei:ei + 1, :]),
                writes=[f"gub{par}"])
            for tc in range(NT):
                gk = ["S0", "S1"]
                for half in range(2):
                    for c in range(8):
                        P.op("tensor", lambda e, c=c, tc=tc, half=half, par=par: e.matmul(
                            S_ps[:, half * 512:(half + 1) * 512],
                            XLNF[c][:, tc * 128:(tc + 1) * 128],
                            gu_ap(par, c)[:, half * 512:(half + 1) * 512],
                            start=(c == 0), stop=False),
                             reads=[f"guw{par}_{c}", f"xlnf{c}"], writes=[gk[half]])
                    P.op("tensor", lambda e, half=half, rbase=rbase: e.matmul(
                        S_ps[:, half * 512:(half + 1) * 512], ONER[0:1, 0:128],
                        ROWS[0:1, rbase + half * 512: rbase + (half + 1) * 512],
                        start=False, stop=True),
                         reads=[f"gub{par}", "ONER"], writes=[gk[half]])
                if stage < 2:
                    P.op("vector", lambda e, tc=tc: e.tensor_tensor(
                        XRES[:, tc * D:(tc + 1) * D], XRES[:, tc * D:(tc + 1) * D],
                        S_ps[:, 0:1024], ALU.add),
                         reads=gk + [f"xr{tc}"], writes=[f"xr{tc}"])
                    continue
                GUp = S_ps[:, 0:1024]
                P.op("vector", lambda e: e.tensor_scalar(
                    SCR[:, 0:512], GUp[:, 0:1024:2], 7.0, None, ALU.min),
                     reads=gk, writes=["gtw"])
                P.op("scalar", lambda e: e.activation(
                    SCR[:, 512:1024], SCR[:, 0:512], AF.Sigmoid, scale=1.702),
                     reads=["gtw"], writes=["sig"])
                P.op("vector", lambda e, tc=tc, ei=ei: e.scalar_tensor_tensor(
                    SCR[:, 0:512], SCR[:, 0:512],
                    GT[:, tc * 16 + ei: tc * 16 + ei + 1], SCR[:, 512:1024],
                    ALU.mult, ALU.mult),
                     reads=["sig", "gtw", f"sc{tc}"], writes=["gtw"])
                P.op("vector", lambda e: e.tensor_scalar(
                    SCR[:, 512:1024], GUp[:, 1:1024:2], -7.0, 7.0, ALU.max, ALU.min),
                     reads=gk, writes=["sig"])
                P.op("vector", lambda e: e.scalar_tensor_tensor(
                    PBF[:, 0:512], SCR[:, 512:1024], 1.0, SCR[:, 0:512],
                    ALU.add, ALU.mult),
                     reads=["sig", "gtw"], writes=["ps0"])
                if 2 <= stage < 3:
                    P.op("vector", lambda e, tc=tc: e.tensor_tensor(
                        XRES[:, tc * D:(tc + 1) * D][:, 0:512],
                        XRES[:, tc * D:(tc + 1) * D][:, 0:512],
                        PBF[:, 0:512], ALU.add),
                         reads=["ps0", f"xr{tc}"], writes=[f"xr{tc}"])
                    continue
                for c in range(4):
                    P.op("tensor", lambda e, c=c: e.transpose(
                        TRbf[:, c * 128:(c + 1) * 128], PBF[:, c * 128:(c + 1) * 128],
                        IDBF[:]),
                         reads=["ps0", "IDBF"], writes=["bkTR"])
                for c in range(4):
                    P.op("vector", lambda e, c=c, par=par: e.tensor_copy(
                        QKV[:, 4096 + par * 512 + c * 128: 4096 + par * 512 + (c + 1) * 128],
                        TRbf[:, c * 128:(c + 1) * 128]),
                         reads=["bkTR"], writes=[f"actw{par}"])
                if stage == 3:
                    P.op("vector", lambda e, tc=tc, par=par: e.tensor_tensor(
                        XRES[:, tc * D:(tc + 1) * D][:, 0:512],
                        XRES[:, tc * D:(tc + 1) * D][:, 0:512],
                        QKV[:, 4096 + par * 512: 4608 + par * 512], ALU.add),
                         reads=[f"actw{par}", f"xr{tc}"], writes=[f"xr{tc}"])
                    continue
                dk = ["S2", "S3"]
                for half in range(2):
                    for c4 in range(4):
                        P.op("tensor", lambda e, c4=c4, half=half, par=par, ei=ei: e.matmul(
                            S_ps[:, 1024 + half * 512: 1024 + (half + 1) * 512],
                            QKV[:, 4096 + par * 512 + c4 * 128: 4096 + par * 512 + (c4 + 1) * 128],
                            QKV[:, c4 * 1024 + half * 512: c4 * 1024 + (half + 1) * 512],
                            start=(c4 == 0), stop=(c4 == 3 and ei != 0)),
                             reads=[f"actw{par}", f"dnw{c4}"], writes=[dk[half]])
                    if ei == 0:
                        P.op("tensor", lambda e, half=half, tc=tc: e.matmul(
                            S_ps[:, 1024 + half * 512: 1024 + (half + 1) * 512],
                            VT[0:16, tc * 128:(tc + 1) * 128],
                            QKV[0:16, 5120 + half * 512: 5120 + (half + 1) * 512],
                            start=False, stop=True),
                             reads=["sbt", "dnball"], writes=[dk[half]])
                P.op("vector", lambda e, tc=tc: e.tensor_tensor(
                    XRES[:, tc * D:(tc + 1) * D], XRES[:, tc * D:(tc + 1) * D],
                    S_ps[:, 1024:2048], ALU.add),
                     reads=dk + [f"xr{tc}"], writes=[f"xr{tc}"])

    def emit_out():
        outs = []
        for t in range(NT):
            outs.append(P.dma("gpsimd", lambda e, t=t: e.dma_start(
                y_out[t * 128:(t + 1) * 128, :], XRES[:, t * D:(t + 1) * D]),
                reads=[f"xr{t}"]))
        P.op("gpsimd", lambda e: e.engine_nop(), extra_deps=outs)

    emit_ln(0, fm=True)
    if upto >= 1:
        emit_l0_attn()
    if upto >= 1.5:
        emit_ln(1, fm=False)
        if debug:
            _x1d = nc.dram_tensor("dbg_x1", [T, D], F32, kind="ExternalOutput").ap()
            _x1deps = []
            for t in range(NT):
                _x1deps.append(P.dma("gpsimd", lambda e, t=t: e.dma_start(
                    _x1d[t * 128:(t + 1) * 128, :], XRES[:, t * D:(t + 1) * D]),
                    reads=[f"xr{t}"]))
            P.op("gpsimd", lambda e: e.engine_nop(), extra_deps=_x1deps)
            _std = nc.dram_tensor("dbg_stat1", [128, 48], F32, kind="ExternalOutput").ap()
            _i = P.dma("gpsimd", lambda e: e.dma_start(_std, STAT[:, 0:48]),
                       reads=[f"stb{t}" for t in range(NT)] + [f"str{t}" for t in range(NT)])
            P.op("gpsimd", lambda e: e.engine_nop(), extra_deps=[_i])
        if upto >= 1.7:
            emit_l0_ffn()
    if upto >= 3:
        emit_ln(2, fm=True)
        emit_l1_attn()
    if upto >= 3.5:
        emit_ln(3, fm=True)
        _stage = {3.71: 1, 3.72: 2, 3.721: 2.1, 3.722: 2.2, 3.7225: 2.25, 3.723: 2.3,
                  3.724: 2.4, 3.73: 3}.get(upto, 4)
        emit_l1_moe(nexp=(16 if upto >= 4 else (2 if upto >= 3.7 else 0)),
                    router_only=(upto < 3.7), stage=_stage)
    emit_out()

    if debug:
        def dump(name, ap, reads):
            dt_ = nc.dram_tensor(name, list(ap.shape), F32, kind="ExternalOutput").ap()
            i = P.dma("gpsimd", lambda e: e.dma_start(dt_, ap), reads=reads)
            P.op("gpsimd", lambda e: e.engine_nop(), extra_deps=[i])
        dump("dbg_xlnf0", XLNF[0][:], ["xlnf0"])
        dump("dbg_stat", STAT[:, 0:48], ["stb15"])
        if upto == 1:
            dump("dbg_ctxf0", CTXF[0][:], ["ctxf0"])
            dump("dbg_qkv", QKV[:], ["qkvA", "qkvB", "qkvC"])
            dump("dbg_bt", BT[:], ["bt15"])
            dump("dbg_vt", VT[:], ["vt"])
        if upto == 2:
            dump("dbg_hg", ABIG[:], ["abig" for fs in range(8)])
            dump("dbg_fsc", SMALL[:, 40:61], ["f59", "f58", "f50", "f48"])
            dump("dbg_w1t", XLNF[0][:].bitcast(FP8), ["xlnf0"])
            dump("dbg_w2t", CTXF[0][:].bitcast(FP8), ["ctxf0"])
            dump("dbg_hqf", QKV[:, 1024:5120], ["qkvA", "qkvB", "qkvC"])
            dump("dbg_sw", SMALL[:, 120:127], ["fswb", "fac"])

    P.materialize(nc, ctx)
    ctx.close()
    return nc


def pack_inputs(inputs):
    import ml_dtypes
    ins = {
        "w_in": inputs["L0_attn_w_in"], "b_in": inputs["L0_attn_b_in"],
        "w_out": inputs["L0_attn_w_out"], "b_out": inputs["L0_attn_b_out"],
        "f_w1": inputs["L0_ffn_w1"], "f_b1": inputs["L0_ffn_b1"],
        "f_w2": inputs["L0_ffn_w2"], "f_b2": inputs["L0_ffn_b2"],
        "a_wq": inputs["L1_attn_Wq"], "a_bq": inputs["L1_attn_bq"],
        "a_wk": inputs["L1_attn_Wk"], "a_bk": inputs["L1_attn_bk"],
        "a_wv": inputs["L1_attn_Wv"], "a_bv": inputs["L1_attn_bv"],
        "a_wo": inputs["L1_attn_Wo"], "a_bo": inputs["L1_attn_bo"],
        "rel_e": inputs["L1_attn_rel_embed"],
        "gru_w": inputs["L1_attn_gru_w"], "gru_b": inputs["L1_attn_gru_b"],
        "gru_c": np.asarray(inputs["L1_attn_gru_const"]).reshape(16),
        "gru_c2": np.asarray(inputs["L1_attn_gru_const"]).reshape(8, 2).T,
        "r_w": inputs["L1_ffn_router_weight"],
        "m_gu": inputs["L1_ffn_gate_up_proj"], "m_gub": inputs["L1_ffn_gate_up_proj_bias"],
        "m_dn": inputs["L1_ffn_down_proj"], "m_dnb": inputs["L1_ffn_down_proj_bias"],
        "c_oh": _bucket_onehot(),
        "c_id32": np.eye(128, dtype=np.float32),
        "c_idbf": np.eye(128, dtype=ml_dtypes.bfloat16),
        "c_p4": np.kron(np.eye(2, dtype=np.float32), np.ones((4, 1), np.float32)),
    }
    es2 = np.zeros((128, 2), dtype=ml_dtypes.bfloat16)
    es2[0:64, 0] = 1; es2[64:128, 1] = 1
    ins["c_es2"] = es2
    mi, mf, ml = _l0_masks()
    ins["c_mi"] = mi.astype(ml_dtypes.bfloat16)
    ins["c_mf"] = mf.astype(ml_dtypes.bfloat16)
    ins["c_ml"] = ml.astype(ml_dtypes.bfloat16)
    return {k: np.ascontiguousarray(v) for k, v in ins.items()}


def kernel(**inputs):
    ins = pack_inputs(inputs)
    x = np.ascontiguousarray(inputs["L0_attn_x"], dtype=np.float32)
    nc = build_program()
    in_maps = []
    for c in range(8):
        m = dict(ins)
        m["x"] = np.ascontiguousarray(x[c])
        in_maps.append(m)
    res = run_bass_kernel_spmd(nc, in_maps, core_ids=list(range(8)))
    out = np.stack([np.asarray(res.results[c]["y"]) for c in range(8)], axis=0)
    return out.astype(np.float32)


# version 08
# speedup vs baseline: 1.0142x; 1.0142x over previous
import numpy as np
from contextlib import ExitStack

import concourse.bass as bass
import concourse.mybir as mybir
from concourse.alu_op_type import AluOpType as ALU
from concourse.bass_utils import run_bass_kernel_spmd

F32 = mybir.dt.float32
BF16 = mybir.dt.bfloat16
FP8 = mybir.dt.float8e4
AF = mybir.ActivationFunctionType
AX = mybir.AxisListType.X

B, T, D = 8, 2048, 1024
NH, HD = 16, 64
NT = T // 128
FF = 4096
NE, NI = 16, 512
NEG = -1.0e30
M32 = 12582912.0
M64 = 6755399441055744.0

ENGINES = ("tensor", "vector", "scalar", "sync", "gpsimd")


def _bucket_onehot():
    rel = np.arange(4095, dtype=np.int64) - 2047
    nb = 16
    buckets = (rel > 0).astype(np.int64) * nb
    rp = np.abs(rel)
    max_exact = nb // 2
    is_small = rp < max_exact
    large = max_exact + (
        np.log(np.maximum(rp, 1) / max_exact) / np.log(128 / max_exact) * (nb - max_exact)
    ).astype(np.int64)
    large = np.minimum(large, nb - 1)
    buckets = buckets + np.where(is_small, rp, large)
    oh = np.zeros((32, 4095), dtype=np.float32)
    oh[buckets, np.arange(4095)] = 1.0
    return oh


def _l0_masks():
    i = np.arange(128)[:, None]
    j384 = np.arange(384)[None, :]
    j256 = np.arange(256)[None, :]
    inter = np.where((j384 >= i) & (j384 <= i + 256), 0.0, NEG).astype(np.float32)
    first = np.where(j256 <= i + 128, 0.0, NEG).astype(np.float32)
    last = np.where(j256 >= i, 0.0, NEG).astype(np.float32)
    return inter, first, last


class _MockInst:
    def then_inc(self, *a, **k):
        return self


class _MockEng:
    def __init__(self):
        self.calls = []

    def __getattr__(self, name):
        def f(*args, **kwargs):
            self.calls.append((name, args, kwargs))
            return _MockInst()
        return f


def _ap_is_psum(ap):
    try:
        return "psum" in str(getattr(ap, "space", "")).lower()
    except Exception:
        return False


def _ap_keys(ap, gran_sb=512, gran_ps=2048):
    try:
        t = ap.tensor
        name = t.name
        esz = {"float32": 4, "bfloat16": 2, "float16": 2, "int32": 4,
               "float8e4": 1, "int8": 1, "uint8": 1}.get(str(ap.dtype).split(".")[-1], 4)
        dims = [list(d) for d in ap.ap]
        space = str(getattr(ap, "space", ""))
        onchip = ("SB" in space) or ("PSUM" in space) or ("Psum" in space) or \
                 ("State" in space) or ("sb" in space.lower() and "dram" not in space.lower())
        is_ps = "psum" in space.lower()
        if not onchip and "dram" not in space.lower():
            onchip = True
        use = dims[1:] if (onchip and len(dims) > 1) else dims
        off = int(ap.offset)
        if onchip and len(dims) > 1 and dims[0][0] > 0:
            off = off % int(dims[0][0])
        lo = hi = off
        for st, ct in use:
            ext = int(st) * (int(ct) - 1)
            if ext >= 0:
                hi += ext
            else:
                lo += ext
        gran = gran_ps if is_ps else gran_sb
        b0 = (lo * esz) // gran
        b1 = (hi * esz) // gran
        return [f"{name}:{g}" for g in range(b0, b1 + 1)]
    except Exception:
        return []


def _is_ap(x):
    return hasattr(x, "tensor") and hasattr(x, "offset")


class Plan:

    def __init__(self):
        self.ops = []
        self.last_w = {}
        self.readers = {}
        self.eng_pos = {e: [] for e in ENGINES}

    def op(self, engine, fn, reads=(), writes=(), extra_deps=(), tag="", dma=False):
        idx = len(self.ops)
        mock = _MockEng()
        fn(mock)
        rkeys = list(reads)
        wkeys = list(writes)
        pkeys = []
        defer = False
        for name, args, kwargs in mock.calls:
            if name in ("bn_stats", "bn_aggr", "tensor_reduce", "reduce_max",
                        "reduce_sum", "max", "max_index", "tensor_tensor_reduce",
                        "reciprocal"):
                defer = True
                for a in args[1:]:
                    if _is_ap(a):
                        pkeys += _ap_keys(a)
            if "accum_out" in kwargs and kwargs["accum_out"] is not None:
                defer = True
            aps = [a for a in args if _is_ap(a)]
            if aps:
                wkeys += _ap_keys(aps[0])
                for a in aps[1:]:
                    rkeys += _ap_keys(a)
            if name == "tensor_scalar":
                for a in args[2:4]:
                    if _is_ap(a):
                        pkeys += _ap_keys(a)
            elif name == "scalar_tensor_tensor" and len(args) > 2 and _is_ap(args[2]):
                pkeys += _ap_keys(args[2])
            for kw, v in kwargs.items():
                if _is_ap(v):
                    if kw == "accum_out":
                        wkeys += _ap_keys(v)
                    else:
                        rkeys += _ap_keys(v)
                        if kw in ("bias", "scale", "alpha"):
                            pkeys += _ap_keys(v)
        deps = set(extra_deps)
        hard = set()
        for k in rkeys:
            if k in self.last_w:
                d = self.last_w[k]
                deps.add(d)
                if self.ops[d]["defer"]:
                    hard.add(d)
        for k in pkeys:
            if k in self.last_w:
                hard.add(self.last_w[k])
        for k in wkeys:
            if k in self.last_w:
                d = self.last_w[k]
                deps.add(d)
                if self.ops[d]["defer"]:
                    hard.add(d)
            for r in self.readers.get(k, ()):
                deps.add(r)
        deps.discard(idx)
        hard.discard(idx)
        self.ops.append(dict(engine=engine, fn=fn, deps=deps, hard=hard, idx=idx,
                             inc=0, tag=tag, dma=dma, defer=defer))
        self.eng_pos[engine].append(idx)
        for k in wkeys:
            self.last_w[k] = idx
            self.readers[k] = []
        for k in rkeys:
            self.readers.setdefault(k, []).append(idx)
        return idx

    def dma(self, engine, fn, reads=(), writes=(), extra_deps=(), tag=""):
        return self.op(engine, fn, reads, writes, extra_deps, tag, dma=True)

    def materialize(self, nc, ctx):
        ops = self.ops
        need_inc = [False] * len(ops)
        for o in ops:
            for d in o["deps"]:
                if (ops[d]["engine"] != o["engine"] or ops[d]["dma"]
                        or o["engine"] != "tensor"):
                    need_inc[d] = True
            for d in o["hard"]:
                need_inc[d] = True
        sems = {e: ctx.enter_context(nc.semaphore(f"sem_{e}")) for e in ENGINES}
        cum = {e: 0 for e in ENGINES}
        val_after = [0] * len(ops)
        for o in ops:
            e = o["engine"]
            if need_inc[o["idx"]]:
                amt = 16 if o["dma"] else 1
                cum[e] += amt
                o["inc"] = amt
            val_after[o["idx"]] = cum[e]
        hwm = {e: {e2: 0 for e2 in ENGINES} for e in ENGINES}

        def emit_engine(eng_name, e):
            my = hwm[eng_name]
            for idx in self.eng_pos[eng_name]:
                o = ops[idx]
                waits = {}
                for d in o["deps"]:
                    pe = ops[d]["engine"]
                    if (pe == eng_name and not ops[d]["dma"] and d not in o["hard"]
                            and eng_name == "tensor"):
                        continue
                    v = val_after[d]
                    if v > waits.get(pe, 0):
                        waits[pe] = v
                for d in o["hard"]:
                    pe = ops[d]["engine"]
                    v = val_after[d]
                    if v > waits.get(pe, 0):
                        waits[pe] = v
                for pe in sorted(waits):
                    if waits[pe] > my[pe]:
                        e.wait_ge(sems[pe], waits[pe])
                        my[pe] = waits[pe]
                inst = o["fn"](e)
                if o["inc"]:
                    inst.then_inc(sems[eng_name], o["inc"])

        blk = ctx.enter_context(nc.Block())
        for name in ENGINES:
            getattr(blk, name)(lambda e, _n=name: emit_engine(_n, e))
        return sems


def build_program(upto=99, debug=False):
    nc = bass.Bass("TRN2", target_bir_lowering=False, detect_race_conditions=False)
    P = Plan()
    ctx = ExitStack()

    di = {}
    def din(name, shape, dt=F32):
        di[name] = nc.dram_tensor(name, list(shape), dt, kind="ExternalInput").ap()
        return di[name]

    x_in = din("x", (T, D))
    w_in = din("w_in", (3 * D, D)); b_in = din("b_in", (3 * D,))
    w_out = din("w_out", (D, D)); b_out = din("b_out", (D,))
    f_w1 = din("f_w1", (FF, D)); f_b1 = din("f_b1", (FF,))
    f_w2 = din("f_w2", (D, FF)); f_b2 = din("f_b2", (D,))
    a_w = {s: din(f"a_w{s}", (D, D)) for s in "qkvo"}
    a_b = {s: din(f"a_b{s}", (D,)) for s in "qkvo"}
    rel_e = din("rel_e", (32, 16))
    gru_w = din("gru_w", (64, 8)); gru_b = din("gru_b", (8,))
    gru_c = din("gru_c", (16,))
    gru_c2 = din("gru_c2", (2, 8))
    r_w = din("r_w", (D, NE))
    m_gu = din("m_gu", (NE, D, 2 * NI)); m_gub = din("m_gub", (NE, 2 * NI))
    m_dn = din("m_dn", (NE, NI, D)); m_dnb = din("m_dnb", (NE, D))
    c_oh = din("c_oh", (32, 4095))
    c_mi = din("c_mi", (128, 384), BF16); c_mf = din("c_mf", (128, 256), BF16); c_ml = din("c_ml", (128, 256), BF16)
    c_id32 = din("c_id32", (128, 128))
    c_idbf = din("c_idbf", (128, 128), BF16)
    c_es2 = din("c_es2", (128, 2), BF16)
    c_p4 = din("c_p4", (8, 2))

    y_out = nc.dram_tensor("y", [T, D], F32, kind="ExternalOutput").ap()
    pb_dram = nc.dram_tensor("pb_scratch", [NH, 4608], F32, kind="Internal").ap()

    def sbt(name, shape, dt=F32):
        return ctx.enter_context(nc.sbuf_tensor(name, list(shape), dt))

    XRES = sbt("XRES", (128, NT * D))
    XLNF = [sbt(f"XLNF{c}", (128, T), BF16) for c in range(8)]
    CTXF = [sbt(f"CTXF{c}", (128, T), BF16) for c in range(8)]
    ABIG = sbt("ABIG", (128, 4096))
    PTT = sbt("PTT", (128, 4096), BF16)
    QKV = sbt("QKV", (128, 3 * T), BF16)
    VT = sbt("VT", (128, T), BF16)
    WSL = sbt("WSL", (128, 3072), BF16)
    SCR = sbt("SCR", (128, 2048))
    PBF = sbt("PBF", (128, 2048), BF16)
    ROWS = sbt("ROWS", (1, 6144), BF16)
    ID32 = sbt("ID32", (128, 128))
    IDBF = sbt("IDBF", (128, 128), BF16)
    ES2 = sbt("ES2", (128, 2), BF16)
    STAT = sbt("STAT", (128, 64))
    SMALL = sbt("SMALL", (128, 128))
    BT = sbt("BT", (128, 256))
    GT = sbt("GT", (128, 256))
    MK = sbt("MK", (128, 896), BF16)
    CTM = sbt("CTM", (128, 256), BF16)
    ONER = sbt("ONER", (1, 512), BF16)
    U1R = sbt("U1R", (1, 256), BF16)
    P75 = sbt("P75", (128, 1)); PDET = sbt("PDET", (128, 1)); PM1 = sbt("PM1", (128, 1))
    FROW = sbt("FROW", (1, 128))
    GAB = sbt("GAB", (128, 4), BF16)
    CEPS = sbt("CEPS", (128, 4))

    S_ps = ctx.enter_context(nc.psum_tensor("S_ps", [128, 2048], F32))
    TR_ps = ctx.enter_context(nc.psum_tensor("TR_ps", [128, 512], F32))
    AV_ps = ctx.enter_context(nc.psum_tensor("AV_ps", [128, 512], F32))
    AX_ps = ctx.enter_context(nc.psum_tensor("AX_ps", [128, 512], F32))
    AY_ps = ctx.enter_context(nc.psum_tensor("AY_ps", [128, 512], F32))
    TRbf = TR_ps[:].bitcast(BF16)
    AXbf = AX_ps[:].bitcast(BF16)
    AVbf = AV_ps[:].bitcast(BF16)

    SKEY = ["S0", "S1", "S2", "S3"]

    P.dma("sync", lambda e: e.dma_start(ID32[:], c_id32), writes=["ID32"])
    P.dma("sync", lambda e: e.dma_start(IDBF[:], c_idbf), writes=["IDBF"])
    P.dma("sync", lambda e: e.dma_start(ES2[:], c_es2), writes=["ES2"])
    P.dma("sync", lambda e: e.dma_start(MK[:, 0:384], c_mi), writes=["MK"])
    P.dma("sync", lambda e: e.dma_start(MK[:, 384:640], c_mf), writes=["MK"])
    P.dma("sync", lambda e: e.dma_start(MK[:, 640:896], c_ml), writes=["MK"])
    for t in range(NT):
        P.dma("sync", lambda e, t=t: e.dma_start(
            XRES[:, t * D:(t + 1) * D], x_in[t * 128:(t + 1) * 128, :]),
            writes=[f"xr{t}"])
    P.op("vector", lambda e: e.memset(P75[:], 0.75), writes=["P75"])
    P.op("vector", lambda e: e.memset(CEPS[:, 0:1], 1e-5), writes=["ceps"])
    P.op("vector", lambda e: e.memset(CEPS[:, 1:2], 1e-30), writes=["ceps"])
    P.op("vector", lambda e: e.memset(ONER[:], 1.0), writes=["ONER"])
    P.op("vector", lambda e: e.tensor_scalar(PDET[:], P75[:], M64, M64, ALU.add, ALU.subtract),
         reads=["P75"], writes=["PDET"])
    P.op("vector", lambda e: e.tensor_scalar(PM1[:], PDET[:], -1.0, 1.0, ALU.mult, ALU.add),
         reads=["PDET"], writes=["PM1"])

    def emit_ln(blk, fm=True):
        for t in range(NT):
            xa = XRES[:, t * D:(t + 1) * D]
            mcol = STAT[:, t:t + 1]
            rcol = STAT[:, 16 + t:17 + t]
            bcol = STAT[:, 32 + t:33 + t]
            P.op("vector", lambda e, xa=xa: e.bn_stats(SMALL[:, 0:6], xa[:, 0:512]),
                 reads=[f"xr{t}"], writes=["bnA"])
            P.op("vector", lambda e, xa=xa: e.bn_stats(SMALL[:, 6:12], xa[:, 512:1024]),
                 reads=[f"xr{t}"], writes=["bnB"])
            P.op("vector", lambda e: e.bn_aggr(SMALL[:, 16:18], SMALL[:, 0:12]),
                 reads=["bnA", "bnB"], writes=["mv"])
            P.op("vector", lambda e, mcol=mcol: e.tensor_copy(mcol, SMALL[:, 16:17]),
                 reads=["mv"], writes=[f"stm{t}"])
            P.op("scalar", lambda e: e.activation(SMALL[:, 18:19], SMALL[:, 17:18],
                                                  AF.Sqrt, bias=CEPS[:, 0:1]),
                 reads=["mv", "ceps"], writes=["sd"])
            P.op("vector", lambda e, rcol=rcol: e.reciprocal(rcol, SMALL[:, 18:19]),
                 reads=["sd"], writes=[f"str{t}"])
            P.op("vector", lambda e, mcol=mcol, rcol=rcol, bcol=bcol: e.tensor_scalar(
                bcol, mcol, rcol, -1.0, ALU.mult, ALU.mult),
                 reads=[f"stm{t}", f"str{t}"], writes=[f"stb{t}"])
            if fm:
                P.op("scalar", lambda e, xa=xa, rcol=rcol, bcol=bcol: e.activation(
                    PBF[:, 0:1024], xa, AF.Identity, bias=bcol, scale=rcol),
                     reads=[f"xr{t}", f"str{t}", f"stb{t}"], writes=["ps0", "ps1"])
                for c in range(8):
                    P.op("tensor", lambda e, c=c: e.transpose(
                        TRbf[:, c * 128:(c + 1) * 128], PBF[:, c * 128:(c + 1) * 128], IDBF[:]),
                         reads=["ps0", "ps1", "IDBF"], writes=["bkTR"])
                for c in range(8):
                    P.op("vector", lambda e, c=c, t=t: e.tensor_copy(
                        XLNF[c][:, t * 128:(t + 1) * 128], TRbf[:, c * 128:(c + 1) * 128]),
                         reads=["bkTR"], writes=[f"xlnf{c}"])

    def tr_f32_to(dst_ap_fn, src_key, dst_key):
        for c in range(8):
            P.op("tensor", lambda e, c=c: e.transpose(
                TR_ps[:, (c % 4) * 128:(c % 4 + 1) * 128],
                ABIG[:, c * 128:(c + 1) * 128], ID32[:]),
                 reads=[src_key, "ID32"], writes=["bkTR"])
            P.op("scalar", lambda e, c=c: e.copy(
                dst_ap_fn(c), TR_ps[:, (c % 4) * 128:(c % 4 + 1) * 128]),
                 reads=["bkTR"], writes=[dst_key])

    def emit_l0_attn():
        P.dma("gpsimd", lambda e: e.dma_start(ROWS[0:1, 0:3072], b_in.unsqueeze(0)),
              writes=["rows_bin"])
        P.dma("gpsimd", lambda e: e.dma_start(ROWS[0:1, 3072:4096], b_out.unsqueeze(0)),
              writes=["rows_bout"])

        def qt_mm_phase(hp, qt):
            hA = 2 * hp
            lo = max(0, qt * 128 - 128)
            hi = min(T, qt * 128 + 256)
            w = hi - lo
            for h01 in range(2):
                sbank = h01 * 2 + qt % 2
                scol = sbank * 512
                P.op("tensor", lambda e, h01=h01, qt=qt, lo=lo, w=w, scol=scol: e.matmul(
                    S_ps[:, scol:scol + w],
                    QKV[h01 * 64:(h01 + 1) * 64, qt * 128:(qt + 1) * 128],
                    QKV[h01 * 64:(h01 + 1) * 64, T + lo: T + lo + w],
                    start=True, stop=True),
                     reads=["qkvA", "qkvB"], writes=[SKEY[sbank]], tag=f"s{hp}_{qt}_{h01}")

        def qt_tail_phase(hp, qt):
            hA = 2 * hp
            lo = max(0, qt * 128 - 128)
            hi = min(T, qt * 128 + 256)
            w = hi - lo
            nb = w // 128
            mofs = 384 if qt == 0 else (640 if qt == NT - 1 else 0)
            TB = TRbf if qt % 2 == 0 else AXbf
            tbk = "bkTR" if qt % 2 == 0 else "bkAX"
            for h01 in range(2):
                h = hA + h01
                sbank = h01 * 2 + qt % 2
                scol = sbank * 512
                pslot = sbank * 512
                P.op("vector", lambda e, scol=scol, w=w, mofs=mofs: e.tensor_tensor(
                    S_ps[:, scol:scol + w], S_ps[:, scol:scol + w], MK[:, mofs:mofs + w],
                    ALU.add),
                     reads=[SKEY[sbank], "MK"], writes=[SKEY[sbank]])
                zc = 24 + sbank
                P.op("scalar", lambda e, scol=scol, w=w, pslot=pslot, qt=qt, h=h, zc=zc:
                     e.activation(PBF[:, pslot:pslot + w], S_ps[:, scol:scol + w], AF.Exp,
                                  bias=BT[:, qt * 16 + h: qt * 16 + h + 1],
                                  accum_out=SMALL[:, zc:zc + 1]),
                     reads=[SKEY[sbank], f"bt{qt}"], writes=[f"ps{sbank}", f"z{sbank}"])
                P.op("vector", lambda e, zc=zc: e.reciprocal(
                    SMALL[:, zc + 4:zc + 5], SMALL[:, zc:zc + 1]),
                     reads=[f"z{sbank}"], writes=[f"zr{sbank}"])
                for bkt in range(nb):
                    P.op("tensor", lambda e, pslot=pslot, bkt=bkt, h01=h01, TB=TB: e.transpose(
                        TB[:, (h01 * 3 + bkt) * 128:(h01 * 3 + bkt + 1) * 128],
                        PBF[:, pslot + bkt * 128: pslot + (bkt + 1) * 128], IDBF[:]),
                         reads=[f"ps{sbank}", "IDBF"], writes=[tbk])
                P.op("vector", lambda e, h01=h01, TB=TB, nb=nb: e.tensor_copy(
                    PTT[:, h01 * 2048:h01 * 2048 + nb * 128],
                    TB[:, h01 * 3 * 128:(h01 * 3 + nb) * 128]),
                     reads=[tbk], writes=[f"ptb{h01}_0", f"ptb{h01}_1", f"ptb{h01}_2"])
                avk = "AV" if h01 == 0 else "AY"
                avp = AV_ps if h01 == 0 else AY_ps
                for bkt in range(nb):
                    ktg = lo // 128 + bkt
                    P.op("tensor", lambda e, bkt=bkt, h01=h01, ktg=ktg, avp=avp, nb=nb: e.matmul(
                        avp[:, 0:64], PTT[:, h01 * 2048 + bkt * 128:h01 * 2048 + (bkt + 1) * 128],
                        VT[:, ktg * 128 + h01 * 64: ktg * 128 + h01 * 64 + 64],
                        start=(bkt == 0), stop=(bkt == nb - 1)),
                         reads=[f"ptb{h01}_{bkt}", "vt"], writes=[avk])
                cslot = sbank * 64
                P.op("vector", lambda e, avp=avp, zc=zc, cslot=cslot: e.tensor_scalar(
                    CTM[:, cslot:cslot + 64], avp[:, 0:64],
                    SMALL[:, zc + 4:zc + 5], None, ALU.mult),
                     reads=[avk, f"zr{sbank}"], writes=[f"ctm{sbank}"])
                tcol = 768 if h01 == 0 else 896
                P.op("tensor", lambda e, cslot=cslot, TB=TB, tcol=tcol: e.transpose(
                    TB[0:64, tcol:tcol + 128], CTM[:, cslot:cslot + 64], IDBF[:]),
                     reads=[f"ctm{sbank}", "IDBF"], writes=[tbk])
                P.op("vector", lambda e, h01=h01, qt=qt, hp=hp, TB=TB, tcol=tcol: e.tensor_copy(
                    CTXF[hp][h01 * 64:(h01 + 1) * 64, qt * 128:(qt + 1) * 128],
                    TB[0:64, tcol:tcol + 128]),
                     reads=[tbk], writes=[f"ctxf{hp}"])

        for hp in range(8):
            hA = 2 * hp
            for si, s in enumerate("qkv"):
                P.dma("sync", lambda e, si=si, hp=hp: e.dma_start(
                    ABIG[:, 0:1024], w_in[si * D + hp * 128: si * D + hp * 128 + 128, :]),
                      writes=["abig"])
                tr_f32_to(lambda c, si=si: WSL[:, si * 1024 + c * 128: si * 1024 + (c + 1) * 128],
                          "abig", f"wsl{s}")
            for si, s in enumerate("qkv"):
                for tc4 in range(4):
                    for c in range(8):
                        P.op("tensor", lambda e, si=si, c=c, tc4=tc4: e.matmul(
                            AX_ps[:, 0:512],
                            WSL[:, si * 1024 + c * 128: si * 1024 + (c + 1) * 128],
                            XLNF[c][:, tc4 * 512:(tc4 + 1) * 512],
                            start=(c == 0), stop=False),
                             reads=[f"wsl{s}", f"xlnf{c}"], writes=["bkAX"])
                    P.op("tensor", lambda e, si=si, hp=hp: e.matmul(
                        AX_ps[:, 0:512],
                        ROWS[0:1, si * 1024 + hp * 128: si * 1024 + (hp + 1) * 128],
                        ONER[0:1, 0:512], start=False, stop=True),
                         reads=["rows_bin", "ONER"], writes=["bkAX"])
                    P.op("scalar", lambda e, si=si, tc4=tc4, s=s: e.activation(
                        QKV[:, si * T + tc4 * 512: si * T + (tc4 + 1) * 512],
                        AX_ps[:, 0:512], AF.Identity, scale=(0.125 if si == 0 else 1.0)),
                         reads=["bkAX"], writes=[["qkvA", "qkvB", "qkvC"][si]])
            for kt in range(16):
                P.op("tensor", lambda e, kt=kt: e.transpose(
                    TRbf[:, (kt % 8) * 128:(kt % 8 + 1) * 128],
                    QKV[:, 2 * T + kt * 128: 2 * T + (kt + 1) * 128], IDBF[:]),
                     reads=["qkvC", "IDBF"], writes=["bkTR"])
                P.op("vector", lambda e, kt=kt: e.tensor_copy(
                    VT[:, kt * 128:(kt + 1) * 128], TRbf[:, (kt % 8) * 128:(kt % 8 + 1) * 128]),
                     reads=["bkTR"], writes=["vt"])
            P.op("scalar", lambda e: e.activation(PBF[:, 0:2048], QKV[:, 0:T], AF.Square),
                 reads=["qkvA"], writes=["ps0", "ps1", "ps2", "ps3"])
            for tc4 in range(4):
                P.op("tensor", lambda e, tc4=tc4: e.matmul(
                    S_ps[0:2, tc4 * 512:(tc4 + 1) * 512], ES2[:],
                    PBF[:, tc4 * 512:(tc4 + 1) * 512], start=True, stop=True),
                     reads=["ps0", "ps1", "ps2", "ps3", "ES2"], writes=[SKEY[tc4]])
            P.op("scalar", lambda e: e.activation(SCR[0:2, 0:2048], S_ps[0:2, 0:2048],
                                                  AF.Sqrt, bias=CEPS[0:2, 1:2]),
                 reads=SKEY + ["ceps"], writes=["scrA", "scrB"])
            P.op("scalar", lambda e: e.activation(PBF[:, 0:2048], QKV[:, T:2 * T], AF.Square),
                 reads=["qkvB"], writes=["ps0", "ps1", "ps2", "ps3"])
            for tc4 in range(4):
                P.op("tensor", lambda e, tc4=tc4: e.matmul(
                    S_ps[0:2, tc4 * 512:(tc4 + 1) * 512], ES2[:],
                    PBF[:, tc4 * 512:(tc4 + 1) * 512], start=True, stop=True),
                     reads=["ps0", "ps1", "ps2", "ps3", "ES2"], writes=[SKEY[tc4]])
            P.op("vector", lambda e: e.tensor_reduce(SMALL[0:2, 20:21], S_ps[0:2, 0:2048],
                                                     AX, ALU.max),
                 reads=SKEY, writes=["kmax2"])
            P.op("scalar", lambda e: e.activation(SMALL[0:2, 21:22], SMALL[0:2, 20:21],
                                                  AF.Sqrt, bias=CEPS[0:2, 1:2]),
                 reads=["kmax2", "ceps"], writes=["kmax"])
            P.op("vector", lambda e: e.tensor_scalar(
                SCR[0:2, 0:2048], SCR[0:2, 0:2048], SMALL[0:2, 21:22], -1.0,
                ALU.mult, ALU.mult),
                 reads=["scrA", "scrB", "kmax"], writes=["negB", "scrA", "scrB"])
            for qt in range(NT):
                P.op("tensor", lambda e, qt=qt: e.transpose(
                    TR_ps[:, 0:2], SCR[0:2, qt * 128:(qt + 1) * 128], ID32[0:2, 0:2]),
                     reads=["negB", "scrA", "scrB", "ID32"], writes=["bkTR"])
                P.op("vector", lambda e, qt=qt, hA=hA: e.tensor_copy(
                    BT[:, qt * 16 + hA: qt * 16 + hA + 2], TR_ps[:, 0:2]),
                     reads=["bkTR"], writes=[f"bt{qt}"])
            for qt in range(NT + 1):
                if qt < NT:
                    qt_mm_phase(hp, qt)
                if qt > 0:
                    qt_tail_phase(hp, qt - 1)

        def wot_ap(cc):
            if cc < 6:
                return QKV[:, cc * 1024:(cc + 1) * 1024]
            return WSL[:, (cc - 6) * 1024:(cc - 5) * 1024]

        def wot_alias(cc):
            return ["qkvA", "qkvA", "qkvB", "qkvB", "qkvC", "qkvC", "wslq", "wslk"][cc]
        for c in range(8):
            P.dma("sync", lambda e, c=c: e.dma_start(
                ABIG[:, 0:1024], w_out[c * 128:(c + 1) * 128, :]), writes=["abig"])
            for cc in range(8):
                P.op("tensor", lambda e, cc=cc: e.transpose(
                    TR_ps[:, (cc % 4) * 128:(cc % 4 + 1) * 128],
                    ABIG[:, cc * 128:(cc + 1) * 128], ID32[:]),
                     reads=["abig", "ID32"], writes=["bkTR"])
                P.op("scalar", lambda e, c=c, cc=cc: e.copy(
                    wot_ap(cc)[:, c * 128:(c + 1) * 128],
                    TR_ps[:, (cc % 4) * 128:(cc % 4 + 1) * 128]),
                     reads=["bkTR"], writes=["wotx", wot_alias(cc)])
        for t in range(NT):
            pk = ["S0", "S1"]
            for c in range(8):
                for half in range(2):
                    P.op("tensor", lambda e, c=c, t=t, half=half: e.matmul(
                        S_ps[:, half * 512:(half + 1) * 512],
                        CTXF[c][:, t * 128:(t + 1) * 128],
                        wot_ap(c)[:, half * 512:(half + 1) * 512],
                        start=(c == 0), stop=False),
                         reads=[f"ctxf{c}", "wotx"], writes=pk)
            for half in range(2):
                P.op("tensor", lambda e, half=half: e.matmul(
                    S_ps[:, half * 512:(half + 1) * 512], ONER[0:1, 0:128],
                    ROWS[0:1, 3072 + half * 512: 3072 + (half + 1) * 512],
                    start=False, stop=True),
                     reads=["ONER", "rows_bout"], writes=pk)
            P.op("vector", lambda e, t=t: e.tensor_scalar(
                SMALL[:, 62:63], STAT[:, 16 + t:17 + t], 1.0, None, ALU.add),
                 reads=[f"str{t}"], writes=["pr1"])
            P.op("scalar", lambda e, t=t: e.activation(
                SCR[:, 0:1024], XRES[:, t * D:(t + 1) * D], AF.Identity,
                bias=STAT[:, 32 + t:33 + t], scale=SMALL[:, 62:63]),
                 reads=[f"xr{t}", f"stb{t}", "pr1"], writes=["scrA"])
            P.op("vector", lambda e, t=t: e.tensor_tensor(
                XRES[:, t * D:(t + 1) * D], SCR[:, 0:1024], S_ps[:, 0:1024], ALU.add),
                 reads=["scrA"] + pk, writes=[f"xr{t}"])

    def emit_l0_ffn():
        P.dma("gpsimd", lambda e: e.dma_start(ROWS[0:1, 0:4096], f_b1.unsqueeze(0)),
              writes=["rows_bin", "rows_bout"])
        P.dma("gpsimd", lambda e: e.dma_start(ROWS[0:1, 4096:5120], f_b2.unsqueeze(0)),
              writes=["rows_bout"])

        P.op("vector", lambda e: e.memset(SMALL[:, 120:122], 0.0), writes=["fac"])
        P.op("vector", lambda e: e.memset(CEPS[:, 2:3], 1.0), writes=["ceps1"])
        for wi, (wap, nch) in enumerate(((f_w1, 8), (f_w2, 8))):
            for ch in range(nch):
                if wi == 0:
                    src_ap = wap[ch * 512:(ch + 1) * 512, :].rearrange(
                        "(a p) d -> p a d", p=128)
                    dst_ap = ABIG[:].rearrange("p (a d) -> p a d", a=4)
                else:
                    src_ap = wap[ch * 128:(ch + 1) * 128, :]
                    dst_ap = ABIG[:]
                P.dma("sync", lambda e, src_ap=src_ap, dst_ap=dst_ap: e.dma_start(
                    dst_ap, src_ap), writes=["abig"])
                P.op("vector", lambda e: e.tensor_reduce(
                    SMALL[:, 122:123], ABIG[:], AX, ALU.add, apply_absolute_value=True),
                     reads=["abig"], writes=["fred"])
                P.op("vector", lambda e, wi=wi: e.tensor_tensor(
                    SMALL[:, 120 + wi:121 + wi], SMALL[:, 120 + wi:121 + wi],
                    SMALL[:, 122:123], ALU.add),
                     reads=["fred", "fac"], writes=["fac"])
        P.op("tensor", lambda e: e.matmul(TR_ps[0:1, 0:2], CEPS[:, 2:3], SMALL[:, 120:122],
                                          start=True, stop=True),
             reads=["fac", "ceps1"], writes=["bkTR"])
        P.op("vector", lambda e: e.tensor_scalar(
            SCR[0:1, 0:2], TR_ps[0:1, 0:2], 1.0 / (FF * D), 1e-5, ALU.mult, ALU.max),
             reads=["bkTR"], writes=["fc12", "scrA"])
        P.op("vector", lambda e: e.reciprocal(SCR[0:1, 2:4], SCR[0:1, 0:2]),
             reads=["fc12"], writes=["fsw12", "scrA"])
        P.op("vector", lambda e: e.memset(FROW[:], 1.0), writes=["fonesr"])
        P.op("tensor", lambda e: e.matmul(TR_ps[:, 4:8], FROW[:], SCR[0:1, 0:4],
                                          start=True, stop=True),
             reads=["fc12", "fsw12", "fonesr", "scrA"], writes=["bkTR"])
        P.op("vector", lambda e: e.tensor_copy(SMALL[:, 123:127], TR_ps[:, 4:8]),
             reads=["bkTR"], writes=["fswb"])
        C1C, C2C = SMALL[:, 123:124], SMALL[:, 124:125]
        SW1C, SW2C = SMALL[:, 125:126], SMALL[:, 126:127]

        def tern_quarter(load_dma, swc, dst_fn, dst_key_fn):
            load_dma()
            P.op("vector", lambda e, swc=swc: e.tensor_scalar(
                SCR[:, 0:1024], ABIG[:, 0:1024], swc, 0.5, ALU.mult, ALU.is_ge),
                 reads=["abig", "fswb"], writes=["scrA"])
            P.op("vector", lambda e, swc=swc: e.tensor_scalar(
                SCR[:, 1024:2048], ABIG[:, 0:1024], swc, -0.5, ALU.mult, ALU.is_le),
                 reads=["abig", "fswb"], writes=["scrB"])
            P.op("vector", lambda e: e.tensor_tensor(
                PBF[:, 0:1024], SCR[:, 0:1024], SCR[:, 1024:2048], ALU.subtract),
                 reads=["scrA", "scrB"], writes=["ps0", "ps1"])
            for c in range(8):
                P.op("tensor", lambda e, c=c: e.transpose(
                    TRbf[:, c * 128:(c + 1) * 128], PBF[:, c * 128:(c + 1) * 128], IDBF[:]),
                     reads=["ps0", "ps1", "IDBF"], writes=["bkTR"])
            for c in range(8):
                P.op("scalar", lambda e, c=c: e.copy(
                    dst_fn(c), TRbf[:, c * 128:(c + 1) * 128]),
                     reads=["bkTR"], writes=[dst_key_fn(c)])

        for ch in range(32):
            tern_quarter(
                lambda ch=ch: P.dma("sync", lambda e: e.dma_start(
                    ABIG[:, 0:1024], f_w1[ch * 128:(ch + 1) * 128, :]), writes=["abig"]),
                SW1C,
                lambda c, ch=ch: XLNF[c][:].bitcast(FP8)[:, ch * 128:(ch + 1) * 128],
                lambda c: f"xlnf{c}")
        for ch in range(8):
            for qr in range(4):
                def dst(c, ch=ch, qr=qr):
                    fc = qr * 8 + c
                    return CTXF[fc // 4][:].bitcast(FP8)[
                        :, (fc % 4) * 1024 + ch * 128: (fc % 4) * 1024 + (ch + 1) * 128]
                tern_quarter(
                    lambda ch=ch, qr=qr: P.dma("sync", lambda e: e.dma_start(
                        ABIG[:, 0:1024],
                        f_w2[ch * 128:(ch + 1) * 128, qr * 1024:(qr + 1) * 1024]),
                        writes=["abig"]),
                    SW2C, dst, lambda c, qr=qr: f"ctxf{(qr * 8 + c) // 4}")

        def W1T(c):
            return XLNF[c][:].bitcast(FP8)
        def W2T(fc):
            return CTXF[fc // 4][:].bitcast(FP8)[:, (fc % 4) * 1024:(fc % 4 + 1) * 1024]

        for tc in range(NT if upto >= 2 else 0):
            rcol = STAT[:, 16 + tc:17 + tc]
            mcol = STAT[:, tc:tc + 1]
            xa = XRES[:, tc * D:(tc + 1) * D]
            P.op("vector", lambda e, rcol=rcol: e.tensor_tensor(
                SMALL[:, 40:41], rcol, rcol, ALU.mult), reads=[f"str{tc}"], writes=["f40"])
            P.op("vector", lambda e: e.tensor_scalar(
                SMALL[:, 41:42], SMALL[:, 40:41], -1e-5, 1.0 + 1e-5, ALU.mult, ALU.add),
                 reads=["f40"], writes=["f41"])
            P.op("scalar", lambda e: e.activation(SMALL[:, 42:43], SMALL[:, 41:42], AF.Sqrt),
                 reads=["f41"], writes=["f42"])
            P.op("vector", lambda e: e.reciprocal(SMALL[:, 43:44], SMALL[:, 42:43]),
                 reads=["f42"], writes=["f43"])
            P.op("vector", lambda e, xa=xa: e.tensor_reduce(
                SMALL[:, 44:45], xa, AX, ALU.max), reads=[f"xr{tc}"], writes=["f44"])
            P.op("vector", lambda e, xa=xa: e.tensor_reduce(
                SMALL[:, 45:46], xa, AX, ALU.min), reads=[f"xr{tc}"], writes=["f45"])
            P.op("vector", lambda e, mcol=mcol: e.tensor_tensor(
                SMALL[:, 44:45], SMALL[:, 44:45], mcol, ALU.subtract),
                 reads=["f44", f"stm{tc}"], writes=["f44"])
            P.op("vector", lambda e, mcol=mcol: e.tensor_tensor(
                SMALL[:, 45:46], mcol, SMALL[:, 45:46], ALU.subtract),
                 reads=["f45", f"stm{tc}"], writes=["f45"])
            P.op("vector", lambda e: e.tensor_tensor(
                SMALL[:, 44:45], SMALL[:, 44:45], SMALL[:, 45:46], ALU.max),
                 reads=["f44", "f45"], writes=["f44"])
            P.op("vector", lambda e, rcol=rcol: e.tensor_tensor(
                SMALL[:, 45:46], SMALL[:, 44:45], rcol, ALU.mult),
                 reads=["f44", f"str{tc}"], writes=["f45"])
            P.op("vector", lambda e: e.tensor_scalar(
                SMALL[:, 46:47], SMALL[:, 45:46], SMALL[:, 43:44], 1e-5, ALU.mult, ALU.max),
                 reads=["f45", "f43"], writes=["f46"])
            P.op("vector", lambda e: e.reciprocal(SMALL[:, 47:48], SMALL[:, 46:47]),
                 reads=["f46"], writes=["f47"])
            P.op("vector", lambda e: e.tensor_scalar(
                SMALL[:, 47:48], SMALL[:, 47:48], 127.0, None, ALU.mult),
                 reads=["f47"], writes=["f47"])
            P.op("vector", lambda e, rcol=rcol: e.tensor_tensor(
                SMALL[:, 48:49], SMALL[:, 43:44], rcol, ALU.mult),
                 reads=["f43", f"str{tc}"], writes=["f48"])
            P.op("vector", lambda e: e.tensor_tensor(
                SMALL[:, 48:49], SMALL[:, 48:49], SMALL[:, 47:48], ALU.mult),
                 reads=["f48", "f47"], writes=["f48"])
            P.op("vector", lambda e, mcol=mcol: e.tensor_scalar(
                SMALL[:, 49:50], mcol, SMALL[:, 48:49], -1.0, ALU.mult, ALU.mult),
                 reads=[f"stm{tc}", "f48"], writes=["f49"])
            P.op("vector", lambda e: e.tensor_scalar(
                SMALL[:, 50:51], SMALL[:, 46:47], 1.0 / 127.0, C1C, ALU.mult, ALU.mult),
                 reads=["f46", "fswb"], writes=["f50"])
            P.op("vector", lambda e: e.tensor_scalar(
                SMALL[:, 51:52], SMALL[:, 47:48], SW1C, None, ALU.mult),
                 reads=["f47", "fswb"], writes=["f51"])
            P.op("tensor", lambda e: e.transpose(TR_ps[0:1, 16:144], SMALL[:, 51:52], ID32[:]),
                 reads=["f51", "ID32"], writes=["bkTR"])
            P.op("vector", lambda e: e.tensor_copy(U1R[0:1, 0:128], TR_ps[0:1, 16:144]),
                 reads=["bkTR"], writes=["u1r"])
            P.op("scalar", lambda e, xa=xa: e.activation(
                SCR[:, 0:1024], xa, AF.Identity, bias=SMALL[:, 49:50], scale=SMALL[:, 48:49]),
                 reads=[f"xr{tc}", "f48", "f49"], writes=["scrA"])
            P.op("vector", lambda e: e.tensor_scalar(
                SCR[:, 1024:2048], SCR[:, 0:1024], M32, M32, ALU.add, ALU.subtract),
                 reads=["scrA"], writes=["scrB"])
            P.op("vector", lambda e: e.tensor_scalar(
                SCR[:, 0:1024], SCR[:, 0:1024], M64, M64, ALU.add, ALU.subtract),
                 reads=["scrA"], writes=["scrA"])
            P.op("vector", lambda e: e.tensor_tensor(
                SCR[:, 0:1024], SCR[:, 0:1024], SCR[:, 1024:2048], ALU.subtract),
                 reads=["scrA", "scrB"], writes=["scrA"])
            P.op("vector", lambda e: e.scalar_tensor_tensor(
                PBF[:, 0:1024], SCR[:, 0:1024], PDET[:, 0:1], SCR[:, 1024:2048],
                ALU.mult, ALU.add),
                 reads=["scrA", "scrB", "PDET"], writes=["ps0", "ps1"])
            for c in range(8):
                P.op("tensor", lambda e, c=c: e.transpose(
                    TRbf[:, c * 128:(c + 1) * 128], PBF[:, c * 128:(c + 1) * 128], IDBF[:]),
                     reads=["ps0", "ps1", "IDBF"], writes=["bkTR"])
            for c in range(8):
                P.op("vector", lambda e, c=c: e.tensor_copy(
                    QKV[:, c * 128:(c + 1) * 128], TRbf[:, c * 128:(c + 1) * 128]),
                     reads=["bkTR"], writes=["qkvA"])
            P.op("vector", lambda e: e.memset(SMALL[:, 114:115], 0.0), writes=["fmx"])
            for fs in range(8):
                bk = fs % 4
                for c in range(8):
                    P.op("tensor", lambda e, c=c, fs=fs, bk=bk: e.matmul(
                        S_ps[:, bk * 512:bk * 512 + 512],
                        QKV[:, c * 128:(c + 1) * 128],
                        W1T(c)[:, fs * 512:(fs + 1) * 512],
                        start=(c == 0), stop=False),
                         reads=["qkvA", f"xlnf{c}"], writes=[SKEY[bk]])
                P.op("tensor", lambda e, fs=fs, bk=bk: e.matmul(
                    S_ps[:, bk * 512:bk * 512 + 512], U1R[0:1, 0:128],
                    ROWS[0:1, fs * 512:(fs + 1) * 512], start=False, stop=True),
                     reads=["u1r", "rows_bin"], writes=[SKEY[bk]])
                P.op("scalar", lambda e, fs=fs, bk=bk: e.activation(
                    ABIG[:, fs * 512:(fs + 1) * 512], S_ps[:, bk * 512:bk * 512 + 512],
                    AF.Gelu, scale=SMALL[:, 50:51]),
                     reads=[SKEY[bk], "f50"], writes=["abig"])
                P.op("vector", lambda e, fs=fs: e.bn_stats(
                    SMALL[:, 64 + 6 * fs:70 + 6 * fs], ABIG[:, fs * 512:(fs + 1) * 512]),
                     reads=["abig"], writes=[f"fbn{fs}"])
                P.op("vector", lambda e, fs=fs: e.tensor_reduce(
                    SMALL[:, 115:116], ABIG[:, fs * 512:(fs + 1) * 512], AX, ALU.max,
                    apply_absolute_value=True),
                     reads=["abig"], writes=["fred2"])
                P.op("vector", lambda e: e.tensor_tensor(
                    SMALL[:, 114:115], SMALL[:, 114:115], SMALL[:, 115:116], ALU.max),
                     reads=["fred2", "fmx"], writes=["fmx"])
            P.op("vector", lambda e: e.bn_aggr(SMALL[:, 112:114], SMALL[:, 64:112]),
                 reads=[f"fbn{fs}" for fs in range(8)], writes=["fmv2"])
            P.op("vector", lambda e: e.tensor_tensor(
                SMALL[:, 52:53], SMALL[:, 112:113], SMALL[:, 112:113], ALU.mult),
                 reads=["fmv2"], writes=["f52"])
            P.op("vector", lambda e: e.tensor_tensor(
                SMALL[:, 52:53], SMALL[:, 52:53], SMALL[:, 113:114], ALU.add),
                 reads=["f52", "fmv2"], writes=["f52"])
            P.op("scalar", lambda e: e.activation(SMALL[:, 53:54], SMALL[:, 52:53],
                                                  AF.Sqrt, bias=CEPS[:, 0:1]),
                 reads=["f52", "ceps"], writes=["f53"])
            P.op("vector", lambda e: e.reciprocal(SMALL[:, 54:55], SMALL[:, 53:54]),
                 reads=["f53"], writes=["f54"])
            P.op("vector", lambda e: e.tensor_tensor(
                SMALL[:, 55:56], SMALL[:, 54:55], SMALL[:, 114:115], ALU.mult),
                 reads=["f54", "fmx"], writes=["f55"])
            P.op("vector", lambda e: e.tensor_scalar(
                SMALL[:, 56:57], SMALL[:, 55:56], 1.0, 1e-5, ALU.mult, ALU.max),
                 reads=["f55"], writes=["f56"])
            P.op("vector", lambda e: e.reciprocal(SMALL[:, 57:58], SMALL[:, 56:57]),
                 reads=["f56"], writes=["f57"])
            P.op("vector", lambda e: e.tensor_scalar(
                SMALL[:, 57:58], SMALL[:, 57:58], 127.0, None, ALU.mult),
                 reads=["f57"], writes=["f57"])
            P.op("vector", lambda e: e.tensor_tensor(
                SMALL[:, 58:59], SMALL[:, 54:55], SMALL[:, 57:58], ALU.mult),
                 reads=["f54", "f57"], writes=["f58"])
            P.op("vector", lambda e: e.tensor_scalar(
                SMALL[:, 59:60], SMALL[:, 56:57], 1.0 / 127.0, C2C, ALU.mult, ALU.mult),
                 reads=["f56", "fswb"], writes=["f59"])
            P.op("vector", lambda e: e.tensor_scalar(
                SMALL[:, 60:61], SMALL[:, 57:58], SW2C, None, ALU.mult),
                 reads=["f57", "fswb"], writes=["f60"])
            P.op("tensor", lambda e: e.transpose(TR_ps[0:1, 16:144], SMALL[:, 60:61], ID32[:]),
                 reads=["f60", "ID32"], writes=["bkTR"])
            P.op("vector", lambda e: e.tensor_copy(U1R[0:1, 128:256], TR_ps[0:1, 16:144]),
                 reads=["bkTR"], writes=["u2r"])
            for fs in range(8):
                P.op("vector", lambda e, fs=fs: e.tensor_scalar(
                    SCR[:, 0:512], ABIG[:, fs * 512:(fs + 1) * 512], SMALL[:, 58:59],
                    None, ALU.mult),
                     reads=["abig", "f58"], writes=["scrA"])
                P.op("vector", lambda e: e.tensor_scalar(
                    SCR[:, 512:1024], SCR[:, 0:512], M32, M32, ALU.add, ALU.subtract),
                     reads=["scrA"], writes=["scrB"])
                P.op("vector", lambda e: e.tensor_scalar(
                    SCR[:, 0:512], SCR[:, 0:512], M64, M64, ALU.add, ALU.subtract),
                     reads=["scrA"], writes=["scrA"])
                P.op("vector", lambda e: e.tensor_tensor(
                    SCR[:, 0:512], SCR[:, 0:512], SCR[:, 512:1024], ALU.subtract),
                     reads=["scrA", "scrB"], writes=["scrA"])
                P.op("vector", lambda e: e.scalar_tensor_tensor(
                    PBF[:, 0:512], SCR[:, 0:512], PDET[:, 0:1], SCR[:, 512:1024],
                    ALU.mult, ALU.add),
                     reads=["scrA", "scrB", "PDET"], writes=["ps0"])
                for c in range(4):
                    P.op("tensor", lambda e, c=c: e.transpose(
                        TRbf[:, c * 128:(c + 1) * 128], PBF[:, c * 128:(c + 1) * 128],
                        IDBF[:]),
                         reads=["ps0", "IDBF"], writes=["bkTR"])
                for c in range(4):
                    fc = fs * 4 + c
                    P.op("vector", lambda e, c=c, fc=fc: e.tensor_copy(
                        QKV[:, 1024 + fc * 128: 1024 + (fc + 1) * 128],
                        TRbf[:, c * 128:(c + 1) * 128]),
                         reads=["bkTR"], writes=["qkvA", "qkvB", "qkvC"])
            pk = ["S0", "S1"]
            for half in range(2):
                for fc in range(32):
                    P.op("tensor", lambda e, fc=fc, half=half: e.matmul(
                        S_ps[:, half * 512:(half + 1) * 512],
                        QKV[:, 1024 + fc * 128: 1024 + (fc + 1) * 128],
                        W2T(fc)[:, half * 512:(half + 1) * 512],
                        start=(fc == 0), stop=False),
                         reads=["qkvA", "qkvB", "qkvC", f"ctxf{fc // 4}"], writes=[SKEY[half]])
                P.op("tensor", lambda e, half=half: e.matmul(
                    S_ps[:, half * 512:(half + 1) * 512], U1R[0:1, 128:256],
                    ROWS[0:1, 4096 + half * 512: 4096 + (half + 1) * 512],
                    start=False, stop=True),
                     reads=["u2r", "rows_bout"], writes=[SKEY[half]])
            P.op("vector", lambda e, xa=xa: e.scalar_tensor_tensor(
                xa, S_ps[:, 0:1024], SMALL[:, 59:60], xa, ALU.mult, ALU.add),
                 reads=pk + ["f59", f"xr{tc}"], writes=[f"xr{tc}"])

    STRIPBF = ABIG[:].bitcast(BF16)

    def emit_l1_attn():
        for si, s in enumerate("qkvo"):
            P.dma("gpsimd", lambda e, si=si, s=s: e.dma_start(
                ROWS[0:1, si * 1024:(si + 1) * 1024], a_b[s].unsqueeze(0)),
                writes=["rows_bin", "rows_bout"])
        P.dma("sync", lambda e: e.dma_start(SCR[0:32, 1024:1040], rel_e),
              writes=["scrB"])
        P.op("vector", lambda e: e.memset(SMALL[0:16, 94:95], 0.0), writes=["pbm"])
        for ch in range(8):
            w = 512 if ch < 7 else 511
            P.dma("sync", lambda e, ch=ch, w=w: e.dma_start(
                SCR[0:32, 0:w], c_oh[:, ch * 512:ch * 512 + w]), writes=["scrA"])
            P.op("tensor", lambda e, w=w: e.matmul(
                AX_ps[0:16, 0:w], SCR[0:32, 1024:1040], SCR[0:32, 0:w],
                start=True, stop=True),
                 reads=["scrA", "scrB"], writes=["bkAX"])
            P.op("vector", lambda e, w=w: e.tensor_copy(
                SCR[0:16, 1040:1040 + w],
                bass.AP(AX_ps[0:1, 0:1].tensor, w - 1, [[512, 16], [-1, w]])),
                 reads=["bkAX"], writes=["pbev"])
            P.dma("gpsimd", lambda e, ch=ch, w=w: e.dma_start(
                pb_dram[:, 4223 - ch * 512 - w: 4223 - ch * 512], SCR[0:16, 1040:1040 + w]),
                reads=["pbev"], writes=["pbd"])
            P.op("vector", lambda e, w=w: e.tensor_reduce(
                SMALL[0:16, 95:96], AX_ps[0:16, 0:w], AX, ALU.max,
                apply_absolute_value=True),
                 reads=["bkAX"], writes=["pbr"])
            P.op("vector", lambda e: e.tensor_tensor(
                SMALL[0:16, 94:95], SMALL[0:16, 94:95], SMALL[0:16, 95:96], ALU.max),
                 reads=["pbr", "pbm"], writes=["pbm"])
        P.op("vector", lambda e: e.memset(SCR[0:16, 1040:1553], 0.0), writes=["pbev"])
        P.dma("gpsimd", lambda e: e.dma_start(pb_dram[:, 0:128], SCR[0:16, 1040:1168]),
              reads=["pbev"], writes=["pbd"])
        P.dma("gpsimd", lambda e: e.dma_start(pb_dram[:, 4223:4608], SCR[0:16, 1040:1425]),
              reads=["pbev"], writes=["pbd"])
        P.op("tensor", lambda e: e.transpose(TR_ps[0:1, 0:16], SMALL[0:16, 94:95],
                                             ID32[0:16, 0:16]),
             reads=["pbm", "ID32"], writes=["bkTR"])
        P.op("vector", lambda e: e.tensor_copy(SMALL[0:1, 96:112], TR_ps[0:1, 0:16]),
             reads=["bkTR"], writes=["pbrow"])
        P.op("vector", lambda e: e.tensor_reduce(SMALL[0:1, 113:114], SMALL[0:1, 96:112],
                                                 AX, ALU.max),
             reads=["pbrow"], writes=["pbrow"])
        for j in range(2):
            P.dma("gpsimd", lambda e, j=j: e.dma_start(
                SMALL[j:j + 1, 112:113],
                bass.AP(SMALL[0:1, 0:1].tensor, 113, [[128, 1], [1, 1]])),
                reads=["pbrow"], writes=["pbm2"])
        P.dma("sync", lambda e: e.dma_start(SMALL[0:2, 104:112], gru_c2), writes=["cc2"])
        P.dma("sync", lambda e: e.dma_start(SCR[0:64, 1600:1608], gru_w), writes=["gruw"])
        P.dma("sync", lambda e: e.dma_start(SCR[0:8, 1608:1610], c_p4), writes=["gruw"])
        P.dma("sync", lambda e: e.dma_start(SCR[0:8, 1610:1611], gru_b.unsqueeze(1)),
              writes=["gruw"])
        P.op("tensor", lambda e: e.transpose(TR_ps[0:8, 64:128], SCR[0:64, 1600:1608],
                                             ID32[0:64, 0:64]),
             reads=["gruw", "ID32"], writes=["bkTR"])
        P.op("vector", lambda e: e.tensor_copy(SCR[0:8, 1612:1676], TR_ps[0:8, 64:128]),
             reads=["bkTR"], writes=["gruw2"])
        P.op("tensor", lambda e: e.matmul(TR_ps[0:64, 100:102], SCR[0:8, 1612:1676],
                                          SCR[0:8, 1608:1610], start=True, stop=True),
             reads=["gruw2", "gruw"], writes=["bkTR"])
        P.op("tensor", lambda e: e.matmul(TR_ps[64:128, 102:104], SCR[0:8, 1612:1676],
                                          SCR[0:8, 1608:1610], start=True, stop=True),
             reads=["gruw2", "gruw"], writes=["bkTR"])
        P.op("vector", lambda e: e.memset(GAB[:], 0.0), writes=["gab"])
        P.op("vector", lambda e: e.tensor_copy(GAB[0:64, 0:2], TR_ps[0:64, 100:102]),
             reads=["bkTR", "gab"], writes=["gab"])
        P.op("vector", lambda e: e.tensor_copy(GAB[64:128, 2:4], TR_ps[64:128, 102:104]),
             reads=["bkTR", "gab"], writes=["gab"])
        P.op("tensor", lambda e: e.matmul(TR_ps[0:1, 104:106], SCR[0:8, 1610:1611],
                                          SCR[0:8, 1608:1610], start=True, stop=True),
             reads=["gruw"], writes=["bkTR"])
        P.op("vector", lambda e: e.tensor_copy(SMALL[0:1, 88:90], TR_ps[0:1, 104:106]),
             reads=["bkTR"], writes=["bg2r"])
        for j in range(2):
            P.dma("gpsimd", lambda e, j=j: e.dma_start(
                SMALL[j:j + 1, 90:91], bass.AP(SMALL[0:1, 0:1].tensor, 88, [[128, 1], [1, 1]])),
                reads=["bg2r"], writes=["bga"])
            P.dma("gpsimd", lambda e, j=j: e.dma_start(
                SMALL[j:j + 1, 91:92], bass.AP(SMALL[0:1, 0:1].tensor, 89, [[128, 1], [1, 1]])),
                reads=["bg2r"], writes=["bgb"])

        for hp in range(8):
            hA = 2 * hp
            for si, s in enumerate("qkv"):
                for c in range(8):
                    P.dma("gpsimd", lambda e, s=s, si=si, hp=hp, c=c: e.dma_start(
                        WSL[:, si * 1024 + c * 128: si * 1024 + (c + 1) * 128],
                        a_w[s][c * 128:(c + 1) * 128, hp * 128:(hp + 1) * 128]),
                        writes=[f"wsl{s}"])
            for h01 in range(2):
                h = hA + h01
                P.dma("gpsimd", lambda e, h=h, h01=h01: e.dma_start(
                    STRIPBF[:, h01 * 4096:(h01 + 1) * 4096],
                    bass.AP(pb_dram.tensor, h * 4608, [[1, 128], [1, 4096]])),
                    reads=["pbd"], writes=[f"strip{h01}", "abig"])
            for si, s in enumerate("qkv"):
                for tc4 in range(4):
                    for c in range(8):
                        P.op("tensor", lambda e, si=si, c=c, tc4=tc4: e.matmul(
                            AX_ps[:, 0:512],
                            WSL[:, si * 1024 + c * 128: si * 1024 + (c + 1) * 128],
                            XLNF[c][:, tc4 * 512:(tc4 + 1) * 512],
                            start=(c == 0), stop=False),
                             reads=[f"wsl{s}", f"xlnf{c}"], writes=["bkAX"])
                    P.op("tensor", lambda e, si=si, hp=hp: e.matmul(
                        AX_ps[:, 0:512],
                        ROWS[0:1, si * 1024 + hp * 128: si * 1024 + (hp + 1) * 128],
                        ONER[0:1, 0:512], start=False, stop=True),
                         reads=["rows_bin", "ONER"], writes=["bkAX"])
                    P.op("scalar", lambda e, si=si, tc4=tc4: e.activation(
                        QKV[:, si * T + tc4 * 512: si * T + (tc4 + 1) * 512],
                        AX_ps[:, 0:512], AF.Identity, scale=(0.125 if si == 0 else 1.0)),
                         reads=["bkAX"], writes=[["qkvA", "qkvB", "qkvC"][si]])
            for kt in range(16):
                P.op("tensor", lambda e, kt=kt: e.transpose(
                    TRbf[:, (kt % 8) * 128:(kt % 8 + 1) * 128],
                    QKV[:, 2 * T + kt * 128: 2 * T + (kt + 1) * 128], IDBF[:]),
                     reads=["qkvC", "IDBF"], writes=["bkTR"])
                P.op("vector", lambda e, kt=kt: e.tensor_copy(
                    VT[:, kt * 128:(kt + 1) * 128], TRbf[:, (kt % 8) * 128:(kt % 8 + 1) * 128]),
                     reads=["bkTR"], writes=["vt"])
            P.op("scalar", lambda e: e.activation(PBF[:, 0:2048], QKV[:, 0:T], AF.Square),
                 reads=["qkvA"], writes=["ps0", "ps1", "ps2", "ps3"])
            for tc4 in range(4):
                P.op("tensor", lambda e, tc4=tc4: e.matmul(
                    S_ps[0:2, tc4 * 512:(tc4 + 1) * 512], ES2[:],
                    PBF[:, tc4 * 512:(tc4 + 1) * 512], start=True, stop=True),
                     reads=["ps0", "ps1", "ps2", "ps3", "ES2"], writes=[SKEY[tc4]])
            P.op("scalar", lambda e: e.activation(SCR[0:2, 0:2048], S_ps[0:2, 0:2048],
                                                  AF.Sqrt, bias=CEPS[0:2, 1:2]),
                 reads=SKEY + ["ceps"], writes=["negB", "scrA", "scrB"])
            P.op("scalar", lambda e: e.activation(PBF[:, 0:2048], QKV[:, T:2 * T], AF.Square),
                 reads=["qkvB"], writes=["ps0", "ps1", "ps2", "ps3"])
            for tc4 in range(4):
                P.op("tensor", lambda e, tc4=tc4: e.matmul(
                    S_ps[0:2, tc4 * 512:(tc4 + 1) * 512], ES2[:],
                    PBF[:, tc4 * 512:(tc4 + 1) * 512], start=True, stop=True),
                     reads=["ps0", "ps1", "ps2", "ps3", "ES2"], writes=[SKEY[tc4]])
            P.op("vector", lambda e: e.tensor_reduce(SMALL[0:2, 20:21], S_ps[0:2, 0:2048],
                                                     AX, ALU.max),
                 reads=SKEY, writes=["kmax2"])
            P.op("scalar", lambda e: e.activation(SMALL[0:2, 21:22], SMALL[0:2, 20:21],
                                                  AF.Sqrt, bias=CEPS[0:2, 1:2]),
                 reads=["kmax2", "ceps"], writes=["kmax"])
            P.op("vector", lambda e: e.tensor_scalar(
                SCR[0:2, 0:2048], SCR[0:2, 0:2048], SMALL[0:2, 21:22], None, ALU.mult),
                 reads=["negB", "scrA", "scrB", "kmax"], writes=["negB", "scrA", "scrB"])
            pkeys = ["ps0", "ps1", "ps2", "ps3"]
            for tc4 in range(4):
                P.op("tensor", lambda e, tc4=tc4, hp=hp: e.matmul(
                    S_ps[0:2, tc4 * 512:(tc4 + 1) * 512], GAB[:, 1:4:2],
                    XLNF[hp][:, tc4 * 512:(tc4 + 1) * 512], start=True, stop=True),
                     reads=["gab", f"xlnf{hp}"], writes=[SKEY[tc4]])
            P.op("scalar", lambda e: e.activation(S_ps[0:2, 0:2048], S_ps[0:2, 0:2048],
                                                  AF.Sigmoid, bias=SMALL[0:2, 91:92]),
                 reads=SKEY + ["bgb"], writes=SKEY)
            P.op("vector", lambda e, hp=hp: e.tensor_scalar(
                PBF[0:2, 0:2048], S_ps[0:2, 0:2048], SMALL[0:2, 104 + hp:105 + hp], 1.0,
                ALU.mult, ALU.subtract),
                 reads=SKEY + ["cc2"], writes=pkeys)
            for tc4 in range(4):
                P.op("tensor", lambda e, tc4=tc4, hp=hp: e.matmul(
                    S_ps[0:2, tc4 * 512:(tc4 + 1) * 512], GAB[:, 0:3:2],
                    XLNF[hp][:, tc4 * 512:(tc4 + 1) * 512], start=True, stop=True),
                     reads=["gab", f"xlnf{hp}"], writes=[SKEY[tc4]])
            P.op("scalar", lambda e: e.activation(S_ps[0:2, 0:2048], S_ps[0:2, 0:2048],
                                                  AF.Sigmoid, bias=SMALL[0:2, 90:91]),
                 reads=SKEY + ["bga"], writes=SKEY)
            P.op("vector", lambda e: e.scalar_tensor_tensor(
                PBF[0:2, 0:2048], S_ps[0:2, 0:2048], 1.0, PBF[0:2, 0:2048],
                ALU.mult, ALU.mult),
                 reads=SKEY + pkeys, writes=pkeys)
            P.op("vector", lambda e: e.tensor_scalar(
                PBF[0:2, 0:2048], PBF[0:2, 0:2048], 2.0, None, ALU.add),
                 reads=pkeys, writes=pkeys)
            for qt in range(NT):
                P.op("tensor", lambda e, qt=qt: e.transpose(
                    TRbf[:, 0:2], PBF[0:2, qt * 128:(qt + 1) * 128], IDBF[0:2, 0:2]),
                     reads=pkeys + ["IDBF"], writes=["bkTR"])
                P.op("vector", lambda e, qt=qt, hA=hA: e.tensor_copy(
                    GT[:, qt * 16 + hA: qt * 16 + hA + 2], TRbf[:, 0:2]),
                     reads=["bkTR"], writes=[f"gt{qt}"])
            P.op("scalar", lambda e: e.activation(
                PBF[0:2, 0:2048], PBF[0:2, 0:2048], AF.Abs, scale=SMALL[0:2, 112:113]),
                 reads=pkeys + ["pbm2"], writes=pkeys)
            P.op("vector", lambda e: e.tensor_tensor(
                SCR[0:2, 0:2048], SCR[0:2, 0:2048], PBF[0:2, 0:2048], ALU.add),
                 reads=["negB", "scrA", "scrB"] + pkeys, writes=["negB", "scrA", "scrB"])
            for qt in range(NT):
                P.op("tensor", lambda e, qt=qt: e.transpose(
                    TR_ps[:, 0:2], SCR[0:2, qt * 128:(qt + 1) * 128], ID32[0:2, 0:2]),
                     reads=["negB", "scrA", "scrB", "ID32"], writes=["bkTR"])
                P.op("vector", lambda e, qt=qt, hA=hA: e.tensor_scalar(
                    BT[:, qt * 16 + hA: qt * 16 + hA + 2], TR_ps[:, 0:2], -1.0, None,
                    ALU.mult),
                     reads=["bkTR"], writes=[f"bt{qt}"])

            def l1_mm(qt, kc4):
                for h01 in range(2):
                    sbank = h01 * 2 + kc4 % 2
                    P.op("tensor", lambda e, h01=h01, qt=qt, kc4=kc4, sbank=sbank:
                         e.matmul(
                             S_ps[:, sbank * 512:(sbank + 1) * 512],
                             QKV[h01 * 64:(h01 + 1) * 64, qt * 128:(qt + 1) * 128],
                             QKV[h01 * 64:(h01 + 1) * 64, T + kc4 * 512: T + (kc4 + 1) * 512],
                             start=True, stop=True),
                         reads=["qkvA", "qkvB"], writes=[SKEY[sbank]])

            def l1_tail(qt, kc4):
                TB = TRbf if qt % 2 == 0 else AVbf
                tbk = "bkTR" if qt % 2 == 0 else "bkAV"
                for h01 in range(2):
                    h = hA + h01
                    sbank = h01 * 2 + kc4 % 2
                    scol = sbank * 512
                    pslot = h01 * 1024 + (kc4 % 2) * 512
                    pkey = f"ps{h01 * 2 + kc4 % 2}"
                    stro = h01 * 4096 + 2175 + 128 * qt - kc4 * 512
                    P.op("vector", lambda e, scol=scol, qt=qt, h=h, stro=stro:
                         e.scalar_tensor_tensor(
                             S_ps[:, scol:scol + 512],
                             bass.AP(STRIPBF.tensor, stro, [[8192, 128], [-1, 512]]),
                             GT[:, qt * 16 + h: qt * 16 + h + 1],
                             S_ps[:, scol:scol + 512], ALU.mult, ALU.add),
                         reads=[SKEY[sbank], f"strip{h01}", "abig", f"gt{qt}"],
                         writes=[SKEY[sbank]])
                    zc = 24 + h01 * 2
                    P.op("scalar", lambda e, scol=scol, pslot=pslot, qt=qt, h=h, zc=zc,
                         kc4=kc4: e.activation(
                             PBF[:, pslot:pslot + 512], S_ps[:, scol:scol + 512], AF.Exp,
                             bias=BT[:, qt * 16 + h: qt * 16 + h + 1],
                             accum_out=SMALL[:, zc + kc4 % 2: zc + kc4 % 2 + 1]),
                         reads=[SKEY[sbank], f"bt{qt}"],
                         writes=[pkey, f"zz{h01}_{kc4}"])
                    if kc4 % 2 == 1:
                        dst = 32 + h01 + (kc4 // 2) * 4
                        P.op("vector", lambda e, zc=zc, dst=dst: e.tensor_tensor(
                            SMALL[:, dst:dst + 1],
                            SMALL[:, zc:zc + 1], SMALL[:, zc + 1:zc + 2], ALU.add),
                             reads=[f"zz{h01}_{kc4 - 1}", f"zz{h01}_{kc4}"],
                             writes=[f"zp{h01}_{kc4 // 2}"])
                    for bb in range(4):
                        P.op("tensor", lambda e, pslot=pslot, bb=bb, h01=h01, TB=TB:
                             e.transpose(
                                 TB[:, (h01 * 4 + bb) * 128:(h01 * 4 + bb + 1) * 128],
                                 PBF[:, pslot + bb * 128: pslot + (bb + 1) * 128], IDBF[:]),
                             reads=[pkey, "IDBF"], writes=[tbk])
                    P.op("vector", lambda e, h01=h01, TB=TB, kc4=kc4: e.tensor_copy(
                        bass.AP(PTT[0:1, 0:1].tensor, (kc4 * 4) * 256 + h01 * 128,
                                [[4096, 128], [256, 4], [1, 128]]),
                        TB[:, h01 * 512:(h01 + 1) * 512]),
                         reads=[tbk],
                         writes=[f"pt{kc4 * 4}_{h01}", f"pt{kc4 * 4 + 1}_{h01}",
                                 f"pt{kc4 * 4 + 2}_{h01}", f"pt{kc4 * 4 + 3}_{h01}"])
                    if kc4 == 3:
                        P.op("vector", lambda e, h01=h01: e.tensor_tensor(
                            SMALL[:, 40 + h01:41 + h01], SMALL[:, 32 + h01:33 + h01],
                            SMALL[:, 36 + h01:37 + h01], ALU.add),
                             reads=[f"zp{h01}_0", f"zp{h01}_1"], writes=[f"zt{h01}"])
                        P.op("vector", lambda e, h01=h01: e.reciprocal(
                            SMALL[:, 42 + h01:43 + h01], SMALL[:, 40 + h01:41 + h01]),
                             reads=[f"zt{h01}"], writes=[f"zr{h01}"])
                        for kcg in range(16):
                            P.op("tensor", lambda e, kcg=kcg, h01=h01: e.matmul(
                                AY_ps[:, 0:64],
                                PTT[:, kcg * 256 + h01 * 128: kcg * 256 + h01 * 128 + 128],
                                VT[:, kcg * 128 + h01 * 64: kcg * 128 + h01 * 64 + 64],
                                start=(kcg == 0), stop=(kcg == 15)),
                                 reads=[f"pt{kcg}_{h01}", "vt"], writes=["AY"])
                        P.op("vector", lambda e, h01=h01, qt=qt: e.tensor_scalar(
                            CTM[:, (h01 * 2 + qt % 2) * 64:(h01 * 2 + qt % 2) * 64 + 64],
                            AY_ps[:, 0:64], SMALL[:, 42 + h01:43 + h01], None, ALU.mult),
                             reads=["AY", f"zr{h01}"], writes=[f"ctm{h01}_{qt % 2}"])
                        tcol = 768 if h01 == 0 else 896
                        P.op("tensor", lambda e, h01=h01, qt=qt, TB=TB, tcol=tcol:
                             e.transpose(
                                 TB[0:64, tcol:tcol + 128],
                                 CTM[:, (h01 * 2 + qt % 2) * 64:(h01 * 2 + qt % 2) * 64 + 64],
                                 IDBF[:]),
                             reads=[f"ctm{h01}_{qt % 2}", "IDBF"], writes=[tbk])
                        P.op("vector", lambda e, h01=h01, qt=qt, hp=hp, TB=TB, tcol=tcol:
                             e.tensor_copy(
                                 CTXF[hp][h01 * 64:(h01 + 1) * 64, qt * 128:(qt + 1) * 128],
                                 TB[0:64, tcol:tcol + 128]),
                             reads=[tbk], writes=[f"ctxf{hp}"])

            stages = [(qt, kc4) for qt in range(NT) for kc4 in range(4)]
            for i in range(len(stages) + 1):
                if i < len(stages):
                    l1_mm(*stages[i])
                if i > 0:
                    l1_tail(*stages[i - 1])

        def wot_ap(cc):
            if cc < 6:
                return QKV[:, cc * 1024:(cc + 1) * 1024]
            return WSL[:, (cc - 6) * 1024:(cc - 5) * 1024]

        def wot_alias(cc):
            return ["qkvA", "qkvA", "qkvB", "qkvB", "qkvC", "qkvC", "wslq", "wslk"][cc]
        for cc in range(8):
            P.dma("gpsimd", lambda e, cc=cc: e.dma_start(
                wot_ap(cc), a_w["o"][cc * 128:(cc + 1) * 128, :]),
                writes=["wotx", wot_alias(cc)])
        for t in range(NT):
            pk = ["S0", "S1"]
            for c in range(8):
                for half in range(2):
                    P.op("tensor", lambda e, c=c, t=t, half=half: e.matmul(
                        S_ps[:, half * 512:(half + 1) * 512],
                        CTXF[c][:, t * 128:(t + 1) * 128],
                        wot_ap(c)[:, half * 512:(half + 1) * 512],
                        start=(c == 0), stop=False),
                         reads=[f"ctxf{c}", "wotx"], writes=pk)
            for half in range(2):
                P.op("tensor", lambda e, half=half: e.matmul(
                    S_ps[:, half * 512:(half + 1) * 512], ONER[0:1, 0:128],
                    ROWS[0:1, 3072 + half * 512: 3072 + (half + 1) * 512],
                    start=False, stop=True),
                     reads=["ONER", "rows_bout"], writes=pk)
            P.op("vector", lambda e, t=t: e.tensor_tensor(
                XRES[:, t * D:(t + 1) * D], XRES[:, t * D:(t + 1) * D],
                S_ps[:, 0:1024], ALU.add),
                 reads=pk + [f"xr{t}"], writes=[f"xr{t}"])

    SIG7 = 1.0 / (1.0 + float(np.exp(-1.702 * 7.0)))

    def emit_l1_moe(nexp=NE, router_only=False, stage=4):
        ABF32 = ABIG
        DNB = None
        if not router_only and stage >= 4:
            P.dma("gpsimd", lambda e: e.dma_start(QKV[0:16, 5120:6144], m_dnb),
                  writes=["dnball"])
        for c in range(8):
            P.dma("gpsimd", lambda e, c=c: e.dma_start(
                WSL[:, c * 16:(c + 1) * 16], r_w[c * 128:(c + 1) * 128, :]),
                writes=["rw"])
        for tc4 in range(4):
            for c in range(8):
                P.op("tensor", lambda e, c=c, tc4=tc4: e.matmul(
                    AX_ps[0:16, 0:512], WSL[:, c * 16:(c + 1) * 16],
                    XLNF[c][:, tc4 * 512:(tc4 + 1) * 512],
                    start=(c == 0), stop=(c == 7)),
                     reads=["rw", f"xlnf{c}"], writes=["bkAX"])
            P.op("vector", lambda e, tc4=tc4: e.tensor_copy(
                ABIG[0:16, tc4 * 512:(tc4 + 1) * 512], AX_ps[0:16, 0:512]),
                 reads=["bkAX"], writes=["logit"])
        for tc in range(NT):
            P.op("tensor", lambda e, tc=tc: e.transpose(
                TR_ps[:, 0:16], ABIG[0:16, tc * 128:(tc + 1) * 128], ID32[0:16, 0:16]),
                 reads=["logit", "ID32"], writes=["bkTR"])
            P.op("vector", lambda e: e.tensor_copy(SCR[:, 0:16], TR_ps[:, 0:16]),
                 reads=["bkTR"], writes=["lt"])
            P.op("vector", lambda e: e.tensor_reduce(SMALL[:, 44:45], SCR[:, 0:16],
                                                     AX, ALU.max),
                 reads=["lt"], writes=["m1"])
            P.op("vector", lambda e: e.tensor_scalar(
                SCR[:, 16:32], SCR[:, 0:16], SMALL[:, 44:45], None, ALU.is_ge),
                 reads=["lt", "m1"], writes=["eq1"])
            P.op("vector", lambda e: e.scalar_tensor_tensor(
                SCR[:, 32:48], SCR[:, 16:32], -1.0e30, SCR[:, 0:16], ALU.mult, ALU.add),
                 reads=["eq1", "lt"], writes=["l2"])
            P.op("vector", lambda e: e.tensor_reduce(SMALL[:, 45:46], SCR[:, 32:48],
                                                     AX, ALU.max),
                 reads=["l2"], writes=["m2"])
            P.op("vector", lambda e: e.tensor_scalar(
                SCR[:, 48:64], SCR[:, 32:48], SMALL[:, 45:46], None, ALU.is_ge),
                 reads=["l2", "m2"], writes=["eq2"])
            P.op("vector", lambda e: e.tensor_tensor(
                SMALL[:, 46:47], SMALL[:, 45:46], SMALL[:, 44:45], ALU.subtract),
                 reads=["m1", "m2"], writes=["md"])
            P.op("scalar", lambda e: e.activation(SMALL[:, 47:48], SMALL[:, 46:47], AF.Exp),
                 reads=["md"], writes=["dd"])
            P.op("vector", lambda e: e.tensor_scalar(
                SMALL[:, 48:49], SMALL[:, 47:48], 1.0, None, ALU.add),
                 reads=["dd"], writes=["dn"])
            P.op("vector", lambda e: e.reciprocal(SMALL[:, 49:50], SMALL[:, 48:49]),
                 reads=["dn"], writes=["w1"])
            P.op("vector", lambda e: e.tensor_scalar(
                SMALL[:, 50:51], SMALL[:, 49:50], -1.0, 1.0, ALU.mult, ALU.add),
                 reads=["w1"], writes=["w2"])
            P.op("vector", lambda e: e.tensor_scalar(
                SCR[:, 48:64], SCR[:, 48:64], SMALL[:, 50:51], None, ALU.mult),
                 reads=["eq2", "w2"], writes=["eq2"])
            P.op("vector", lambda e, tc=tc: e.scalar_tensor_tensor(
                GT[:, tc * 16:(tc + 1) * 16], SCR[:, 16:32], SMALL[:, 49:50],
                SCR[:, 48:64], ALU.mult, ALU.add),
                 reads=["eq1", "w1", "eq2"], writes=[f"sc{tc}"])
            P.op("tensor", lambda e, tc=tc: e.transpose(
                TR_ps[0:16, 0:128], GT[:, tc * 16:(tc + 1) * 16], ID32[:]),
                 reads=[f"sc{tc}", "ID32"], writes=["bkTR"])
            P.op("vector", lambda e, tc=tc: e.tensor_copy(
                VT[0:16, tc * 128:(tc + 1) * 128], TR_ps[0:16, 0:128]),
                 reads=["bkTR"], writes=["sbt"])

        def gu_ap(par, c):
            return CTXF[par * 4 + c // 2][:, (c % 2) * 1024:(c % 2 + 1) * 1024]

        if router_only:
            return
        for ei in range(nexp):
            par = ei % 2
            rbase = par * 1024
            for c in range(8):
                P.dma("gpsimd", lambda e, ei=ei, c=c, par=par: e.dma_start(
                    gu_ap(par, c), m_gu[ei, c * 128:(c + 1) * 128, :]),
                    writes=[f"guw{par}_{c}"])
            if stage >= 4:
                for c4 in range(4):
                    P.dma("gpsimd", lambda e, ei=ei, c4=c4: e.dma_start(
                        QKV[:, c4 * 1024:(c4 + 1) * 1024],
                        m_dn[ei, c4 * 128:(c4 + 1) * 128, :]),
                        writes=[f"dnw{c4}"])
            P.dma("gpsimd", lambda e, ei=ei, rbase=rbase: e.dma_start(
                ROWS[0:1, rbase:rbase + 1024], m_gub[ei:ei + 1, :]),
                writes=[f"gub{par}"])
            for tc in range(NT):
                gk = ["S0", "S1"]
                for half in range(2):
                    for c in range(8):
                        P.op("tensor", lambda e, c=c, tc=tc, half=half, par=par: e.matmul(
                            S_ps[:, half * 512:(half + 1) * 512],
                            XLNF[c][:, tc * 128:(tc + 1) * 128],
                            gu_ap(par, c)[:, half * 512:(half + 1) * 512],
                            start=(c == 0), stop=False),
                             reads=[f"guw{par}_{c}", f"xlnf{c}"], writes=[gk[half]])
                    P.op("tensor", lambda e, half=half, rbase=rbase: e.matmul(
                        S_ps[:, half * 512:(half + 1) * 512], ONER[0:1, 0:128],
                        ROWS[0:1, rbase + half * 512: rbase + (half + 1) * 512],
                        start=False, stop=True),
                         reads=[f"gub{par}", "ONER"], writes=[gk[half]])
                if stage < 2:
                    P.op("vector", lambda e, tc=tc: e.tensor_tensor(
                        XRES[:, tc * D:(tc + 1) * D], XRES[:, tc * D:(tc + 1) * D],
                        S_ps[:, 0:1024], ALU.add),
                         reads=gk + [f"xr{tc}"], writes=[f"xr{tc}"])
                    continue
                GUp = S_ps[:, 0:1024]
                P.op("vector", lambda e: e.tensor_scalar(
                    SCR[:, 0:512], GUp[:, 0:1024:2], 7.0, None, ALU.min),
                     reads=gk, writes=["gtw"])
                P.op("scalar", lambda e: e.activation(
                    SCR[:, 512:1024], SCR[:, 0:512], AF.Sigmoid, scale=1.702),
                     reads=["gtw"], writes=["sig"])
                P.op("vector", lambda e, tc=tc, ei=ei: e.scalar_tensor_tensor(
                    SCR[:, 0:512], SCR[:, 0:512],
                    GT[:, tc * 16 + ei: tc * 16 + ei + 1], SCR[:, 512:1024],
                    ALU.mult, ALU.mult),
                     reads=["sig", "gtw", f"sc{tc}"], writes=["gtw"])
                P.op("vector", lambda e: e.tensor_scalar(
                    SCR[:, 512:1024], GUp[:, 1:1024:2], -7.0, 7.0, ALU.max, ALU.min),
                     reads=gk, writes=["sig"])
                P.op("vector", lambda e: e.scalar_tensor_tensor(
                    PBF[:, 0:512], SCR[:, 512:1024], 1.0, SCR[:, 0:512],
                    ALU.add, ALU.mult),
                     reads=["sig", "gtw"], writes=["ps0"])
                if 2 <= stage < 3:
                    P.op("vector", lambda e, tc=tc: e.tensor_tensor(
                        XRES[:, tc * D:(tc + 1) * D][:, 0:512],
                        XRES[:, tc * D:(tc + 1) * D][:, 0:512],
                        PBF[:, 0:512], ALU.add),
                         reads=["ps0", f"xr{tc}"], writes=[f"xr{tc}"])
                    continue
                for c in range(4):
                    P.op("tensor", lambda e, c=c: e.transpose(
                        TRbf[:, c * 128:(c + 1) * 128], PBF[:, c * 128:(c + 1) * 128],
                        IDBF[:]),
                         reads=["ps0", "IDBF"], writes=["bkTR"])
                for c in range(4):
                    P.op("vector", lambda e, c=c, par=par: e.tensor_copy(
                        QKV[:, 4096 + par * 512 + c * 128: 4096 + par * 512 + (c + 1) * 128],
                        TRbf[:, c * 128:(c + 1) * 128]),
                         reads=["bkTR"], writes=[f"actw{par}"])
                if stage == 3:
                    P.op("vector", lambda e, tc=tc, par=par: e.tensor_tensor(
                        XRES[:, tc * D:(tc + 1) * D][:, 0:512],
                        XRES[:, tc * D:(tc + 1) * D][:, 0:512],
                        QKV[:, 4096 + par * 512: 4608 + par * 512], ALU.add),
                         reads=[f"actw{par}", f"xr{tc}"], writes=[f"xr{tc}"])
                    continue
                dk = ["S2", "S3"]
                for half in range(2):
                    for c4 in range(4):
                        P.op("tensor", lambda e, c4=c4, half=half, par=par, ei=ei: e.matmul(
                            S_ps[:, 1024 + half * 512: 1024 + (half + 1) * 512],
                            QKV[:, 4096 + par * 512 + c4 * 128: 4096 + par * 512 + (c4 + 1) * 128],
                            QKV[:, c4 * 1024 + half * 512: c4 * 1024 + (half + 1) * 512],
                            start=(c4 == 0), stop=(c4 == 3 and ei != 0)),
                             reads=[f"actw{par}", f"dnw{c4}"], writes=[dk[half]])
                    if ei == 0:
                        P.op("tensor", lambda e, half=half, tc=tc: e.matmul(
                            S_ps[:, 1024 + half * 512: 1024 + (half + 1) * 512],
                            VT[0:16, tc * 128:(tc + 1) * 128],
                            QKV[0:16, 5120 + half * 512: 5120 + (half + 1) * 512],
                            start=False, stop=True),
                             reads=["sbt", "dnball"], writes=[dk[half]])
                P.op("vector", lambda e, tc=tc: e.tensor_tensor(
                    XRES[:, tc * D:(tc + 1) * D], XRES[:, tc * D:(tc + 1) * D],
                    S_ps[:, 1024:2048], ALU.add),
                     reads=dk + [f"xr{tc}"], writes=[f"xr{tc}"])

    def emit_out():
        outs = []
        for t in range(NT):
            outs.append(P.dma("gpsimd", lambda e, t=t: e.dma_start(
                y_out[t * 128:(t + 1) * 128, :], XRES[:, t * D:(t + 1) * D]),
                reads=[f"xr{t}"]))
        P.op("gpsimd", lambda e: e.engine_nop(), extra_deps=outs)

    emit_ln(0, fm=True)
    if upto >= 1:
        emit_l0_attn()
    if upto >= 1.5:
        emit_ln(1, fm=False)
        if debug:
            _x1d = nc.dram_tensor("dbg_x1", [T, D], F32, kind="ExternalOutput").ap()
            _x1deps = []
            for t in range(NT):
                _x1deps.append(P.dma("gpsimd", lambda e, t=t: e.dma_start(
                    _x1d[t * 128:(t + 1) * 128, :], XRES[:, t * D:(t + 1) * D]),
                    reads=[f"xr{t}"]))
            P.op("gpsimd", lambda e: e.engine_nop(), extra_deps=_x1deps)
            _std = nc.dram_tensor("dbg_stat1", [128, 48], F32, kind="ExternalOutput").ap()
            _i = P.dma("gpsimd", lambda e: e.dma_start(_std, STAT[:, 0:48]),
                       reads=[f"stb{t}" for t in range(NT)] + [f"str{t}" for t in range(NT)])
            P.op("gpsimd", lambda e: e.engine_nop(), extra_deps=[_i])
        if upto >= 1.7:
            emit_l0_ffn()
    if upto >= 3:
        emit_ln(2, fm=True)
        emit_l1_attn()
    if upto >= 3.5:
        emit_ln(3, fm=True)
        _stage = {3.71: 1, 3.72: 2, 3.721: 2.1, 3.722: 2.2, 3.7225: 2.25, 3.723: 2.3,
                  3.724: 2.4, 3.73: 3}.get(upto, 4)
        emit_l1_moe(nexp=(16 if upto >= 4 else (2 if upto >= 3.7 else 0)),
                    router_only=(upto < 3.7), stage=_stage)
    emit_out()

    if debug:
        def dump(name, ap, reads):
            dt_ = nc.dram_tensor(name, list(ap.shape), F32, kind="ExternalOutput").ap()
            i = P.dma("gpsimd", lambda e: e.dma_start(dt_, ap), reads=reads)
            P.op("gpsimd", lambda e: e.engine_nop(), extra_deps=[i])
        dump("dbg_xlnf0", XLNF[0][:], ["xlnf0"])
        dump("dbg_stat", STAT[:, 0:48], ["stb15"])
        if upto == 1:
            dump("dbg_ctxf0", CTXF[0][:], ["ctxf0"])
            dump("dbg_qkv", QKV[:], ["qkvA", "qkvB", "qkvC"])
            dump("dbg_bt", BT[:], ["bt15"])
            dump("dbg_vt", VT[:], ["vt"])
        if upto == 2:
            dump("dbg_hg", ABIG[:], ["abig" for fs in range(8)])
            dump("dbg_fsc", SMALL[:, 40:61], ["f59", "f58", "f50", "f48"])
            dump("dbg_w1t", XLNF[0][:].bitcast(FP8), ["xlnf0"])
            dump("dbg_w2t", CTXF[0][:].bitcast(FP8), ["ctxf0"])
            dump("dbg_hqf", QKV[:, 1024:5120], ["qkvA", "qkvB", "qkvC"])
            dump("dbg_sw", SMALL[:, 120:127], ["fswb", "fac"])

    P.materialize(nc, ctx)
    ctx.close()
    return nc


def pack_inputs(inputs):
    import ml_dtypes
    ins = {
        "w_in": inputs["L0_attn_w_in"], "b_in": inputs["L0_attn_b_in"],
        "w_out": inputs["L0_attn_w_out"], "b_out": inputs["L0_attn_b_out"],
        "f_w1": inputs["L0_ffn_w1"], "f_b1": inputs["L0_ffn_b1"],
        "f_w2": inputs["L0_ffn_w2"], "f_b2": inputs["L0_ffn_b2"],
        "a_wq": inputs["L1_attn_Wq"], "a_bq": inputs["L1_attn_bq"],
        "a_wk": inputs["L1_attn_Wk"], "a_bk": inputs["L1_attn_bk"],
        "a_wv": inputs["L1_attn_Wv"], "a_bv": inputs["L1_attn_bv"],
        "a_wo": inputs["L1_attn_Wo"], "a_bo": inputs["L1_attn_bo"],
        "rel_e": inputs["L1_attn_rel_embed"],
        "gru_w": inputs["L1_attn_gru_w"], "gru_b": inputs["L1_attn_gru_b"],
        "gru_c": np.asarray(inputs["L1_attn_gru_const"]).reshape(16),
        "gru_c2": np.asarray(inputs["L1_attn_gru_const"]).reshape(8, 2).T,
        "r_w": inputs["L1_ffn_router_weight"],
        "m_gu": inputs["L1_ffn_gate_up_proj"], "m_gub": inputs["L1_ffn_gate_up_proj_bias"],
        "m_dn": inputs["L1_ffn_down_proj"], "m_dnb": inputs["L1_ffn_down_proj_bias"],
        "c_oh": _bucket_onehot(),
        "c_id32": np.eye(128, dtype=np.float32),
        "c_idbf": np.eye(128, dtype=ml_dtypes.bfloat16),
        "c_p4": np.kron(np.eye(2, dtype=np.float32), np.ones((4, 1), np.float32)),
    }
    es2 = np.zeros((128, 2), dtype=ml_dtypes.bfloat16)
    es2[0:64, 0] = 1; es2[64:128, 1] = 1
    ins["c_es2"] = es2
    mi, mf, ml = _l0_masks()
    ins["c_mi"] = mi.astype(ml_dtypes.bfloat16)
    ins["c_mf"] = mf.astype(ml_dtypes.bfloat16)
    ins["c_ml"] = ml.astype(ml_dtypes.bfloat16)
    return {k: np.ascontiguousarray(v) for k, v in ins.items()}


def kernel(**inputs):
    ins = pack_inputs(inputs)
    x = np.ascontiguousarray(inputs["L0_attn_x"], dtype=np.float32)
    nc = build_program()
    in_maps = []
    for c in range(8):
        m = dict(ins)
        m["x"] = np.ascontiguousarray(x[c])
        in_maps.append(m)
    res = run_bass_kernel_spmd(nc, in_maps, core_ids=list(range(8)))
    out = np.stack([np.asarray(res.results[c]["y"]) for c in range(8)], axis=0)
    return out.astype(np.float32)


# version 09
# speedup vs baseline: 1.2541x; 1.2366x over previous
import numpy as np
from contextlib import ExitStack

import concourse.bass as bass
import concourse.mybir as mybir
from concourse.alu_op_type import AluOpType as ALU
from concourse.bass_utils import run_bass_kernel_spmd

F32 = mybir.dt.float32
BF16 = mybir.dt.bfloat16
FP8 = mybir.dt.float8e4
AF = mybir.ActivationFunctionType
AX = mybir.AxisListType.X

B, T, D = 8, 2048, 1024
NH, HD = 16, 64
NT = T // 128
FF = 4096
NE, NI = 16, 512
NEG = -1.0e30
M32 = 12582912.0
M64 = 6755399441055744.0

ENGINES = ("tensor", "vector", "scalar", "sync", "gpsimd")


def _bucket_onehot():
    rel = np.arange(4095, dtype=np.int64) - 2047
    nb = 16
    buckets = (rel > 0).astype(np.int64) * nb
    rp = np.abs(rel)
    max_exact = nb // 2
    is_small = rp < max_exact
    large = max_exact + (
        np.log(np.maximum(rp, 1) / max_exact) / np.log(128 / max_exact) * (nb - max_exact)
    ).astype(np.int64)
    large = np.minimum(large, nb - 1)
    buckets = buckets + np.where(is_small, rp, large)
    oh = np.zeros((32, 4095), dtype=np.float32)
    oh[buckets, np.arange(4095)] = 1.0
    return oh


def _l0_masks():
    i = np.arange(128)[:, None]
    j384 = np.arange(384)[None, :]
    j256 = np.arange(256)[None, :]
    inter = np.where((j384 >= i) & (j384 <= i + 256), 0.0, NEG).astype(np.float32)
    first = np.where(j256 <= i + 128, 0.0, NEG).astype(np.float32)
    last = np.where(j256 >= i, 0.0, NEG).astype(np.float32)
    return inter, first, last


class _MockInst:
    def then_inc(self, *a, **k):
        return self


class _MockEng:
    def __init__(self):
        self.calls = []

    def __getattr__(self, name):
        def f(*args, **kwargs):
            self.calls.append((name, args, kwargs))
            return _MockInst()
        return f


def _ap_is_psum(ap):
    try:
        return "psum" in str(getattr(ap, "space", "")).lower()
    except Exception:
        return False


def _ap_keys(ap, gran_sb=512, gran_ps=2048):
    try:
        t = ap.tensor
        name = t.name
        esz = {"float32": 4, "bfloat16": 2, "float16": 2, "int32": 4,
               "float8e4": 1, "int8": 1, "uint8": 1}.get(str(ap.dtype).split(".")[-1], 4)
        dims = [list(d) for d in ap.ap]
        space = str(getattr(ap, "space", ""))
        onchip = ("SB" in space) or ("PSUM" in space) or ("Psum" in space) or \
                 ("State" in space) or ("sb" in space.lower() and "dram" not in space.lower())
        is_ps = "psum" in space.lower()
        if not onchip and "dram" not in space.lower():
            onchip = True
        use = dims[1:] if (onchip and len(dims) > 1) else dims
        off = int(ap.offset)
        if onchip and len(dims) > 1 and dims[0][0] > 0:
            off = off % int(dims[0][0])
        lo = hi = off
        for st, ct in use:
            ext = int(st) * (int(ct) - 1)
            if ext >= 0:
                hi += ext
            else:
                lo += ext
        gran = gran_ps if is_ps else gran_sb
        b0 = (lo * esz) // gran
        b1 = (hi * esz) // gran
        return [f"{name}:{g}" for g in range(b0, b1 + 1)]
    except Exception:
        return []


def _is_ap(x):
    return hasattr(x, "tensor") and hasattr(x, "offset")


class Plan:

    def __init__(self):
        self.ops = []
        self.last_w = {}
        self.readers = {}
        self.eng_pos = {e: [] for e in ENGINES}

    def op(self, engine, fn, reads=(), writes=(), extra_deps=(), tag="", dma=False):
        idx = len(self.ops)
        mock = _MockEng()
        fn(mock)
        rkeys = list(reads)
        wkeys = list(writes)
        pkeys = []
        defer = False
        for name, args, kwargs in mock.calls:
            if name in ("bn_stats", "bn_aggr", "tensor_reduce", "reduce_max",
                        "reduce_sum", "max", "max_index", "tensor_tensor_reduce",
                        "reciprocal"):
                defer = True
                for a in args[1:]:
                    if _is_ap(a):
                        pkeys += _ap_keys(a)
            if "accum_out" in kwargs and kwargs["accum_out"] is not None:
                defer = True
            aps = [a for a in args if _is_ap(a)]
            if aps:
                wkeys += _ap_keys(aps[0])
                for a in aps[1:]:
                    rkeys += _ap_keys(a)
            if name == "tensor_scalar":
                for a in args[2:4]:
                    if _is_ap(a):
                        pkeys += _ap_keys(a)
            elif name == "scalar_tensor_tensor" and len(args) > 2 and _is_ap(args[2]):
                pkeys += _ap_keys(args[2])
            for kw, v in kwargs.items():
                if _is_ap(v):
                    if kw == "accum_out":
                        wkeys += _ap_keys(v)
                    else:
                        rkeys += _ap_keys(v)
                        if kw in ("bias", "scale", "alpha"):
                            pkeys += _ap_keys(v)
        deps = set(extra_deps)
        hard = set()
        for k in rkeys:
            if k in self.last_w:
                d = self.last_w[k]
                deps.add(d)
                if self.ops[d]["defer"]:
                    hard.add(d)
        for k in pkeys:
            if k in self.last_w:
                hard.add(self.last_w[k])
        for k in wkeys:
            if k in self.last_w:
                d = self.last_w[k]
                deps.add(d)
                if self.ops[d]["defer"]:
                    hard.add(d)
            for r in self.readers.get(k, ()):
                deps.add(r)
        deps.discard(idx)
        hard.discard(idx)
        self.ops.append(dict(engine=engine, fn=fn, deps=deps, hard=hard, idx=idx,
                             inc=0, tag=tag, dma=dma, defer=defer))
        self.eng_pos[engine].append(idx)
        for k in wkeys:
            self.last_w[k] = idx
            self.readers[k] = []
        for k in rkeys:
            self.readers.setdefault(k, []).append(idx)
        return idx

    def dma(self, engine, fn, reads=(), writes=(), extra_deps=(), tag=""):
        return self.op(engine, fn, reads, writes, extra_deps, tag, dma=True)

    def materialize(self, nc, ctx):
        ops = self.ops
        need_inc = [False] * len(ops)
        for o in ops:
            for d in o["deps"]:
                if (ops[d]["engine"] != o["engine"] or ops[d]["dma"]
                        or o["engine"] != "tensor"):
                    need_inc[d] = True
            for d in o["hard"]:
                need_inc[d] = True
        sems = {e: ctx.enter_context(nc.semaphore(f"sem_{e}")) for e in ENGINES}
        cum = {e: 0 for e in ENGINES}
        val_after = [0] * len(ops)
        for o in ops:
            e = o["engine"]
            if need_inc[o["idx"]]:
                amt = 16 if o["dma"] else 1
                cum[e] += amt
                o["inc"] = amt
            val_after[o["idx"]] = cum[e]
        hwm = {e: {e2: 0 for e2 in ENGINES} for e in ENGINES}

        def emit_engine(eng_name, e):
            my = hwm[eng_name]
            for idx in self.eng_pos[eng_name]:
                o = ops[idx]
                waits = {}
                for d in o["deps"]:
                    pe = ops[d]["engine"]
                    if (pe == eng_name and not ops[d]["dma"] and d not in o["hard"]
                            and eng_name == "tensor"):
                        continue
                    v = val_after[d]
                    if v > waits.get(pe, 0):
                        waits[pe] = v
                for d in o["hard"]:
                    pe = ops[d]["engine"]
                    v = val_after[d]
                    if v > waits.get(pe, 0):
                        waits[pe] = v
                for pe in sorted(waits):
                    if waits[pe] > my[pe]:
                        e.wait_ge(sems[pe], waits[pe])
                        my[pe] = waits[pe]
                inst = o["fn"](e)
                if o["inc"]:
                    inst.then_inc(sems[eng_name], o["inc"])

        blk = ctx.enter_context(nc.Block())
        for name in ENGINES:
            getattr(blk, name)(lambda e, _n=name: emit_engine(_n, e))
        return sems


def build_program(upto=99, debug=False):
    nc = bass.Bass("TRN2", target_bir_lowering=False, detect_race_conditions=False)
    P = Plan()
    ctx = ExitStack()

    di = {}
    def din(name, shape, dt=F32):
        di[name] = nc.dram_tensor(name, list(shape), dt, kind="ExternalInput").ap()
        return di[name]

    x_in = din("x", (T, D))
    w_in = din("w_in", (3 * D, D)); b_in = din("b_in", (3 * D,))
    w_out = din("w_out", (D, D)); b_out = din("b_out", (D,))
    f_w1 = din("f_w1", (FF, D)); f_b1 = din("f_b1", (FF,))
    f_w2 = din("f_w2", (D, FF)); f_b2 = din("f_b2", (D,))
    a_w = {s: din(f"a_w{s}", (D, D)) for s in "qkvo"}
    a_b = {s: din(f"a_b{s}", (D,)) for s in "qkvo"}
    rel_e = din("rel_e", (32, 16))
    gru_w = din("gru_w", (64, 8)); gru_b = din("gru_b", (8,))
    gru_c = din("gru_c", (16,))
    gru_c2 = din("gru_c2", (2, 8))
    r_w = din("r_w", (D, NE))
    m_gu = din("m_gu", (NE, D, 2 * NI)); m_gub = din("m_gub", (NE, 2 * NI))
    m_dn = din("m_dn", (NE, NI, D)); m_dnb = din("m_dnb", (NE, D))
    c_oh = din("c_oh", (32, 4095))
    c_mi = din("c_mi", (128, 384), BF16); c_mf = din("c_mf", (128, 256), BF16); c_ml = din("c_ml", (128, 256), BF16)
    c_id32 = din("c_id32", (128, 128))
    c_idbf = din("c_idbf", (128, 128), BF16)
    c_es2 = din("c_es2", (128, 2), BF16)
    c_p4 = din("c_p4", (8, 2))

    y_out = nc.dram_tensor("y", [T, D], F32, kind="ExternalOutput").ap()
    pb_dram = nc.dram_tensor("pb_scratch", [NH, 4608], F32, kind="Internal").ap()

    def sbt(name, shape, dt=F32):
        return ctx.enter_context(nc.sbuf_tensor(name, list(shape), dt))

    XRES = sbt("XRES", (128, NT * D))
    XLNF = [sbt(f"XLNF{c}", (128, T), BF16) for c in range(8)]
    CTXF = [sbt(f"CTXF{c}", (128, T), BF16) for c in range(8)]
    ABIG = sbt("ABIG", (128, 4096))
    PTT = sbt("PTT", (128, 4096), BF16)
    QKV = sbt("QKV", (128, 3 * T), BF16)
    VT = sbt("VT", (128, T), BF16)
    WSL = sbt("WSL", (128, 3072), BF16)
    SCR = sbt("SCR", (128, 2048))
    PBF = sbt("PBF", (128, 2048), BF16)
    ROWS = sbt("ROWS", (1, 6144), BF16)
    ID32 = sbt("ID32", (128, 128))
    IDBF = sbt("IDBF", (128, 128), BF16)
    ES2 = sbt("ES2", (128, 2), BF16)
    STAT = sbt("STAT", (128, 64))
    SMALL = sbt("SMALL", (128, 128))
    BT = sbt("BT", (128, 256))
    GT = sbt("GT", (128, 256))
    MK = sbt("MK", (128, 896), BF16)
    CTM = sbt("CTM", (128, 256), BF16)
    ONER = sbt("ONER", (1, 512), BF16)
    U1R = sbt("U1R", (1, 256), BF16)
    P75 = sbt("P75", (128, 1)); PDET = sbt("PDET", (128, 1)); PM1 = sbt("PM1", (128, 1))
    MXC = sbt("MXC", (128, 1))
    FROW = sbt("FROW", (1, 128))
    GAB = sbt("GAB", (128, 4), BF16)
    CEPS = sbt("CEPS", (128, 4))

    S_ps = ctx.enter_context(nc.psum_tensor("S_ps", [128, 2048], F32))
    TR_ps = ctx.enter_context(nc.psum_tensor("TR_ps", [128, 512], F32))
    AV_ps = ctx.enter_context(nc.psum_tensor("AV_ps", [128, 512], F32))
    AX_ps = ctx.enter_context(nc.psum_tensor("AX_ps", [128, 512], F32))
    AY_ps = ctx.enter_context(nc.psum_tensor("AY_ps", [128, 512], F32))
    TRbf = TR_ps[:].bitcast(BF16)
    AXbf = AX_ps[:].bitcast(BF16)
    AVbf = AV_ps[:].bitcast(BF16)

    SKEY = ["S0", "S1", "S2", "S3"]

    P.dma("sync", lambda e: e.dma_start(ID32[:], c_id32), writes=["ID32"])
    P.dma("sync", lambda e: e.dma_start(IDBF[:], c_idbf), writes=["IDBF"])
    P.dma("sync", lambda e: e.dma_start(ES2[:], c_es2), writes=["ES2"])
    P.dma("sync", lambda e: e.dma_start(MK[:, 0:384], c_mi), writes=["MK"])
    P.dma("sync", lambda e: e.dma_start(MK[:, 384:640], c_mf), writes=["MK"])
    P.dma("sync", lambda e: e.dma_start(MK[:, 640:896], c_ml), writes=["MK"])
    for t in range(NT):
        P.dma("sync", lambda e, t=t: e.dma_start(
            XRES[:, t * D:(t + 1) * D], x_in[t * 128:(t + 1) * 128, :]),
            writes=[f"xr{t}"])
    P.op("vector", lambda e: e.memset(P75[:], 0.75), writes=["P75"])
    P.op("vector", lambda e: e.memset(CEPS[:, 0:1], 1e-5), writes=["ceps"])
    P.op("vector", lambda e: e.memset(CEPS[:, 1:2], 1e-30), writes=["ceps"])
    P.op("vector", lambda e: e.memset(ONER[:], 1.0), writes=["ONER"])
    P.op("vector", lambda e: e.tensor_scalar(PDET[:], P75[:], M64, M64, ALU.add, ALU.subtract),
         reads=["P75"], writes=["PDET"])
    P.op("vector", lambda e: e.tensor_scalar(PM1[:], PDET[:], -1.0, 1.0, ALU.mult, ALU.add),
         reads=["PDET"], writes=["PM1"])
    P.op("vector", lambda e: e.tensor_scalar(MXC[:], PDET[:], M64 - M32, M32, ALU.mult, ALU.add),
         reads=["PDET"], writes=["MXC"])

    def emit_ln(blk, fm=True):
        for t in range(NT):
            xa = XRES[:, t * D:(t + 1) * D]
            mcol = STAT[:, t:t + 1]
            rcol = STAT[:, 16 + t:17 + t]
            bcol = STAT[:, 32 + t:33 + t]
            P.op("vector", lambda e, xa=xa: e.bn_stats(SMALL[:, 0:6], xa[:, 0:512]),
                 reads=[f"xr{t}"], writes=["bnA"])
            P.op("vector", lambda e, xa=xa: e.bn_stats(SMALL[:, 6:12], xa[:, 512:1024]),
                 reads=[f"xr{t}"], writes=["bnB"])
            P.op("vector", lambda e: e.bn_aggr(SMALL[:, 16:18], SMALL[:, 0:12]),
                 reads=["bnA", "bnB"], writes=["mv"])
            P.op("vector", lambda e, mcol=mcol: e.tensor_copy(mcol, SMALL[:, 16:17]),
                 reads=["mv"], writes=[f"stm{t}"])
            P.op("scalar", lambda e: e.activation(SMALL[:, 18:19], SMALL[:, 17:18],
                                                  AF.Sqrt, bias=CEPS[:, 0:1]),
                 reads=["mv", "ceps"], writes=["sd"])
            P.op("vector", lambda e, rcol=rcol: e.reciprocal(rcol, SMALL[:, 18:19]),
                 reads=["sd"], writes=[f"str{t}"])
            P.op("vector", lambda e, mcol=mcol, rcol=rcol, bcol=bcol: e.tensor_scalar(
                bcol, mcol, rcol, -1.0, ALU.mult, ALU.mult),
                 reads=[f"stm{t}", f"str{t}"], writes=[f"stb{t}"])
            if fm:
                P.op("scalar", lambda e, xa=xa, rcol=rcol, bcol=bcol: e.activation(
                    PBF[:, 0:1024], xa, AF.Identity, bias=bcol, scale=rcol),
                     reads=[f"xr{t}", f"str{t}", f"stb{t}"], writes=["ps0", "ps1"])
                for c in range(8):
                    P.op("tensor", lambda e, c=c: e.transpose(
                        TRbf[:, c * 128:(c + 1) * 128], PBF[:, c * 128:(c + 1) * 128], IDBF[:]),
                         reads=["ps0", "ps1", "IDBF"], writes=["bkTR"])
                for c in range(8):
                    P.op("vector", lambda e, c=c, t=t: e.tensor_copy(
                        XLNF[c][:, t * 128:(t + 1) * 128], TRbf[:, c * 128:(c + 1) * 128]),
                         reads=["bkTR"], writes=[f"xlnf{c}"])

    def tr_f32_to(dst_ap_fn, src_key, dst_key):
        for c in range(8):
            P.op("tensor", lambda e, c=c: e.transpose(
                TR_ps[:, (c % 4) * 128:(c % 4 + 1) * 128],
                ABIG[:, c * 128:(c + 1) * 128], ID32[:]),
                 reads=[src_key, "ID32"], writes=["bkTR"])
            P.op("scalar", lambda e, c=c: e.copy(
                dst_ap_fn(c), TR_ps[:, (c % 4) * 128:(c % 4 + 1) * 128]),
                 reads=["bkTR"], writes=[dst_key])

    def emit_l0_attn():
        P.dma("gpsimd", lambda e: e.dma_start(ROWS[0:1, 0:3072], b_in.unsqueeze(0)),
              writes=["rows_bin"])
        P.dma("gpsimd", lambda e: e.dma_start(ROWS[0:1, 3072:4096], b_out.unsqueeze(0)),
              writes=["rows_bout"])

        def qt_mm_phase(hp, qt):
            hA = 2 * hp
            lo = max(0, qt * 128 - 128)
            hi = min(T, qt * 128 + 256)
            w = hi - lo
            for h01 in range(2):
                sbank = h01 * 2 + qt % 2
                scol = sbank * 512
                P.op("tensor", lambda e, h01=h01, qt=qt, lo=lo, w=w, scol=scol: e.matmul(
                    S_ps[:, scol:scol + w],
                    QKV[h01 * 64:(h01 + 1) * 64, qt * 128:(qt + 1) * 128],
                    QKV[h01 * 64:(h01 + 1) * 64, T + lo: T + lo + w],
                    start=True, stop=True),
                     reads=["qkvA", "qkvB"], writes=[SKEY[sbank]], tag=f"s{hp}_{qt}_{h01}")

        def qt_tail_phase(hp, qt):
            hA = 2 * hp
            lo = max(0, qt * 128 - 128)
            hi = min(T, qt * 128 + 256)
            w = hi - lo
            nb = w // 128
            mofs = 384 if qt == 0 else (640 if qt == NT - 1 else 0)
            TB = TRbf if qt % 2 == 0 else AXbf
            tbk = "bkTR" if qt % 2 == 0 else "bkAX"
            for h01 in range(2):
                h = hA + h01
                sbank = h01 * 2 + qt % 2
                scol = sbank * 512
                pslot = sbank * 512
                P.op("vector", lambda e, scol=scol, w=w, mofs=mofs: e.tensor_tensor(
                    S_ps[:, scol:scol + w], S_ps[:, scol:scol + w], MK[:, mofs:mofs + w],
                    ALU.add),
                     reads=[SKEY[sbank], "MK"], writes=[SKEY[sbank]])
                zc = 24 + sbank
                P.op("scalar", lambda e, scol=scol, w=w, pslot=pslot, qt=qt, h=h, zc=zc:
                     e.activation(PBF[:, pslot:pslot + w], S_ps[:, scol:scol + w], AF.Exp,
                                  bias=BT[:, qt * 16 + h: qt * 16 + h + 1],
                                  accum_out=SMALL[:, zc:zc + 1]),
                     reads=[SKEY[sbank], f"bt{qt}"], writes=[f"ps{sbank}", f"z{sbank}"])
                P.op("vector", lambda e, zc=zc: e.reciprocal(
                    SMALL[:, zc + 4:zc + 5], SMALL[:, zc:zc + 1]),
                     reads=[f"z{sbank}"], writes=[f"zr{sbank}"])
                for bkt in range(nb):
                    P.op("tensor", lambda e, pslot=pslot, bkt=bkt, h01=h01, TB=TB: e.transpose(
                        TB[:, (h01 * 3 + bkt) * 128:(h01 * 3 + bkt + 1) * 128],
                        PBF[:, pslot + bkt * 128: pslot + (bkt + 1) * 128], IDBF[:]),
                         reads=[f"ps{sbank}", "IDBF"], writes=[tbk])
                P.op("vector", lambda e, h01=h01, TB=TB, nb=nb: e.tensor_copy(
                    PTT[:, h01 * 2048:h01 * 2048 + nb * 128],
                    TB[:, h01 * 3 * 128:(h01 * 3 + nb) * 128]),
                     reads=[tbk], writes=[f"ptb{h01}_0", f"ptb{h01}_1", f"ptb{h01}_2"])
                avk = "AV" if h01 == 0 else "AY"
                avp = AV_ps if h01 == 0 else AY_ps
                for bkt in range(nb):
                    ktg = lo // 128 + bkt
                    P.op("tensor", lambda e, bkt=bkt, h01=h01, ktg=ktg, avp=avp, nb=nb: e.matmul(
                        avp[:, 0:64], PTT[:, h01 * 2048 + bkt * 128:h01 * 2048 + (bkt + 1) * 128],
                        VT[:, ktg * 128 + h01 * 64: ktg * 128 + h01 * 64 + 64],
                        start=(bkt == 0), stop=(bkt == nb - 1)),
                         reads=[f"ptb{h01}_{bkt}", "vt"], writes=[avk])
                cslot = sbank * 64
                P.op("vector", lambda e, avp=avp, zc=zc, cslot=cslot: e.tensor_scalar(
                    CTM[:, cslot:cslot + 64], avp[:, 0:64],
                    SMALL[:, zc + 4:zc + 5], None, ALU.mult),
                     reads=[avk, f"zr{sbank}"], writes=[f"ctm{sbank}"])
                tcol = 768 if h01 == 0 else 896
                P.op("tensor", lambda e, cslot=cslot, TB=TB, tcol=tcol: e.transpose(
                    TB[0:64, tcol:tcol + 128], CTM[:, cslot:cslot + 64], IDBF[:]),
                     reads=[f"ctm{sbank}", "IDBF"], writes=[tbk])
                P.op("vector", lambda e, h01=h01, qt=qt, hp=hp, TB=TB, tcol=tcol: e.tensor_copy(
                    CTXF[hp][h01 * 64:(h01 + 1) * 64, qt * 128:(qt + 1) * 128],
                    TB[0:64, tcol:tcol + 128]),
                     reads=[tbk], writes=[f"ctxf{hp}"])

        for hp in range(8):
            hA = 2 * hp
            for si, s in enumerate("qkv"):
                P.dma("sync", lambda e, si=si, hp=hp: e.dma_start(
                    ABIG[:, 0:1024], w_in[si * D + hp * 128: si * D + hp * 128 + 128, :]),
                      writes=["abig"])
                tr_f32_to(lambda c, si=si: WSL[:, si * 1024 + c * 128: si * 1024 + (c + 1) * 128],
                          "abig", f"wsl{s}")
            for si, s in enumerate("qkv"):
                for tc4 in range(4):
                    for c in range(8):
                        P.op("tensor", lambda e, si=si, c=c, tc4=tc4: e.matmul(
                            AX_ps[:, 0:512],
                            WSL[:, si * 1024 + c * 128: si * 1024 + (c + 1) * 128],
                            XLNF[c][:, tc4 * 512:(tc4 + 1) * 512],
                            start=(c == 0), stop=False),
                             reads=[f"wsl{s}", f"xlnf{c}"], writes=["bkAX"])
                    P.op("tensor", lambda e, si=si, hp=hp: e.matmul(
                        AX_ps[:, 0:512],
                        ROWS[0:1, si * 1024 + hp * 128: si * 1024 + (hp + 1) * 128],
                        ONER[0:1, 0:512], start=False, stop=True),
                         reads=["rows_bin", "ONER"], writes=["bkAX"])
                    P.op("scalar", lambda e, si=si, tc4=tc4, s=s: e.activation(
                        QKV[:, si * T + tc4 * 512: si * T + (tc4 + 1) * 512],
                        AX_ps[:, 0:512], AF.Identity, scale=(0.125 if si == 0 else 1.0)),
                         reads=["bkAX"], writes=[["qkvA", "qkvB", "qkvC"][si]])
            for kt in range(16):
                P.op("tensor", lambda e, kt=kt: e.transpose(
                    TRbf[:, (kt % 8) * 128:(kt % 8 + 1) * 128],
                    QKV[:, 2 * T + kt * 128: 2 * T + (kt + 1) * 128], IDBF[:]),
                     reads=["qkvC", "IDBF"], writes=["bkTR"])
                P.op("vector", lambda e, kt=kt: e.tensor_copy(
                    VT[:, kt * 128:(kt + 1) * 128], TRbf[:, (kt % 8) * 128:(kt % 8 + 1) * 128]),
                     reads=["bkTR"], writes=["vt"])
            P.op("scalar", lambda e: e.activation(PBF[:, 0:2048], QKV[:, 0:T], AF.Square),
                 reads=["qkvA"], writes=["ps0", "ps1", "ps2", "ps3"])
            for tc4 in range(4):
                P.op("tensor", lambda e, tc4=tc4: e.matmul(
                    S_ps[0:2, tc4 * 512:(tc4 + 1) * 512], ES2[:],
                    PBF[:, tc4 * 512:(tc4 + 1) * 512], start=True, stop=True),
                     reads=["ps0", "ps1", "ps2", "ps3", "ES2"], writes=[SKEY[tc4]])
            P.op("scalar", lambda e: e.activation(SCR[0:2, 0:2048], S_ps[0:2, 0:2048],
                                                  AF.Sqrt, bias=CEPS[0:2, 1:2]),
                 reads=SKEY + ["ceps"], writes=["scrA", "scrB"])
            P.op("scalar", lambda e: e.activation(PBF[:, 0:2048], QKV[:, T:2 * T], AF.Square),
                 reads=["qkvB"], writes=["ps0", "ps1", "ps2", "ps3"])
            for tc4 in range(4):
                P.op("tensor", lambda e, tc4=tc4: e.matmul(
                    S_ps[0:2, tc4 * 512:(tc4 + 1) * 512], ES2[:],
                    PBF[:, tc4 * 512:(tc4 + 1) * 512], start=True, stop=True),
                     reads=["ps0", "ps1", "ps2", "ps3", "ES2"], writes=[SKEY[tc4]])
            P.op("vector", lambda e: e.tensor_reduce(SMALL[0:2, 20:21], S_ps[0:2, 0:2048],
                                                     AX, ALU.max),
                 reads=SKEY, writes=["kmax2"])
            P.op("scalar", lambda e: e.activation(SMALL[0:2, 21:22], SMALL[0:2, 20:21],
                                                  AF.Sqrt, bias=CEPS[0:2, 1:2]),
                 reads=["kmax2", "ceps"], writes=["kmax"])
            P.op("vector", lambda e: e.tensor_scalar(
                SCR[0:2, 0:2048], SCR[0:2, 0:2048], SMALL[0:2, 21:22], -1.0,
                ALU.mult, ALU.mult),
                 reads=["scrA", "scrB", "kmax"], writes=["negB", "scrA", "scrB"])
            for qt in range(NT):
                P.op("tensor", lambda e, qt=qt: e.transpose(
                    TR_ps[:, 0:2], SCR[0:2, qt * 128:(qt + 1) * 128], ID32[0:2, 0:2]),
                     reads=["negB", "scrA", "scrB", "ID32"], writes=["bkTR"])
                P.op("vector", lambda e, qt=qt, hA=hA: e.tensor_copy(
                    BT[:, qt * 16 + hA: qt * 16 + hA + 2], TR_ps[:, 0:2]),
                     reads=["bkTR"], writes=[f"bt{qt}"])
            for qt in range(NT + 1):
                if qt < NT:
                    qt_mm_phase(hp, qt)
                if qt > 0:
                    qt_tail_phase(hp, qt - 1)

        def wot_ap(cc):
            if cc < 6:
                return QKV[:, cc * 1024:(cc + 1) * 1024]
            return WSL[:, (cc - 6) * 1024:(cc - 5) * 1024]

        def wot_alias(cc):
            return ["qkvA", "qkvA", "qkvB", "qkvB", "qkvC", "qkvC", "wslq", "wslk"][cc]
        for c in range(8):
            P.dma("sync", lambda e, c=c: e.dma_start(
                ABIG[:, 0:1024], w_out[c * 128:(c + 1) * 128, :]), writes=["abig"])
            for cc in range(8):
                P.op("tensor", lambda e, cc=cc: e.transpose(
                    TR_ps[:, (cc % 4) * 128:(cc % 4 + 1) * 128],
                    ABIG[:, cc * 128:(cc + 1) * 128], ID32[:]),
                     reads=["abig", "ID32"], writes=["bkTR"])
                P.op("scalar", lambda e, c=c, cc=cc: e.copy(
                    wot_ap(cc)[:, c * 128:(c + 1) * 128],
                    TR_ps[:, (cc % 4) * 128:(cc % 4 + 1) * 128]),
                     reads=["bkTR"], writes=["wotx", wot_alias(cc)])
        for t in range(NT):
            pk = ["S0", "S1"]
            for c in range(8):
                for half in range(2):
                    P.op("tensor", lambda e, c=c, t=t, half=half: e.matmul(
                        S_ps[:, half * 512:(half + 1) * 512],
                        CTXF[c][:, t * 128:(t + 1) * 128],
                        wot_ap(c)[:, half * 512:(half + 1) * 512],
                        start=(c == 0), stop=False),
                         reads=[f"ctxf{c}", "wotx"], writes=pk)
            for half in range(2):
                P.op("tensor", lambda e, half=half: e.matmul(
                    S_ps[:, half * 512:(half + 1) * 512], ONER[0:1, 0:128],
                    ROWS[0:1, 3072 + half * 512: 3072 + (half + 1) * 512],
                    start=False, stop=True),
                     reads=["ONER", "rows_bout"], writes=pk)
            P.op("vector", lambda e, t=t: e.tensor_scalar(
                SMALL[:, 62:63], STAT[:, 16 + t:17 + t], 1.0, None, ALU.add),
                 reads=[f"str{t}"], writes=["pr1"])
            P.op("scalar", lambda e, t=t: e.activation(
                SCR[:, 0:1024], XRES[:, t * D:(t + 1) * D], AF.Identity,
                bias=STAT[:, 32 + t:33 + t], scale=SMALL[:, 62:63]),
                 reads=[f"xr{t}", f"stb{t}", "pr1"], writes=["scrA"])
            P.op("vector", lambda e, t=t: e.tensor_tensor(
                XRES[:, t * D:(t + 1) * D], SCR[:, 0:1024], S_ps[:, 0:1024], ALU.add),
                 reads=["scrA"] + pk, writes=[f"xr{t}"])

    def emit_l0_ffn():
        P.dma("gpsimd", lambda e: e.dma_start(ROWS[0:1, 0:4096], f_b1.unsqueeze(0)),
              writes=["rows_bin", "rows_bout"])
        P.dma("gpsimd", lambda e: e.dma_start(ROWS[0:1, 4096:5120], f_b2.unsqueeze(0)),
              writes=["rows_bout"])

        P.op("vector", lambda e: e.memset(SMALL[:, 120:122], 0.0), writes=["fac"])
        P.op("vector", lambda e: e.memset(CEPS[:, 2:3], 1.0), writes=["ceps1"])
        for wi, (wap, nch) in enumerate(((f_w1, 8), (f_w2, 8))):
            for ch in range(nch):
                if wi == 0:
                    src_ap = wap[ch * 512:(ch + 1) * 512, :].rearrange(
                        "(a p) d -> p a d", p=128)
                    dst_ap = ABIG[:].rearrange("p (a d) -> p a d", a=4)
                else:
                    src_ap = wap[ch * 128:(ch + 1) * 128, :]
                    dst_ap = ABIG[:]
                P.dma("sync", lambda e, src_ap=src_ap, dst_ap=dst_ap: e.dma_start(
                    dst_ap, src_ap), writes=["abig"])
                P.op("vector", lambda e: e.tensor_reduce(
                    SMALL[:, 122:123], ABIG[:], AX, ALU.add, apply_absolute_value=True),
                     reads=["abig"], writes=["fred"])
                P.op("vector", lambda e, wi=wi: e.tensor_tensor(
                    SMALL[:, 120 + wi:121 + wi], SMALL[:, 120 + wi:121 + wi],
                    SMALL[:, 122:123], ALU.add),
                     reads=["fred", "fac"], writes=["fac"])
        P.op("tensor", lambda e: e.matmul(TR_ps[0:1, 0:2], CEPS[:, 2:3], SMALL[:, 120:122],
                                          start=True, stop=True),
             reads=["fac", "ceps1"], writes=["bkTR"])
        P.op("vector", lambda e: e.tensor_scalar(
            SCR[0:1, 0:2], TR_ps[0:1, 0:2], 1.0 / (FF * D), 1e-5, ALU.mult, ALU.max),
             reads=["bkTR"], writes=["fc12", "scrA"])
        P.op("vector", lambda e: e.reciprocal(SCR[0:1, 2:4], SCR[0:1, 0:2]),
             reads=["fc12"], writes=["fsw12", "scrA"])
        P.op("vector", lambda e: e.memset(FROW[:], 1.0), writes=["fonesr"])
        P.op("tensor", lambda e: e.matmul(TR_ps[:, 4:8], FROW[:], SCR[0:1, 0:4],
                                          start=True, stop=True),
             reads=["fc12", "fsw12", "fonesr", "scrA"], writes=["bkTR"])
        P.op("vector", lambda e: e.tensor_copy(SMALL[:, 123:127], TR_ps[:, 4:8]),
             reads=["bkTR"], writes=["fswb"])
        C1C, C2C = SMALL[:, 123:124], SMALL[:, 124:125]
        SW1C, SW2C = SMALL[:, 125:126], SMALL[:, 126:127]

        def tern_quarter(load_dma, swc, dst_fn, dst_key_fn):
            load_dma()
            P.op("vector", lambda e, swc=swc: e.tensor_scalar(
                SCR[:, 0:1024], ABIG[:, 0:1024], swc, 0.5, ALU.mult, ALU.is_ge),
                 reads=["abig", "fswb"], writes=["scrA"])
            P.op("vector", lambda e, swc=swc: e.tensor_scalar(
                SCR[:, 1024:2048], ABIG[:, 0:1024], swc, -0.5, ALU.mult, ALU.is_le),
                 reads=["abig", "fswb"], writes=["scrB"])
            P.op("vector", lambda e: e.tensor_tensor(
                PBF[:, 0:1024], SCR[:, 0:1024], SCR[:, 1024:2048], ALU.subtract),
                 reads=["scrA", "scrB"], writes=["ps0", "ps1"])
            for c in range(8):
                P.op("tensor", lambda e, c=c: e.transpose(
                    TRbf[:, c * 128:(c + 1) * 128], PBF[:, c * 128:(c + 1) * 128], IDBF[:]),
                     reads=["ps0", "ps1", "IDBF"], writes=["bkTR"])
            for c in range(8):
                P.op("scalar", lambda e, c=c: e.copy(
                    dst_fn(c), TRbf[:, c * 128:(c + 1) * 128]),
                     reads=["bkTR"], writes=[dst_key_fn(c)])

        for ch in range(32):
            tern_quarter(
                lambda ch=ch: P.dma("sync", lambda e: e.dma_start(
                    ABIG[:, 0:1024], f_w1[ch * 128:(ch + 1) * 128, :]), writes=["abig"]),
                SW1C,
                lambda c, ch=ch: XLNF[c][:].bitcast(FP8)[:, ch * 128:(ch + 1) * 128],
                lambda c: f"xlnf{c}")
        for ch in range(8):
            for qr in range(4):
                def dst(c, ch=ch, qr=qr):
                    fc = qr * 8 + c
                    return CTXF[fc // 4][:].bitcast(FP8)[
                        :, (fc % 4) * 1024 + ch * 128: (fc % 4) * 1024 + (ch + 1) * 128]
                tern_quarter(
                    lambda ch=ch, qr=qr: P.dma("sync", lambda e: e.dma_start(
                        ABIG[:, 0:1024],
                        f_w2[ch * 128:(ch + 1) * 128, qr * 1024:(qr + 1) * 1024]),
                        writes=["abig"]),
                    SW2C, dst, lambda c, qr=qr: f"ctxf{(qr * 8 + c) // 4}")

        def W1T(c):
            return XLNF[c][:].bitcast(FP8)
        def W2T(fc):
            return CTXF[fc // 4][:].bitcast(FP8)[:, (fc % 4) * 1024:(fc % 4 + 1) * 1024]

        for tc in range(NT if upto >= 2 else 0):
            rcol = STAT[:, 16 + tc:17 + tc]
            mcol = STAT[:, tc:tc + 1]
            xa = XRES[:, tc * D:(tc + 1) * D]
            P.op("vector", lambda e, rcol=rcol: e.tensor_tensor(
                SMALL[:, 40:41], rcol, rcol, ALU.mult), reads=[f"str{tc}"], writes=["f40"])
            P.op("vector", lambda e: e.tensor_scalar(
                SMALL[:, 41:42], SMALL[:, 40:41], -1e-5, 1.0 + 1e-5, ALU.mult, ALU.add),
                 reads=["f40"], writes=["f41"])
            P.op("scalar", lambda e: e.activation(SMALL[:, 42:43], SMALL[:, 41:42], AF.Sqrt),
                 reads=["f41"], writes=["f42"])
            P.op("vector", lambda e: e.reciprocal(SMALL[:, 43:44], SMALL[:, 42:43]),
                 reads=["f42"], writes=["f43"])
            P.op("vector", lambda e, xa=xa: e.tensor_reduce(
                SMALL[:, 44:45], xa, AX, ALU.max), reads=[f"xr{tc}"], writes=["f44"])
            P.op("vector", lambda e, xa=xa: e.tensor_reduce(
                SMALL[:, 45:46], xa, AX, ALU.min), reads=[f"xr{tc}"], writes=["f45"])
            P.op("vector", lambda e, mcol=mcol: e.tensor_tensor(
                SMALL[:, 44:45], SMALL[:, 44:45], mcol, ALU.subtract),
                 reads=["f44", f"stm{tc}"], writes=["f44"])
            P.op("vector", lambda e, mcol=mcol: e.tensor_tensor(
                SMALL[:, 45:46], mcol, SMALL[:, 45:46], ALU.subtract),
                 reads=["f45", f"stm{tc}"], writes=["f45"])
            P.op("vector", lambda e: e.tensor_tensor(
                SMALL[:, 44:45], SMALL[:, 44:45], SMALL[:, 45:46], ALU.max),
                 reads=["f44", "f45"], writes=["f44"])
            P.op("vector", lambda e, rcol=rcol: e.tensor_tensor(
                SMALL[:, 45:46], SMALL[:, 44:45], rcol, ALU.mult),
                 reads=["f44", f"str{tc}"], writes=["f45"])
            P.op("vector", lambda e: e.tensor_scalar(
                SMALL[:, 46:47], SMALL[:, 45:46], SMALL[:, 43:44], 1e-5, ALU.mult, ALU.max),
                 reads=["f45", "f43"], writes=["f46"])
            P.op("vector", lambda e: e.reciprocal(SMALL[:, 47:48], SMALL[:, 46:47]),
                 reads=["f46"], writes=["f47"])
            P.op("vector", lambda e: e.tensor_scalar(
                SMALL[:, 47:48], SMALL[:, 47:48], 127.0, None, ALU.mult),
                 reads=["f47"], writes=["f47"])
            P.op("vector", lambda e, rcol=rcol: e.tensor_tensor(
                SMALL[:, 48:49], SMALL[:, 43:44], rcol, ALU.mult),
                 reads=["f43", f"str{tc}"], writes=["f48"])
            P.op("vector", lambda e: e.tensor_tensor(
                SMALL[:, 48:49], SMALL[:, 48:49], SMALL[:, 47:48], ALU.mult),
                 reads=["f48", "f47"], writes=["f48"])
            P.op("vector", lambda e, mcol=mcol: e.tensor_scalar(
                SMALL[:, 49:50], mcol, SMALL[:, 48:49], -1.0, ALU.mult, ALU.mult),
                 reads=[f"stm{tc}", "f48"], writes=["f49"])
            P.op("vector", lambda e: e.tensor_scalar(
                SMALL[:, 50:51], SMALL[:, 46:47], 1.0 / 127.0, C1C, ALU.mult, ALU.mult),
                 reads=["f46", "fswb"], writes=["f50"])
            P.op("vector", lambda e: e.tensor_scalar(
                SMALL[:, 51:52], SMALL[:, 47:48], SW1C, None, ALU.mult),
                 reads=["f47", "fswb"], writes=["f51"])
            P.op("tensor", lambda e: e.transpose(TR_ps[0:1, 16:144], SMALL[:, 51:52], ID32[:]),
                 reads=["f51", "ID32"], writes=["bkTR"])
            P.op("vector", lambda e: e.tensor_copy(U1R[0:1, 0:128], TR_ps[0:1, 16:144]),
                 reads=["bkTR"], writes=["u1r"])
            P.op("scalar", lambda e, xa=xa: e.activation(
                SCR[:, 0:1024], xa, AF.Identity, bias=SMALL[:, 49:50], scale=SMALL[:, 48:49]),
                 reads=[f"xr{tc}", "f48", "f49"], writes=["scrA"])
            P.op("vector", lambda e: e.tensor_scalar(
                PBF[:, 0:1024], SCR[:, 0:1024], MXC[:, 0:1], MXC[:, 0:1],
                ALU.add, ALU.subtract),
                 reads=["scrA", "MXC"], writes=["ps0", "ps1"])
            for c in range(8):
                P.op("tensor", lambda e, c=c: e.transpose(
                    TRbf[:, c * 128:(c + 1) * 128], PBF[:, c * 128:(c + 1) * 128], IDBF[:]),
                     reads=["ps0", "ps1", "IDBF"], writes=["bkTR"])
            P.op("vector", lambda e: e.tensor_copy(
                QKV[:, 0:1024], TRbf[:, 0:1024]),
                 reads=["bkTR"], writes=["qkvA"])
            P.op("vector", lambda e: e.memset(SMALL[:, 114:115], 0.0), writes=["fmx"])
            for fs in range(8):
                bk = fs % 4
                for c in range(8):
                    P.op("tensor", lambda e, c=c, fs=fs, bk=bk: e.matmul(
                        S_ps[:, bk * 512:bk * 512 + 512],
                        QKV[:, c * 128:(c + 1) * 128],
                        W1T(c)[:, fs * 512:(fs + 1) * 512],
                        start=(c == 0), stop=False),
                         reads=["qkvA", f"xlnf{c}"], writes=[SKEY[bk]])
                P.op("tensor", lambda e, fs=fs, bk=bk: e.matmul(
                    S_ps[:, bk * 512:bk * 512 + 512], U1R[0:1, 0:128],
                    ROWS[0:1, fs * 512:(fs + 1) * 512], start=False, stop=True),
                     reads=["u1r", "rows_bin"], writes=[SKEY[bk]])
                P.op("scalar", lambda e, fs=fs, bk=bk: e.activation(
                    ABIG[:, fs * 512:(fs + 1) * 512], S_ps[:, bk * 512:bk * 512 + 512],
                    AF.Gelu, scale=SMALL[:, 50:51]),
                     reads=[SKEY[bk], "f50"], writes=["abig"])
                P.op("vector", lambda e, fs=fs: e.bn_stats(
                    SMALL[:, 64 + 6 * fs:70 + 6 * fs], ABIG[:, fs * 512:(fs + 1) * 512]),
                     reads=["abig"], writes=[f"fbn{fs}"])
                P.op("vector", lambda e, fs=fs: e.tensor_reduce(
                    SMALL[:, 115:116], ABIG[:, fs * 512:(fs + 1) * 512], AX, ALU.max,
                    apply_absolute_value=True),
                     reads=["abig"], writes=["fred2"])
                P.op("vector", lambda e: e.tensor_tensor(
                    SMALL[:, 114:115], SMALL[:, 114:115], SMALL[:, 115:116], ALU.max),
                     reads=["fred2", "fmx"], writes=["fmx"])
            P.op("vector", lambda e: e.bn_aggr(SMALL[:, 112:114], SMALL[:, 64:112]),
                 reads=[f"fbn{fs}" for fs in range(8)], writes=["fmv2"])
            P.op("vector", lambda e: e.tensor_tensor(
                SMALL[:, 52:53], SMALL[:, 112:113], SMALL[:, 112:113], ALU.mult),
                 reads=["fmv2"], writes=["f52"])
            P.op("vector", lambda e: e.tensor_tensor(
                SMALL[:, 52:53], SMALL[:, 52:53], SMALL[:, 113:114], ALU.add),
                 reads=["f52", "fmv2"], writes=["f52"])
            P.op("scalar", lambda e: e.activation(SMALL[:, 53:54], SMALL[:, 52:53],
                                                  AF.Sqrt, bias=CEPS[:, 0:1]),
                 reads=["f52", "ceps"], writes=["f53"])
            P.op("vector", lambda e: e.reciprocal(SMALL[:, 54:55], SMALL[:, 53:54]),
                 reads=["f53"], writes=["f54"])
            P.op("vector", lambda e: e.tensor_tensor(
                SMALL[:, 55:56], SMALL[:, 54:55], SMALL[:, 114:115], ALU.mult),
                 reads=["f54", "fmx"], writes=["f55"])
            P.op("vector", lambda e: e.tensor_scalar(
                SMALL[:, 56:57], SMALL[:, 55:56], 1.0, 1e-5, ALU.mult, ALU.max),
                 reads=["f55"], writes=["f56"])
            P.op("vector", lambda e: e.reciprocal(SMALL[:, 57:58], SMALL[:, 56:57]),
                 reads=["f56"], writes=["f57"])
            P.op("vector", lambda e: e.tensor_scalar(
                SMALL[:, 57:58], SMALL[:, 57:58], 127.0, None, ALU.mult),
                 reads=["f57"], writes=["f57"])
            P.op("vector", lambda e: e.tensor_tensor(
                SMALL[:, 58:59], SMALL[:, 54:55], SMALL[:, 57:58], ALU.mult),
                 reads=["f54", "f57"], writes=["f58"])
            P.op("vector", lambda e: e.tensor_scalar(
                SMALL[:, 59:60], SMALL[:, 56:57], 1.0 / 127.0, C2C, ALU.mult, ALU.mult),
                 reads=["f56", "fswb"], writes=["f59"])
            P.op("vector", lambda e: e.tensor_scalar(
                SMALL[:, 60:61], SMALL[:, 57:58], SW2C, None, ALU.mult),
                 reads=["f57", "fswb"], writes=["f60"])
            P.op("tensor", lambda e: e.transpose(TR_ps[0:1, 16:144], SMALL[:, 60:61], ID32[:]),
                 reads=["f60", "ID32"], writes=["bkTR"])
            P.op("vector", lambda e: e.tensor_copy(U1R[0:1, 128:256], TR_ps[0:1, 16:144]),
                 reads=["bkTR"], writes=["u2r"])
            for fs in range(8):
                P.op("scalar", lambda e, fs=fs: e.activation(
                    SCR[:, 0:512], ABIG[:, fs * 512:(fs + 1) * 512],
                    AF.Identity, scale=SMALL[:, 58:59]),
                     reads=["abig", "f58"], writes=["scrA"])
                P.op("vector", lambda e: e.tensor_scalar(
                    PBF[:, 0:512], SCR[:, 0:512], MXC[:, 0:1], MXC[:, 0:1],
                    ALU.add, ALU.subtract),
                     reads=["scrA", "MXC"], writes=["ps0"])
                for c in range(4):
                    P.op("tensor", lambda e, c=c: e.transpose(
                        TRbf[:, c * 128:(c + 1) * 128], PBF[:, c * 128:(c + 1) * 128],
                        IDBF[:]),
                         reads=["ps0", "IDBF"], writes=["bkTR"])
                P.op("vector", lambda e, fs=fs: e.tensor_copy(
                    QKV[:, 1024 + fs * 512: 1024 + (fs + 1) * 512], TRbf[:, 0:512]),
                     reads=["bkTR"], writes=["qkvA", "qkvB", "qkvC"])
            pk = ["S0", "S1"]
            for half in range(2):
                for fc in range(32):
                    P.op("tensor", lambda e, fc=fc, half=half: e.matmul(
                        S_ps[:, half * 512:(half + 1) * 512],
                        QKV[:, 1024 + fc * 128: 1024 + (fc + 1) * 128],
                        W2T(fc)[:, half * 512:(half + 1) * 512],
                        start=(fc == 0), stop=False),
                         reads=["qkvA", "qkvB", "qkvC", f"ctxf{fc // 4}"], writes=[SKEY[half]])
                P.op("tensor", lambda e, half=half: e.matmul(
                    S_ps[:, half * 512:(half + 1) * 512], U1R[0:1, 128:256],
                    ROWS[0:1, 4096 + half * 512: 4096 + (half + 1) * 512],
                    start=False, stop=True),
                     reads=["u2r", "rows_bout"], writes=[SKEY[half]])
            P.op("vector", lambda e, xa=xa: e.scalar_tensor_tensor(
                xa, S_ps[:, 0:1024], SMALL[:, 59:60], xa, ALU.mult, ALU.add),
                 reads=pk + ["f59", f"xr{tc}"], writes=[f"xr{tc}"])

    STRIPBF = ABIG[:].bitcast(BF16)

    def emit_l1_attn():
        for si, s in enumerate("qkvo"):
            P.dma("gpsimd", lambda e, si=si, s=s: e.dma_start(
                ROWS[0:1, si * 1024:(si + 1) * 1024], a_b[s].unsqueeze(0)),
                writes=["rows_bin", "rows_bout"])
        P.dma("sync", lambda e: e.dma_start(SCR[0:32, 1024:1040], rel_e),
              writes=["scrB"])
        P.op("vector", lambda e: e.memset(SMALL[0:16, 94:95], 0.0), writes=["pbm"])
        for ch in range(8):
            w = 512 if ch < 7 else 511
            P.dma("sync", lambda e, ch=ch, w=w: e.dma_start(
                SCR[0:32, 0:w], c_oh[:, ch * 512:ch * 512 + w]), writes=["scrA"])
            P.op("tensor", lambda e, w=w: e.matmul(
                AX_ps[0:16, 0:w], SCR[0:32, 1024:1040], SCR[0:32, 0:w],
                start=True, stop=True),
                 reads=["scrA", "scrB"], writes=["bkAX"])
            P.op("vector", lambda e, w=w: e.tensor_copy(
                SCR[0:16, 1040:1040 + w],
                bass.AP(AX_ps[0:1, 0:1].tensor, w - 1, [[512, 16], [-1, w]])),
                 reads=["bkAX"], writes=["pbev"])
            P.dma("gpsimd", lambda e, ch=ch, w=w: e.dma_start(
                pb_dram[:, 4223 - ch * 512 - w: 4223 - ch * 512], SCR[0:16, 1040:1040 + w]),
                reads=["pbev"], writes=["pbd"])
            P.op("vector", lambda e, w=w: e.tensor_reduce(
                SMALL[0:16, 95:96], AX_ps[0:16, 0:w], AX, ALU.max,
                apply_absolute_value=True),
                 reads=["bkAX"], writes=["pbr"])
            P.op("vector", lambda e: e.tensor_tensor(
                SMALL[0:16, 94:95], SMALL[0:16, 94:95], SMALL[0:16, 95:96], ALU.max),
                 reads=["pbr", "pbm"], writes=["pbm"])
        P.op("vector", lambda e: e.memset(SCR[0:16, 1040:1553], 0.0), writes=["pbev"])
        P.dma("gpsimd", lambda e: e.dma_start(pb_dram[:, 0:128], SCR[0:16, 1040:1168]),
              reads=["pbev"], writes=["pbd"])
        P.dma("gpsimd", lambda e: e.dma_start(pb_dram[:, 4223:4608], SCR[0:16, 1040:1425]),
              reads=["pbev"], writes=["pbd"])
        P.op("tensor", lambda e: e.transpose(TR_ps[0:1, 0:16], SMALL[0:16, 94:95],
                                             ID32[0:16, 0:16]),
             reads=["pbm", "ID32"], writes=["bkTR"])
        P.op("vector", lambda e: e.tensor_copy(SMALL[0:1, 96:112], TR_ps[0:1, 0:16]),
             reads=["bkTR"], writes=["pbrow"])
        P.op("vector", lambda e: e.tensor_reduce(SMALL[0:1, 113:114], SMALL[0:1, 96:112],
                                                 AX, ALU.max),
             reads=["pbrow"], writes=["pbrow"])
        for j in range(2):
            P.dma("gpsimd", lambda e, j=j: e.dma_start(
                SMALL[j:j + 1, 112:113],
                bass.AP(SMALL[0:1, 0:1].tensor, 113, [[128, 1], [1, 1]])),
                reads=["pbrow"], writes=["pbm2"])
        P.dma("sync", lambda e: e.dma_start(SMALL[0:2, 104:112], gru_c2), writes=["cc2"])
        P.dma("sync", lambda e: e.dma_start(SCR[0:64, 1600:1608], gru_w), writes=["gruw"])
        P.dma("sync", lambda e: e.dma_start(SCR[0:8, 1608:1610], c_p4), writes=["gruw"])
        P.dma("sync", lambda e: e.dma_start(SCR[0:8, 1610:1611], gru_b.unsqueeze(1)),
              writes=["gruw"])
        P.op("tensor", lambda e: e.transpose(TR_ps[0:8, 64:128], SCR[0:64, 1600:1608],
                                             ID32[0:64, 0:64]),
             reads=["gruw", "ID32"], writes=["bkTR"])
        P.op("vector", lambda e: e.tensor_copy(SCR[0:8, 1612:1676], TR_ps[0:8, 64:128]),
             reads=["bkTR"], writes=["gruw2"])
        P.op("tensor", lambda e: e.matmul(TR_ps[0:64, 100:102], SCR[0:8, 1612:1676],
                                          SCR[0:8, 1608:1610], start=True, stop=True),
             reads=["gruw2", "gruw"], writes=["bkTR"])
        P.op("tensor", lambda e: e.matmul(TR_ps[64:128, 102:104], SCR[0:8, 1612:1676],
                                          SCR[0:8, 1608:1610], start=True, stop=True),
             reads=["gruw2", "gruw"], writes=["bkTR"])
        P.op("vector", lambda e: e.memset(GAB[:], 0.0), writes=["gab"])
        P.op("vector", lambda e: e.tensor_copy(GAB[0:64, 0:2], TR_ps[0:64, 100:102]),
             reads=["bkTR", "gab"], writes=["gab"])
        P.op("vector", lambda e: e.tensor_copy(GAB[64:128, 2:4], TR_ps[64:128, 102:104]),
             reads=["bkTR", "gab"], writes=["gab"])
        P.op("tensor", lambda e: e.matmul(TR_ps[0:1, 104:106], SCR[0:8, 1610:1611],
                                          SCR[0:8, 1608:1610], start=True, stop=True),
             reads=["gruw"], writes=["bkTR"])
        P.op("vector", lambda e: e.tensor_copy(SMALL[0:1, 88:90], TR_ps[0:1, 104:106]),
             reads=["bkTR"], writes=["bg2r"])
        for j in range(2):
            P.dma("gpsimd", lambda e, j=j: e.dma_start(
                SMALL[j:j + 1, 90:91], bass.AP(SMALL[0:1, 0:1].tensor, 88, [[128, 1], [1, 1]])),
                reads=["bg2r"], writes=["bga"])
            P.dma("gpsimd", lambda e, j=j: e.dma_start(
                SMALL[j:j + 1, 91:92], bass.AP(SMALL[0:1, 0:1].tensor, 89, [[128, 1], [1, 1]])),
                reads=["bg2r"], writes=["bgb"])

        for hp in range(8):
            hA = 2 * hp
            for si, s in enumerate("qkv"):
                for c in range(8):
                    P.dma("gpsimd", lambda e, s=s, si=si, hp=hp, c=c: e.dma_start(
                        WSL[:, si * 1024 + c * 128: si * 1024 + (c + 1) * 128],
                        a_w[s][c * 128:(c + 1) * 128, hp * 128:(hp + 1) * 128]),
                        writes=[f"wsl{s}"])
            for h01 in range(2):
                h = hA + h01
                P.dma("gpsimd", lambda e, h=h, h01=h01: e.dma_start(
                    STRIPBF[:, h01 * 4096:(h01 + 1) * 4096],
                    bass.AP(pb_dram.tensor, h * 4608, [[1, 128], [1, 4096]])),
                    reads=["pbd"], writes=[f"strip{h01}", "abig"])
            for si, s in enumerate("qkv"):
                for tc4 in range(4):
                    for c in range(8):
                        P.op("tensor", lambda e, si=si, c=c, tc4=tc4: e.matmul(
                            AX_ps[:, 0:512],
                            WSL[:, si * 1024 + c * 128: si * 1024 + (c + 1) * 128],
                            XLNF[c][:, tc4 * 512:(tc4 + 1) * 512],
                            start=(c == 0), stop=False),
                             reads=[f"wsl{s}", f"xlnf{c}"], writes=["bkAX"])
                    P.op("tensor", lambda e, si=si, hp=hp: e.matmul(
                        AX_ps[:, 0:512],
                        ROWS[0:1, si * 1024 + hp * 128: si * 1024 + (hp + 1) * 128],
                        ONER[0:1, 0:512], start=False, stop=True),
                         reads=["rows_bin", "ONER"], writes=["bkAX"])
                    P.op("scalar", lambda e, si=si, tc4=tc4: e.activation(
                        QKV[:, si * T + tc4 * 512: si * T + (tc4 + 1) * 512],
                        AX_ps[:, 0:512], AF.Identity, scale=(0.125 if si == 0 else 1.0)),
                         reads=["bkAX"], writes=[["qkvA", "qkvB", "qkvC"][si]])
            for kt in range(16):
                P.op("tensor", lambda e, kt=kt: e.transpose(
                    TRbf[:, (kt % 8) * 128:(kt % 8 + 1) * 128],
                    QKV[:, 2 * T + kt * 128: 2 * T + (kt + 1) * 128], IDBF[:]),
                     reads=["qkvC", "IDBF"], writes=["bkTR"])
                P.op("vector", lambda e, kt=kt: e.tensor_copy(
                    VT[:, kt * 128:(kt + 1) * 128], TRbf[:, (kt % 8) * 128:(kt % 8 + 1) * 128]),
                     reads=["bkTR"], writes=["vt"])
            P.op("scalar", lambda e: e.activation(PBF[:, 0:2048], QKV[:, 0:T], AF.Square),
                 reads=["qkvA"], writes=["ps0", "ps1", "ps2", "ps3"])
            for tc4 in range(4):
                P.op("tensor", lambda e, tc4=tc4: e.matmul(
                    S_ps[0:2, tc4 * 512:(tc4 + 1) * 512], ES2[:],
                    PBF[:, tc4 * 512:(tc4 + 1) * 512], start=True, stop=True),
                     reads=["ps0", "ps1", "ps2", "ps3", "ES2"], writes=[SKEY[tc4]])
            P.op("scalar", lambda e: e.activation(SCR[0:2, 0:2048], S_ps[0:2, 0:2048],
                                                  AF.Sqrt, bias=CEPS[0:2, 1:2]),
                 reads=SKEY + ["ceps"], writes=["negB", "scrA", "scrB"])
            P.op("scalar", lambda e: e.activation(PBF[:, 0:2048], QKV[:, T:2 * T], AF.Square),
                 reads=["qkvB"], writes=["ps0", "ps1", "ps2", "ps3"])
            for tc4 in range(4):
                P.op("tensor", lambda e, tc4=tc4: e.matmul(
                    S_ps[0:2, tc4 * 512:(tc4 + 1) * 512], ES2[:],
                    PBF[:, tc4 * 512:(tc4 + 1) * 512], start=True, stop=True),
                     reads=["ps0", "ps1", "ps2", "ps3", "ES2"], writes=[SKEY[tc4]])
            P.op("vector", lambda e: e.tensor_reduce(SMALL[0:2, 20:21], S_ps[0:2, 0:2048],
                                                     AX, ALU.max),
                 reads=SKEY, writes=["kmax2"])
            P.op("scalar", lambda e: e.activation(SMALL[0:2, 21:22], SMALL[0:2, 20:21],
                                                  AF.Sqrt, bias=CEPS[0:2, 1:2]),
                 reads=["kmax2", "ceps"], writes=["kmax"])
            P.op("vector", lambda e: e.tensor_scalar(
                SCR[0:2, 0:2048], SCR[0:2, 0:2048], SMALL[0:2, 21:22], None, ALU.mult),
                 reads=["negB", "scrA", "scrB", "kmax"], writes=["negB", "scrA", "scrB"])
            pkeys = ["ps0", "ps1", "ps2", "ps3"]
            for tc4 in range(4):
                P.op("tensor", lambda e, tc4=tc4, hp=hp: e.matmul(
                    S_ps[0:2, tc4 * 512:(tc4 + 1) * 512], GAB[:, 1:4:2],
                    XLNF[hp][:, tc4 * 512:(tc4 + 1) * 512], start=True, stop=True),
                     reads=["gab", f"xlnf{hp}"], writes=[SKEY[tc4]])
            P.op("scalar", lambda e: e.activation(S_ps[0:2, 0:2048], S_ps[0:2, 0:2048],
                                                  AF.Sigmoid, bias=SMALL[0:2, 91:92]),
                 reads=SKEY + ["bgb"], writes=SKEY)
            P.op("vector", lambda e, hp=hp: e.tensor_scalar(
                PBF[0:2, 0:2048], S_ps[0:2, 0:2048], SMALL[0:2, 104 + hp:105 + hp], 1.0,
                ALU.mult, ALU.subtract),
                 reads=SKEY + ["cc2"], writes=pkeys)
            for tc4 in range(4):
                P.op("tensor", lambda e, tc4=tc4, hp=hp: e.matmul(
                    S_ps[0:2, tc4 * 512:(tc4 + 1) * 512], GAB[:, 0:3:2],
                    XLNF[hp][:, tc4 * 512:(tc4 + 1) * 512], start=True, stop=True),
                     reads=["gab", f"xlnf{hp}"], writes=[SKEY[tc4]])
            P.op("scalar", lambda e: e.activation(S_ps[0:2, 0:2048], S_ps[0:2, 0:2048],
                                                  AF.Sigmoid, bias=SMALL[0:2, 90:91]),
                 reads=SKEY + ["bga"], writes=SKEY)
            P.op("vector", lambda e: e.scalar_tensor_tensor(
                PBF[0:2, 0:2048], S_ps[0:2, 0:2048], 1.0, PBF[0:2, 0:2048],
                ALU.mult, ALU.mult),
                 reads=SKEY + pkeys, writes=pkeys)
            P.op("vector", lambda e: e.tensor_scalar(
                PBF[0:2, 0:2048], PBF[0:2, 0:2048], 2.0, None, ALU.add),
                 reads=pkeys, writes=pkeys)
            for qt in range(NT):
                P.op("tensor", lambda e, qt=qt: e.transpose(
                    TRbf[:, 0:2], PBF[0:2, qt * 128:(qt + 1) * 128], IDBF[0:2, 0:2]),
                     reads=pkeys + ["IDBF"], writes=["bkTR"])
                P.op("vector", lambda e, qt=qt, hA=hA: e.tensor_copy(
                    GT[:, qt * 16 + hA: qt * 16 + hA + 2], TRbf[:, 0:2]),
                     reads=["bkTR"], writes=[f"gt{qt}"])
            P.op("scalar", lambda e: e.activation(
                PBF[0:2, 0:2048], PBF[0:2, 0:2048], AF.Abs, scale=SMALL[0:2, 112:113]),
                 reads=pkeys + ["pbm2"], writes=pkeys)
            P.op("vector", lambda e: e.tensor_tensor(
                SCR[0:2, 0:2048], SCR[0:2, 0:2048], PBF[0:2, 0:2048], ALU.add),
                 reads=["negB", "scrA", "scrB"] + pkeys, writes=["negB", "scrA", "scrB"])
            for qt in range(NT):
                P.op("tensor", lambda e, qt=qt: e.transpose(
                    TR_ps[:, 0:2], SCR[0:2, qt * 128:(qt + 1) * 128], ID32[0:2, 0:2]),
                     reads=["negB", "scrA", "scrB", "ID32"], writes=["bkTR"])
                P.op("vector", lambda e, qt=qt, hA=hA: e.tensor_scalar(
                    BT[:, qt * 16 + hA: qt * 16 + hA + 2], TR_ps[:, 0:2], -1.0, None,
                    ALU.mult),
                     reads=["bkTR"], writes=[f"bt{qt}"])

            def l1_mm(qt, kc4):
                for h01 in range(2):
                    sbank = h01 * 2 + kc4 % 2
                    P.op("tensor", lambda e, h01=h01, qt=qt, kc4=kc4, sbank=sbank:
                         e.matmul(
                             S_ps[:, sbank * 512:(sbank + 1) * 512],
                             QKV[h01 * 64:(h01 + 1) * 64, qt * 128:(qt + 1) * 128],
                             QKV[h01 * 64:(h01 + 1) * 64, T + kc4 * 512: T + (kc4 + 1) * 512],
                             start=True, stop=True),
                         reads=["qkvA", "qkvB"], writes=[SKEY[sbank]])

            def l1_tail(qt, kc4):
                TB = TRbf if qt % 2 == 0 else AVbf
                tbk = "bkTR" if qt % 2 == 0 else "bkAV"
                for h01 in range(2):
                    h = hA + h01
                    sbank = h01 * 2 + kc4 % 2
                    scol = sbank * 512
                    pslot = h01 * 1024 + (kc4 % 2) * 512
                    pkey = f"ps{h01 * 2 + kc4 % 2}"
                    stro = h01 * 4096 + 2175 + 128 * qt - kc4 * 512
                    P.op("vector", lambda e, scol=scol, qt=qt, h=h, stro=stro:
                         e.scalar_tensor_tensor(
                             S_ps[:, scol:scol + 512],
                             bass.AP(STRIPBF.tensor, stro, [[8192, 128], [-1, 512]]),
                             GT[:, qt * 16 + h: qt * 16 + h + 1],
                             S_ps[:, scol:scol + 512], ALU.mult, ALU.add),
                         reads=[SKEY[sbank], f"strip{h01}", "abig", f"gt{qt}"],
                         writes=[SKEY[sbank]])
                    zc = 24 + h01 * 2
                    P.op("scalar", lambda e, scol=scol, pslot=pslot, qt=qt, h=h, zc=zc,
                         kc4=kc4: e.activation(
                             PBF[:, pslot:pslot + 512], S_ps[:, scol:scol + 512], AF.Exp,
                             bias=BT[:, qt * 16 + h: qt * 16 + h + 1],
                             accum_out=SMALL[:, zc + kc4 % 2: zc + kc4 % 2 + 1]),
                         reads=[SKEY[sbank], f"bt{qt}"],
                         writes=[pkey, f"zz{h01}_{kc4}"])
                    if kc4 % 2 == 1:
                        dst = 32 + h01 + (kc4 // 2) * 4
                        P.op("vector", lambda e, zc=zc, dst=dst: e.tensor_tensor(
                            SMALL[:, dst:dst + 1],
                            SMALL[:, zc:zc + 1], SMALL[:, zc + 1:zc + 2], ALU.add),
                             reads=[f"zz{h01}_{kc4 - 1}", f"zz{h01}_{kc4}"],
                             writes=[f"zp{h01}_{kc4 // 2}"])
                    for bb in range(4):
                        P.op("tensor", lambda e, pslot=pslot, bb=bb, h01=h01, TB=TB:
                             e.transpose(
                                 TB[:, (h01 * 4 + bb) * 128:(h01 * 4 + bb + 1) * 128],
                                 PBF[:, pslot + bb * 128: pslot + (bb + 1) * 128], IDBF[:]),
                             reads=[pkey, "IDBF"], writes=[tbk])
                    P.op("vector", lambda e, h01=h01, TB=TB, kc4=kc4: e.tensor_copy(
                        bass.AP(PTT[0:1, 0:1].tensor, (kc4 * 4) * 256 + h01 * 128,
                                [[4096, 128], [256, 4], [1, 128]]),
                        TB[:, h01 * 512:(h01 + 1) * 512]),
                         reads=[tbk],
                         writes=[f"pt{kc4 * 4}_{h01}", f"pt{kc4 * 4 + 1}_{h01}",
                                 f"pt{kc4 * 4 + 2}_{h01}", f"pt{kc4 * 4 + 3}_{h01}"])
                    if kc4 == 3:
                        P.op("vector", lambda e, h01=h01: e.tensor_tensor(
                            SMALL[:, 40 + h01:41 + h01], SMALL[:, 32 + h01:33 + h01],
                            SMALL[:, 36 + h01:37 + h01], ALU.add),
                             reads=[f"zp{h01}_0", f"zp{h01}_1"], writes=[f"zt{h01}"])
                        P.op("vector", lambda e, h01=h01: e.reciprocal(
                            SMALL[:, 42 + h01:43 + h01], SMALL[:, 40 + h01:41 + h01]),
                             reads=[f"zt{h01}"], writes=[f"zr{h01}"])
                        for kcg in range(16):
                            P.op("tensor", lambda e, kcg=kcg, h01=h01: e.matmul(
                                AY_ps[:, 0:64],
                                PTT[:, kcg * 256 + h01 * 128: kcg * 256 + h01 * 128 + 128],
                                VT[:, kcg * 128 + h01 * 64: kcg * 128 + h01 * 64 + 64],
                                start=(kcg == 0), stop=(kcg == 15)),
                                 reads=[f"pt{kcg}_{h01}", "vt"], writes=["AY"])
                        P.op("vector", lambda e, h01=h01, qt=qt: e.tensor_scalar(
                            CTM[:, (h01 * 2 + qt % 2) * 64:(h01 * 2 + qt % 2) * 64 + 64],
                            AY_ps[:, 0:64], SMALL[:, 42 + h01:43 + h01], None, ALU.mult),
                             reads=["AY", f"zr{h01}"], writes=[f"ctm{h01}_{qt % 2}"])
                        tcol = 768 if h01 == 0 else 896
                        P.op("tensor", lambda e, h01=h01, qt=qt, TB=TB, tcol=tcol:
                             e.transpose(
                                 TB[0:64, tcol:tcol + 128],
                                 CTM[:, (h01 * 2 + qt % 2) * 64:(h01 * 2 + qt % 2) * 64 + 64],
                                 IDBF[:]),
                             reads=[f"ctm{h01}_{qt % 2}", "IDBF"], writes=[tbk])
                        P.op("vector", lambda e, h01=h01, qt=qt, hp=hp, TB=TB, tcol=tcol:
                             e.tensor_copy(
                                 CTXF[hp][h01 * 64:(h01 + 1) * 64, qt * 128:(qt + 1) * 128],
                                 TB[0:64, tcol:tcol + 128]),
                             reads=[tbk], writes=[f"ctxf{hp}"])

            stages = [(qt, kc4) for qt in range(NT) for kc4 in range(4)]
            for i in range(len(stages) + 1):
                if i < len(stages):
                    l1_mm(*stages[i])
                if i > 0:
                    l1_tail(*stages[i - 1])

        def wot_ap(cc):
            if cc < 6:
                return QKV[:, cc * 1024:(cc + 1) * 1024]
            return WSL[:, (cc - 6) * 1024:(cc - 5) * 1024]

        def wot_alias(cc):
            return ["qkvA", "qkvA", "qkvB", "qkvB", "qkvC", "qkvC", "wslq", "wslk"][cc]
        for cc in range(8):
            P.dma("gpsimd", lambda e, cc=cc: e.dma_start(
                wot_ap(cc), a_w["o"][cc * 128:(cc + 1) * 128, :]),
                writes=["wotx", wot_alias(cc)])
        for t in range(NT):
            pk = ["S0", "S1"]
            for c in range(8):
                for half in range(2):
                    P.op("tensor", lambda e, c=c, t=t, half=half: e.matmul(
                        S_ps[:, half * 512:(half + 1) * 512],
                        CTXF[c][:, t * 128:(t + 1) * 128],
                        wot_ap(c)[:, half * 512:(half + 1) * 512],
                        start=(c == 0), stop=False),
                         reads=[f"ctxf{c}", "wotx"], writes=pk)
            for half in range(2):
                P.op("tensor", lambda e, half=half: e.matmul(
                    S_ps[:, half * 512:(half + 1) * 512], ONER[0:1, 0:128],
                    ROWS[0:1, 3072 + half * 512: 3072 + (half + 1) * 512],
                    start=False, stop=True),
                     reads=["ONER", "rows_bout"], writes=pk)
            P.op("vector", lambda e, t=t: e.tensor_tensor(
                XRES[:, t * D:(t + 1) * D], XRES[:, t * D:(t + 1) * D],
                S_ps[:, 0:1024], ALU.add),
                 reads=pk + [f"xr{t}"], writes=[f"xr{t}"])

    SIG7 = 1.0 / (1.0 + float(np.exp(-1.702 * 7.0)))

    def emit_l1_moe(nexp=NE, router_only=False, stage=4):
        ABF32 = ABIG
        DNB = None
        if not router_only and stage >= 4:
            P.dma("gpsimd", lambda e: e.dma_start(QKV[0:16, 5120:6144], m_dnb),
                  writes=["dnball"])
        for c in range(8):
            P.dma("gpsimd", lambda e, c=c: e.dma_start(
                WSL[:, c * 16:(c + 1) * 16], r_w[c * 128:(c + 1) * 128, :]),
                writes=["rw"])
        for tc4 in range(4):
            for c in range(8):
                P.op("tensor", lambda e, c=c, tc4=tc4: e.matmul(
                    AX_ps[0:16, 0:512], WSL[:, c * 16:(c + 1) * 16],
                    XLNF[c][:, tc4 * 512:(tc4 + 1) * 512],
                    start=(c == 0), stop=(c == 7)),
                     reads=["rw", f"xlnf{c}"], writes=["bkAX"])
            P.op("vector", lambda e, tc4=tc4: e.tensor_copy(
                ABIG[0:16, tc4 * 512:(tc4 + 1) * 512], AX_ps[0:16, 0:512]),
                 reads=["bkAX"], writes=["logit"])
        for tc in range(NT):
            P.op("tensor", lambda e, tc=tc: e.transpose(
                TR_ps[:, 0:16], ABIG[0:16, tc * 128:(tc + 1) * 128], ID32[0:16, 0:16]),
                 reads=["logit", "ID32"], writes=["bkTR"])
            P.op("vector", lambda e: e.tensor_copy(SCR[:, 0:16], TR_ps[:, 0:16]),
                 reads=["bkTR"], writes=["lt"])
            P.op("vector", lambda e: e.tensor_reduce(SMALL[:, 44:45], SCR[:, 0:16],
                                                     AX, ALU.max),
                 reads=["lt"], writes=["m1"])
            P.op("vector", lambda e: e.tensor_scalar(
                SCR[:, 16:32], SCR[:, 0:16], SMALL[:, 44:45], None, ALU.is_ge),
                 reads=["lt", "m1"], writes=["eq1"])
            P.op("vector", lambda e: e.scalar_tensor_tensor(
                SCR[:, 32:48], SCR[:, 16:32], -1.0e30, SCR[:, 0:16], ALU.mult, ALU.add),
                 reads=["eq1", "lt"], writes=["l2"])
            P.op("vector", lambda e: e.tensor_reduce(SMALL[:, 45:46], SCR[:, 32:48],
                                                     AX, ALU.max),
                 reads=["l2"], writes=["m2"])
            P.op("vector", lambda e: e.tensor_scalar(
                SCR[:, 48:64], SCR[:, 32:48], SMALL[:, 45:46], None, ALU.is_ge),
                 reads=["l2", "m2"], writes=["eq2"])
            P.op("vector", lambda e: e.tensor_tensor(
                SMALL[:, 46:47], SMALL[:, 45:46], SMALL[:, 44:45], ALU.subtract),
                 reads=["m1", "m2"], writes=["md"])
            P.op("scalar", lambda e: e.activation(SMALL[:, 47:48], SMALL[:, 46:47], AF.Exp),
                 reads=["md"], writes=["dd"])
            P.op("vector", lambda e: e.tensor_scalar(
                SMALL[:, 48:49], SMALL[:, 47:48], 1.0, None, ALU.add),
                 reads=["dd"], writes=["dn"])
            P.op("vector", lambda e: e.reciprocal(SMALL[:, 49:50], SMALL[:, 48:49]),
                 reads=["dn"], writes=["w1"])
            P.op("vector", lambda e: e.tensor_scalar(
                SMALL[:, 50:51], SMALL[:, 49:50], -1.0, 1.0, ALU.mult, ALU.add),
                 reads=["w1"], writes=["w2"])
            P.op("vector", lambda e: e.tensor_scalar(
                SCR[:, 48:64], SCR[:, 48:64], SMALL[:, 50:51], None, ALU.mult),
                 reads=["eq2", "w2"], writes=["eq2"])
            P.op("vector", lambda e, tc=tc: e.scalar_tensor_tensor(
                GT[:, tc * 16:(tc + 1) * 16], SCR[:, 16:32], SMALL[:, 49:50],
                SCR[:, 48:64], ALU.mult, ALU.add),
                 reads=["eq1", "w1", "eq2"], writes=[f"sc{tc}"])
            P.op("tensor", lambda e, tc=tc: e.transpose(
                TR_ps[0:16, 0:128], GT[:, tc * 16:(tc + 1) * 16], ID32[:]),
                 reads=[f"sc{tc}", "ID32"], writes=["bkTR"])
            P.op("vector", lambda e, tc=tc: e.tensor_copy(
                VT[0:16, tc * 128:(tc + 1) * 128], TR_ps[0:16, 0:128]),
                 reads=["bkTR"], writes=["sbt"])

        def gu_ap(par, c):
            return CTXF[par * 4 + c // 2][:, (c % 2) * 1024:(c % 2 + 1) * 1024]

        if router_only:
            return
        if stage >= 4:
            P.op("vector", lambda e: e.tensor_scalar(
                GT[:, 0:256], GT[:, 0:256], 1.0 / 1.702, None, ALU.mult),
                 reads=[f"sc{t}" for t in range(NT)],
                 writes=[f"sc{t}" for t in range(NT)])

            def gup(it):
                return S_ps[:, (it % 2) * 1024:(it % 2) * 1024 + 1024]

            def gupk(it):
                return ["S0", "S1"] if it % 2 == 0 else ["S2", "S3"]

            def dn_ap(ei, c4):
                if ei % 2 == 0:
                    return QKV[:, c4 * 1024:(c4 + 1) * 1024]
                return WSL[:, c4 * 1024:(c4 + 1) * 1024] if c4 < 3 else PBF[:, 1024:2048]

            def emit_head(it):
                ei, tc = divmod(it, NT)
                par = ei % 2
                if tc == 0:
                    for c in range(8):
                        P.dma("gpsimd", lambda e, ei=ei, c=c, par=par: e.dma_start(
                            gu_ap(par, c), m_gu[ei, c * 128:(c + 1) * 128, :]),
                            writes=[f"guw{par}_{c}"])
                    for c4 in range(4):
                        P.dma("gpsimd", lambda e, ei=ei, c4=c4: e.dma_start(
                            dn_ap(ei, c4), m_dn[ei, c4 * 128:(c4 + 1) * 128, :]),
                            writes=[f"dnw{par}_{c4}"])
                    P.dma("gpsimd", lambda e, ei=ei, par=par: e.dma_start(
                        ROWS[0:1, par * 1024:par * 1024 + 1024], m_gub[ei:ei + 1, :]),
                        writes=[f"gub{par}"])
                gk = gupk(it)
                gp = gup(it)
                for c in range(8):
                    for half in range(2):
                        P.op("tensor", lambda e, c=c, tc=tc, half=half, par=par,
                             gp=gp: e.matmul(
                            gp[:, half * 512:(half + 1) * 512],
                            XLNF[c][:, tc * 128:(tc + 1) * 128],
                            gu_ap(par, c)[:, half * 512:(half + 1) * 512],
                            start=(c == 0), stop=False),
                             reads=[f"guw{par}_{c}", f"xlnf{c}"], writes=[gk[half]])
                for half in range(2):
                    P.op("tensor", lambda e, half=half, par=par, gp=gp: e.matmul(
                        gp[:, half * 512:(half + 1) * 512], ONER[0:1, 0:128],
                        ROWS[0:1, par * 1024 + half * 512: par * 1024 + (half + 1) * 512],
                        start=False, stop=True),
                         reads=[f"gub{par}", "ONER"], writes=[gk[half]])

            def emit_tail(it):
                ei, tc = divmod(it, NT)
                par = ei % 2
                sl = it % 2
                gk = gupk(it)
                GUp = gup(it)
                P.op("vector", lambda e, GUp=GUp: e.tensor_scalar(
                    SCR[:, 0:512], GUp[:, 0:1024:2], 7.0, None, ALU.min),
                     reads=gk, writes=["gtw"])
                P.op("scalar", lambda e: e.activation(
                    SCR[:, 512:1024], SCR[:, 0:512], AF.Silu, scale=1.702),
                     reads=["gtw"], writes=["sig"])
                P.op("vector", lambda e, GUp=GUp: e.tensor_scalar(
                    SCR[:, 1024:1536], GUp[:, 1:1024:2], -7.0, 7.0, ALU.max, ALU.min),
                     reads=gk, writes=["upw"])
                P.op("vector", lambda e, tc=tc, ei=ei: e.tensor_scalar(
                    SCR[:, 1024:1536], SCR[:, 1024:1536],
                    GT[:, tc * 16 + ei: tc * 16 + ei + 1],
                    GT[:, tc * 16 + ei: tc * 16 + ei + 1], ALU.mult, ALU.add),
                     reads=["upw", f"sc{tc}"], writes=["upw"])
                P.op("vector", lambda e: e.tensor_tensor(
                    PBF[:, 0:512], SCR[:, 1024:1536], SCR[:, 512:1024], ALU.mult),
                     reads=["upw", "sig"], writes=["ps0"])
                for c in range(4):
                    P.op("tensor", lambda e, c=c: e.transpose(
                        TRbf[:, c * 128:(c + 1) * 128], PBF[:, c * 128:(c + 1) * 128],
                        IDBF[:]),
                         reads=["ps0", "IDBF"], writes=["bkTR"])
                P.op("vector", lambda e, sl=sl: e.tensor_copy(
                    QKV[:, 4096 + sl * 512: 4608 + sl * 512], TRbf[:, 0:512]),
                     reads=["bkTR"], writes=[f"actw{sl}"])
                for half in range(2):
                    dst = AX_ps if half == 0 else AV_ps
                    dk = "bkAX" if half == 0 else "bkAV"
                    for c4 in range(4):
                        P.op("tensor", lambda e, c4=c4, half=half, sl=sl, ei=ei,
                             dst=dst: e.matmul(
                            dst[:, 0:512],
                            QKV[:, 4096 + sl * 512 + c4 * 128:
                                 4096 + sl * 512 + (c4 + 1) * 128],
                            dn_ap(ei, c4)[:, half * 512:(half + 1) * 512],
                            start=(c4 == 0), stop=(c4 == 3 and ei != 0)),
                             reads=[f"actw{sl}", f"dnw{par}_{c4}"], writes=[dk])
                    if ei == 0:
                        P.op("tensor", lambda e, half=half, tc=tc, dst=dst: e.matmul(
                            dst[:, 0:512], VT[0:16, tc * 128:(tc + 1) * 128],
                            QKV[0:16, 5120 + half * 512: 5120 + (half + 1) * 512],
                            start=False, stop=True),
                             reads=["sbt", "dnball"], writes=[dk])
                P.op("vector", lambda e, tc=tc: e.tensor_tensor(
                    XRES[:, tc * D:tc * D + 512], XRES[:, tc * D:tc * D + 512],
                    AX_ps[:, 0:512], ALU.add),
                     reads=["bkAX", f"xr{tc}"], writes=[f"xr{tc}"])
                P.op("vector", lambda e, tc=tc: e.tensor_tensor(
                    XRES[:, tc * D + 512:tc * D + 1024],
                    XRES[:, tc * D + 512:tc * D + 1024],
                    AV_ps[:, 0:512], ALU.add),
                     reads=["bkAV", f"xr{tc}"], writes=[f"xr{tc}"])

            for it in range(nexp * NT):
                emit_head(it)
                if it >= 1:
                    emit_tail(it - 1)
            emit_tail(nexp * NT - 1)
            return
        for ei in range(nexp):
            par = ei % 2
            rbase = par * 1024
            for c in range(8):
                P.dma("gpsimd", lambda e, ei=ei, c=c, par=par: e.dma_start(
                    gu_ap(par, c), m_gu[ei, c * 128:(c + 1) * 128, :]),
                    writes=[f"guw{par}_{c}"])
            if stage >= 4:
                for c4 in range(4):
                    P.dma("gpsimd", lambda e, ei=ei, c4=c4: e.dma_start(
                        QKV[:, c4 * 1024:(c4 + 1) * 1024],
                        m_dn[ei, c4 * 128:(c4 + 1) * 128, :]),
                        writes=[f"dnw{c4}"])
            P.dma("gpsimd", lambda e, ei=ei, rbase=rbase: e.dma_start(
                ROWS[0:1, rbase:rbase + 1024], m_gub[ei:ei + 1, :]),
                writes=[f"gub{par}"])
            for tc in range(NT):
                gk = ["S0", "S1"]
                for half in range(2):
                    for c in range(8):
                        P.op("tensor", lambda e, c=c, tc=tc, half=half, par=par: e.matmul(
                            S_ps[:, half * 512:(half + 1) * 512],
                            XLNF[c][:, tc * 128:(tc + 1) * 128],
                            gu_ap(par, c)[:, half * 512:(half + 1) * 512],
                            start=(c == 0), stop=False),
                             reads=[f"guw{par}_{c}", f"xlnf{c}"], writes=[gk[half]])
                    P.op("tensor", lambda e, half=half, rbase=rbase: e.matmul(
                        S_ps[:, half * 512:(half + 1) * 512], ONER[0:1, 0:128],
                        ROWS[0:1, rbase + half * 512: rbase + (half + 1) * 512],
                        start=False, stop=True),
                         reads=[f"gub{par}", "ONER"], writes=[gk[half]])
                if stage < 2:
                    P.op("vector", lambda e, tc=tc: e.tensor_tensor(
                        XRES[:, tc * D:(tc + 1) * D], XRES[:, tc * D:(tc + 1) * D],
                        S_ps[:, 0:1024], ALU.add),
                         reads=gk + [f"xr{tc}"], writes=[f"xr{tc}"])
                    continue
                GUp = S_ps[:, 0:1024]
                P.op("vector", lambda e: e.tensor_scalar(
                    SCR[:, 0:512], GUp[:, 0:1024:2], 7.0, None, ALU.min),
                     reads=gk, writes=["gtw"])
                P.op("scalar", lambda e: e.activation(
                    SCR[:, 512:1024], SCR[:, 0:512], AF.Sigmoid, scale=1.702),
                     reads=["gtw"], writes=["sig"])
                P.op("vector", lambda e, tc=tc, ei=ei: e.scalar_tensor_tensor(
                    SCR[:, 0:512], SCR[:, 0:512],
                    GT[:, tc * 16 + ei: tc * 16 + ei + 1], SCR[:, 512:1024],
                    ALU.mult, ALU.mult),
                     reads=["sig", "gtw", f"sc{tc}"], writes=["gtw"])
                P.op("vector", lambda e: e.tensor_scalar(
                    SCR[:, 512:1024], GUp[:, 1:1024:2], -7.0, 7.0, ALU.max, ALU.min),
                     reads=gk, writes=["sig"])
                P.op("vector", lambda e: e.scalar_tensor_tensor(
                    PBF[:, 0:512], SCR[:, 512:1024], 1.0, SCR[:, 0:512],
                    ALU.add, ALU.mult),
                     reads=["sig", "gtw"], writes=["ps0"])
                if 2 <= stage < 3:
                    P.op("vector", lambda e, tc=tc: e.tensor_tensor(
                        XRES[:, tc * D:(tc + 1) * D][:, 0:512],
                        XRES[:, tc * D:(tc + 1) * D][:, 0:512],
                        PBF[:, 0:512], ALU.add),
                         reads=["ps0", f"xr{tc}"], writes=[f"xr{tc}"])
                    continue
                for c in range(4):
                    P.op("tensor", lambda e, c=c: e.transpose(
                        TRbf[:, c * 128:(c + 1) * 128], PBF[:, c * 128:(c + 1) * 128],
                        IDBF[:]),
                         reads=["ps0", "IDBF"], writes=["bkTR"])
                for c in range(4):
                    P.op("vector", lambda e, c=c, par=par: e.tensor_copy(
                        QKV[:, 4096 + par * 512 + c * 128: 4096 + par * 512 + (c + 1) * 128],
                        TRbf[:, c * 128:(c + 1) * 128]),
                         reads=["bkTR"], writes=[f"actw{par}"])
                if stage == 3:
                    P.op("vector", lambda e, tc=tc, par=par: e.tensor_tensor(
                        XRES[:, tc * D:(tc + 1) * D][:, 0:512],
                        XRES[:, tc * D:(tc + 1) * D][:, 0:512],
                        QKV[:, 4096 + par * 512: 4608 + par * 512], ALU.add),
                         reads=[f"actw{par}", f"xr{tc}"], writes=[f"xr{tc}"])
                    continue
                dk = ["S2", "S3"]
                for half in range(2):
                    for c4 in range(4):
                        P.op("tensor", lambda e, c4=c4, half=half, par=par, ei=ei: e.matmul(
                            S_ps[:, 1024 + half * 512: 1024 + (half + 1) * 512],
                            QKV[:, 4096 + par * 512 + c4 * 128: 4096 + par * 512 + (c4 + 1) * 128],
                            QKV[:, c4 * 1024 + half * 512: c4 * 1024 + (half + 1) * 512],
                            start=(c4 == 0), stop=(c4 == 3 and ei != 0)),
                             reads=[f"actw{par}", f"dnw{c4}"], writes=[dk[half]])
                    if ei == 0:
                        P.op("tensor", lambda e, half=half, tc=tc: e.matmul(
                            S_ps[:, 1024 + half * 512: 1024 + (half + 1) * 512],
                            VT[0:16, tc * 128:(tc + 1) * 128],
                            QKV[0:16, 5120 + half * 512: 5120 + (half + 1) * 512],
                            start=False, stop=True),
                             reads=["sbt", "dnball"], writes=[dk[half]])
                P.op("vector", lambda e, tc=tc: e.tensor_tensor(
                    XRES[:, tc * D:(tc + 1) * D], XRES[:, tc * D:(tc + 1) * D],
                    S_ps[:, 1024:2048], ALU.add),
                     reads=dk + [f"xr{tc}"], writes=[f"xr{tc}"])

    def emit_out():
        outs = []
        for t in range(NT):
            outs.append(P.dma("gpsimd", lambda e, t=t: e.dma_start(
                y_out[t * 128:(t + 1) * 128, :], XRES[:, t * D:(t + 1) * D]),
                reads=[f"xr{t}"]))
        P.op("gpsimd", lambda e: e.engine_nop(), extra_deps=outs)

    emit_ln(0, fm=True)
    if upto >= 1:
        emit_l0_attn()
    if upto >= 1.5:
        emit_ln(1, fm=False)
        if debug:
            _x1d = nc.dram_tensor("dbg_x1", [T, D], F32, kind="ExternalOutput").ap()
            _x1deps = []
            for t in range(NT):
                _x1deps.append(P.dma("gpsimd", lambda e, t=t: e.dma_start(
                    _x1d[t * 128:(t + 1) * 128, :], XRES[:, t * D:(t + 1) * D]),
                    reads=[f"xr{t}"]))
            P.op("gpsimd", lambda e: e.engine_nop(), extra_deps=_x1deps)
            _std = nc.dram_tensor("dbg_stat1", [128, 48], F32, kind="ExternalOutput").ap()
            _i = P.dma("gpsimd", lambda e: e.dma_start(_std, STAT[:, 0:48]),
                       reads=[f"stb{t}" for t in range(NT)] + [f"str{t}" for t in range(NT)])
            P.op("gpsimd", lambda e: e.engine_nop(), extra_deps=[_i])
        if upto >= 1.7:
            emit_l0_ffn()
    if upto >= 3:
        emit_ln(2, fm=True)
        emit_l1_attn()
    if upto >= 3.5:
        emit_ln(3, fm=True)
        _stage = {3.71: 1, 3.72: 2, 3.721: 2.1, 3.722: 2.2, 3.7225: 2.25, 3.723: 2.3,
                  3.724: 2.4, 3.73: 3}.get(upto, 4)
        emit_l1_moe(nexp=(16 if upto >= 4 else (2 if upto >= 3.7 else 0)),
                    router_only=(upto < 3.7), stage=_stage)
    emit_out()

    if debug:
        def dump(name, ap, reads):
            dt_ = nc.dram_tensor(name, list(ap.shape), F32, kind="ExternalOutput").ap()
            i = P.dma("gpsimd", lambda e: e.dma_start(dt_, ap), reads=reads)
            P.op("gpsimd", lambda e: e.engine_nop(), extra_deps=[i])
        dump("dbg_xlnf0", XLNF[0][:], ["xlnf0"])
        dump("dbg_stat", STAT[:, 0:48], ["stb15"])
        if upto == 1:
            dump("dbg_ctxf0", CTXF[0][:], ["ctxf0"])
            dump("dbg_qkv", QKV[:], ["qkvA", "qkvB", "qkvC"])
            dump("dbg_bt", BT[:], ["bt15"])
            dump("dbg_vt", VT[:], ["vt"])
        if upto == 2:
            dump("dbg_hg", ABIG[:], ["abig" for fs in range(8)])
            dump("dbg_fsc", SMALL[:, 40:61], ["f59", "f58", "f50", "f48"])
            dump("dbg_w1t", XLNF[0][:].bitcast(FP8), ["xlnf0"])
            dump("dbg_w2t", CTXF[0][:].bitcast(FP8), ["ctxf0"])
            dump("dbg_hqf", QKV[:, 1024:5120], ["qkvA", "qkvB", "qkvC"])
            dump("dbg_sw", SMALL[:, 120:127], ["fswb", "fac"])

    P.materialize(nc, ctx)
    ctx.close()
    return nc


def pack_inputs(inputs):
    import ml_dtypes
    ins = {
        "w_in": inputs["L0_attn_w_in"], "b_in": inputs["L0_attn_b_in"],
        "w_out": inputs["L0_attn_w_out"], "b_out": inputs["L0_attn_b_out"],
        "f_w1": inputs["L0_ffn_w1"], "f_b1": inputs["L0_ffn_b1"],
        "f_w2": inputs["L0_ffn_w2"], "f_b2": inputs["L0_ffn_b2"],
        "a_wq": inputs["L1_attn_Wq"], "a_bq": inputs["L1_attn_bq"],
        "a_wk": inputs["L1_attn_Wk"], "a_bk": inputs["L1_attn_bk"],
        "a_wv": inputs["L1_attn_Wv"], "a_bv": inputs["L1_attn_bv"],
        "a_wo": inputs["L1_attn_Wo"], "a_bo": inputs["L1_attn_bo"],
        "rel_e": inputs["L1_attn_rel_embed"],
        "gru_w": inputs["L1_attn_gru_w"], "gru_b": inputs["L1_attn_gru_b"],
        "gru_c": np.asarray(inputs["L1_attn_gru_const"]).reshape(16),
        "gru_c2": np.asarray(inputs["L1_attn_gru_const"]).reshape(8, 2).T,
        "r_w": inputs["L1_ffn_router_weight"],
        "m_gu": inputs["L1_ffn_gate_up_proj"], "m_gub": inputs["L1_ffn_gate_up_proj_bias"],
        "m_dn": inputs["L1_ffn_down_proj"], "m_dnb": inputs["L1_ffn_down_proj_bias"],
        "c_oh": _bucket_onehot(),
        "c_id32": np.eye(128, dtype=np.float32),
        "c_idbf": np.eye(128, dtype=ml_dtypes.bfloat16),
        "c_p4": np.kron(np.eye(2, dtype=np.float32), np.ones((4, 1), np.float32)),
    }
    es2 = np.zeros((128, 2), dtype=ml_dtypes.bfloat16)
    es2[0:64, 0] = 1; es2[64:128, 1] = 1
    ins["c_es2"] = es2
    mi, mf, ml = _l0_masks()
    ins["c_mi"] = mi.astype(ml_dtypes.bfloat16)
    ins["c_mf"] = mf.astype(ml_dtypes.bfloat16)
    ins["c_ml"] = ml.astype(ml_dtypes.bfloat16)
    return {k: np.ascontiguousarray(v) for k, v in ins.items()}


def kernel(**inputs):
    ins = pack_inputs(inputs)
    x = np.ascontiguousarray(inputs["L0_attn_x"], dtype=np.float32)
    nc = build_program()
    in_maps = []
    for c in range(8):
        m = dict(ins)
        m["x"] = np.ascontiguousarray(x[c])
        in_maps.append(m)
    res = run_bass_kernel_spmd(nc, in_maps, core_ids=list(range(8)))
    out = np.stack([np.asarray(res.results[c]["y"]) for c in range(8)], axis=0)
    return out.astype(np.float32)


# version 11
# speedup vs baseline: 1.4871x; 1.1857x over previous
import numpy as np
from contextlib import ExitStack

import concourse.bass as bass
import concourse.mybir as mybir
from concourse.alu_op_type import AluOpType as ALU
from concourse.bass_utils import run_bass_kernel_spmd

F32 = mybir.dt.float32
BF16 = mybir.dt.bfloat16
FP8 = mybir.dt.float8e4
AF = mybir.ActivationFunctionType
AX = mybir.AxisListType.X

B, T, D = 8, 2048, 1024
NH, HD = 16, 64
NT = T // 128
FF = 4096
NE, NI = 16, 512
NEG = -1.0e30
M32 = 12582912.0
M64 = 6755399441055744.0

ENGINES = ("tensor", "vector", "scalar", "sync", "gpsimd")


def _bucket_onehot():
    rel = np.arange(4095, dtype=np.int64) - 2047
    nb = 16
    buckets = (rel > 0).astype(np.int64) * nb
    rp = np.abs(rel)
    max_exact = nb // 2
    is_small = rp < max_exact
    large = max_exact + (
        np.log(np.maximum(rp, 1) / max_exact) / np.log(128 / max_exact) * (nb - max_exact)
    ).astype(np.int64)
    large = np.minimum(large, nb - 1)
    buckets = buckets + np.where(is_small, rp, large)
    oh = np.zeros((32, 4095), dtype=np.float32)
    oh[buckets, np.arange(4095)] = 1.0
    return oh


def _l0_masks():
    i = np.arange(128)[:, None]
    j384 = np.arange(384)[None, :]
    j256 = np.arange(256)[None, :]
    inter = np.where((j384 >= i) & (j384 <= i + 256), 0.0, NEG).astype(np.float32)
    first = np.where(j256 <= i + 128, 0.0, NEG).astype(np.float32)
    last = np.where(j256 >= i, 0.0, NEG).astype(np.float32)
    return inter, first, last


class _MockInst:
    def then_inc(self, *a, **k):
        return self


class _MockEng:
    def __init__(self):
        self.calls = []

    def __getattr__(self, name):
        def f(*args, **kwargs):
            self.calls.append((name, args, kwargs))
            return _MockInst()
        return f


def _ap_is_psum(ap):
    try:
        return "psum" in str(getattr(ap, "space", "")).lower()
    except Exception:
        return False


def _ap_keys(ap, gran_sb=512, gran_ps=2048):
    try:
        t = ap.tensor
        name = t.name
        esz = {"float32": 4, "bfloat16": 2, "float16": 2, "int32": 4,
               "float8e4": 1, "int8": 1, "uint8": 1}.get(str(ap.dtype).split(".")[-1], 4)
        dims = [list(d) for d in ap.ap]
        space = str(getattr(ap, "space", ""))
        onchip = ("SB" in space) or ("PSUM" in space) or ("Psum" in space) or \
                 ("State" in space) or ("sb" in space.lower() and "dram" not in space.lower())
        is_ps = "psum" in space.lower()
        if not onchip and "dram" not in space.lower():
            onchip = True
        use = dims[1:] if (onchip and len(dims) > 1) else dims
        off = int(ap.offset)
        if onchip and len(dims) > 1 and dims[0][0] > 0:
            off = off % int(dims[0][0])
        lo = hi = off
        for st, ct in use:
            ext = int(st) * (int(ct) - 1)
            if ext >= 0:
                hi += ext
            else:
                lo += ext
        gran = gran_ps if is_ps else gran_sb
        b0 = (lo * esz) // gran
        b1 = (hi * esz) // gran
        return [f"{name}:{g}" for g in range(b0, b1 + 1)]
    except Exception:
        return []


def _is_ap(x):
    return hasattr(x, "tensor") and hasattr(x, "offset")


class Plan:

    def __init__(self):
        self.ops = []
        self.last_w = {}
        self.readers = {}
        self.eng_pos = {e: [] for e in ENGINES}

    def op(self, engine, fn, reads=(), writes=(), extra_deps=(), tag="", dma=False):
        idx = len(self.ops)
        mock = _MockEng()
        fn(mock)
        rkeys = list(reads)
        wkeys = list(writes)
        pkeys = []
        defer = False
        for name, args, kwargs in mock.calls:
            if name in ("bn_stats", "bn_aggr", "tensor_reduce", "reduce_max",
                        "reduce_sum", "max", "max_index", "tensor_tensor_reduce",
                        "reciprocal"):
                defer = True
                for a in args[1:]:
                    if _is_ap(a):
                        pkeys += _ap_keys(a)
            if "accum_out" in kwargs and kwargs["accum_out"] is not None:
                defer = True
            aps = [a for a in args if _is_ap(a)]
            if aps:
                wkeys += _ap_keys(aps[0])
                for a in aps[1:]:
                    rkeys += _ap_keys(a)
            if name == "tensor_scalar":
                for a in args[2:4]:
                    if _is_ap(a):
                        pkeys += _ap_keys(a)
            elif name == "scalar_tensor_tensor" and len(args) > 2 and _is_ap(args[2]):
                pkeys += _ap_keys(args[2])
            for kw, v in kwargs.items():
                if _is_ap(v):
                    if kw == "accum_out":
                        wkeys += _ap_keys(v)
                    else:
                        rkeys += _ap_keys(v)
                        if kw in ("bias", "scale", "alpha"):
                            pkeys += _ap_keys(v)
        deps = set(extra_deps)
        hard = set()
        for k in rkeys:
            if k in self.last_w:
                d = self.last_w[k]
                deps.add(d)
                if self.ops[d]["defer"]:
                    hard.add(d)
        for k in pkeys:
            if k in self.last_w:
                hard.add(self.last_w[k])
        for k in wkeys:
            if k in self.last_w:
                d = self.last_w[k]
                deps.add(d)
                if self.ops[d]["defer"]:
                    hard.add(d)
            for r in self.readers.get(k, ()):
                deps.add(r)
        deps.discard(idx)
        hard.discard(idx)
        self.ops.append(dict(engine=engine, fn=fn, deps=deps, hard=hard, idx=idx,
                             inc=0, tag=tag, dma=dma, defer=defer))
        self.eng_pos[engine].append(idx)
        for k in wkeys:
            self.last_w[k] = idx
            self.readers[k] = []
        for k in rkeys:
            self.readers.setdefault(k, []).append(idx)
        return idx

    def dma(self, engine, fn, reads=(), writes=(), extra_deps=(), tag=""):
        return self.op(engine, fn, reads, writes, extra_deps, tag, dma=True)

    def materialize(self, nc, ctx):
        ops = self.ops
        need_inc = [False] * len(ops)
        for o in ops:
            for d in o["deps"]:
                if (ops[d]["engine"] != o["engine"] or ops[d]["dma"]
                        or o["engine"] != "tensor"):
                    need_inc[d] = True
            for d in o["hard"]:
                need_inc[d] = True
        sems = {e: ctx.enter_context(nc.semaphore(f"sem_{e}")) for e in ENGINES}
        cum = {e: 0 for e in ENGINES}
        val_after = [0] * len(ops)
        for o in ops:
            e = o["engine"]
            if need_inc[o["idx"]]:
                amt = 16 if o["dma"] else 1
                cum[e] += amt
                o["inc"] = amt
            val_after[o["idx"]] = cum[e]
        hwm = {e: {e2: 0 for e2 in ENGINES} for e in ENGINES}

        def emit_engine(eng_name, e):
            my = hwm[eng_name]
            for idx in self.eng_pos[eng_name]:
                o = ops[idx]
                waits = {}
                for d in o["deps"]:
                    pe = ops[d]["engine"]
                    if (pe == eng_name and not ops[d]["dma"] and d not in o["hard"]
                            and eng_name == "tensor"):
                        continue
                    v = val_after[d]
                    if v > waits.get(pe, 0):
                        waits[pe] = v
                for d in o["hard"]:
                    pe = ops[d]["engine"]
                    v = val_after[d]
                    if v > waits.get(pe, 0):
                        waits[pe] = v
                for pe in sorted(waits):
                    if waits[pe] > my[pe]:
                        e.wait_ge(sems[pe], waits[pe])
                        my[pe] = waits[pe]
                inst = o["fn"](e)
                if o["inc"]:
                    inst.then_inc(sems[eng_name], o["inc"])

        blk = ctx.enter_context(nc.Block())
        for name in ENGINES:
            getattr(blk, name)(lambda e, _n=name: emit_engine(_n, e))
        return sems


def build_program(upto=99, debug=False):
    nc = bass.Bass("TRN2", target_bir_lowering=False, detect_race_conditions=False)
    P = Plan()
    ctx = ExitStack()

    di = {}
    def din(name, shape, dt=F32):
        di[name] = nc.dram_tensor(name, list(shape), dt, kind="ExternalInput").ap()
        return di[name]

    x_in = din("x", (T, D))
    w_in = din("w_in", (3 * D, D)); b_in = din("b_in", (3 * D,))
    w_out = din("w_out", (D, D)); b_out = din("b_out", (D,))
    f_w1 = din("f_w1", (FF, D)); f_b1 = din("f_b1", (FF,))
    f_w2 = din("f_w2", (D, FF)); f_b2 = din("f_b2", (D,))
    a_w = {s: din(f"a_w{s}", (D, D)) for s in "qkvo"}
    a_b = {s: din(f"a_b{s}", (D,)) for s in "qkvo"}
    rel_e = din("rel_e", (32, 16))
    gru_w = din("gru_w", (64, 8)); gru_b = din("gru_b", (8,))
    gru_c = din("gru_c", (16,))
    gru_c2 = din("gru_c2", (2, 8))
    r_w = din("r_w", (D, NE))
    m_gu = din("m_gu", (NE, D, 2 * NI)); m_gub = din("m_gub", (NE, 2 * NI))
    m_dn = din("m_dn", (NE, NI, D)); m_dnb = din("m_dnb", (NE, D))
    c_oh = din("c_oh", (32, 4095))
    c_mi = din("c_mi", (128, 384), BF16); c_mf = din("c_mf", (128, 256), BF16); c_ml = din("c_ml", (128, 256), BF16)
    c_id32 = din("c_id32", (128, 128))
    c_idbf = din("c_idbf", (128, 128), BF16)
    c_es2 = din("c_es2", (128, 2), BF16)
    c_p4 = din("c_p4", (8, 2))

    y_out = nc.dram_tensor("y", [T, D], F32, kind="ExternalOutput").ap()
    pb_dram = nc.dram_tensor("pb_scratch", [NH, 4608], F32, kind="Internal").ap()

    def sbt(name, shape, dt=F32):
        return ctx.enter_context(nc.sbuf_tensor(name, list(shape), dt))

    XRES = sbt("XRES", (128, NT * D))
    XLNF = [sbt(f"XLNF{c}", (128, T), BF16) for c in range(8)]
    CTXF = [sbt(f"CTXF{c}", (128, T), BF16) for c in range(8)]
    ABIG = sbt("ABIG", (128, 4096))
    PTT = sbt("PTT", (128, 4096), BF16)
    QKV = sbt("QKV", (128, 3 * T), BF16)
    VT = sbt("VT", (128, T), BF16)
    WSL = sbt("WSL", (128, 3072), BF16)
    SCR = sbt("SCR", (128, 2048))
    PBF = sbt("PBF", (128, 2048), BF16)
    ROWS = sbt("ROWS", (1, 6144), BF16)
    ID32 = sbt("ID32", (128, 128))
    IDBF = sbt("IDBF", (128, 128), BF16)
    ES2 = sbt("ES2", (128, 2), BF16)
    STAT = sbt("STAT", (128, 64))
    SMALL = sbt("SMALL", (128, 128))
    BT = sbt("BT", (128, 256))
    GT = sbt("GT", (128, 256))
    MK = sbt("MK", (128, 896), BF16)
    CTM = sbt("CTM", (128, 256), BF16)
    ONER = sbt("ONER", (1, 512), BF16)
    U1R = sbt("U1R", (1, 256), BF16)
    P75 = sbt("P75", (128, 1)); PDET = sbt("PDET", (128, 1)); PM1 = sbt("PM1", (128, 1))
    MXC = sbt("MXC", (128, 1))
    FROW = sbt("FROW", (1, 128))
    GAB = sbt("GAB", (128, 4), BF16)
    CEPS = sbt("CEPS", (128, 4))

    S_ps = ctx.enter_context(nc.psum_tensor("S_ps", [128, 2048], F32))
    TR_ps = ctx.enter_context(nc.psum_tensor("TR_ps", [128, 512], F32))
    AV_ps = ctx.enter_context(nc.psum_tensor("AV_ps", [128, 512], F32))
    AX_ps = ctx.enter_context(nc.psum_tensor("AX_ps", [128, 512], F32))
    AY_ps = ctx.enter_context(nc.psum_tensor("AY_ps", [128, 512], F32))
    TRbf = TR_ps[:].bitcast(BF16)
    AXbf = AX_ps[:].bitcast(BF16)
    AVbf = AV_ps[:].bitcast(BF16)

    SKEY = ["S0", "S1", "S2", "S3"]

    P.dma("sync", lambda e: e.dma_start(ID32[:], c_id32), writes=["ID32"])
    P.dma("sync", lambda e: e.dma_start(IDBF[:], c_idbf), writes=["IDBF"])
    P.dma("sync", lambda e: e.dma_start(ES2[:], c_es2), writes=["ES2"])
    P.dma("sync", lambda e: e.dma_start(MK[:, 0:384], c_mi), writes=["MK"])
    P.dma("sync", lambda e: e.dma_start(MK[:, 384:640], c_mf), writes=["MK"])
    P.dma("sync", lambda e: e.dma_start(MK[:, 640:896], c_ml), writes=["MK"])
    for t in range(NT):
        P.dma("sync", lambda e, t=t: e.dma_start(
            XRES[:, t * D:(t + 1) * D], x_in[t * 128:(t + 1) * 128, :]),
            writes=[f"xr{t}"])
    P.op("vector", lambda e: e.memset(P75[:], 0.75), writes=["P75"])
    P.op("vector", lambda e: e.memset(CEPS[:, 0:1], 1e-5), writes=["ceps"])
    P.op("vector", lambda e: e.memset(CEPS[:, 1:2], 1e-30), writes=["ceps"])
    P.op("vector", lambda e: e.memset(ONER[:], 1.0), writes=["ONER"])
    P.op("vector", lambda e: e.tensor_scalar(PDET[:], P75[:], M64, M64, ALU.add, ALU.subtract),
         reads=["P75"], writes=["PDET"])
    P.op("vector", lambda e: e.tensor_scalar(PM1[:], PDET[:], -1.0, 1.0, ALU.mult, ALU.add),
         reads=["PDET"], writes=["PM1"])
    P.op("vector", lambda e: e.tensor_scalar(MXC[:], PDET[:], M64 - M32, M32, ALU.mult, ALU.add),
         reads=["PDET"], writes=["MXC"])

    def emit_ln(blk, fm=True):
        for t in range(NT):
            xa = XRES[:, t * D:(t + 1) * D]
            mcol = STAT[:, t:t + 1]
            rcol = STAT[:, 16 + t:17 + t]
            bcol = STAT[:, 32 + t:33 + t]
            P.op("vector", lambda e, xa=xa: e.bn_stats(SMALL[:, 0:6], xa[:, 0:512]),
                 reads=[f"xr{t}"], writes=["bnA"])
            P.op("vector", lambda e, xa=xa: e.bn_stats(SMALL[:, 6:12], xa[:, 512:1024]),
                 reads=[f"xr{t}"], writes=["bnB"])
            P.op("vector", lambda e: e.bn_aggr(SMALL[:, 16:18], SMALL[:, 0:12]),
                 reads=["bnA", "bnB"], writes=["mv"])
            P.op("vector", lambda e, mcol=mcol: e.tensor_copy(mcol, SMALL[:, 16:17]),
                 reads=["mv"], writes=[f"stm{t}"])
            P.op("scalar", lambda e: e.activation(SMALL[:, 18:19], SMALL[:, 17:18],
                                                  AF.Sqrt, bias=CEPS[:, 0:1]),
                 reads=["mv", "ceps"], writes=["sd"])
            P.op("vector", lambda e, rcol=rcol: e.reciprocal(rcol, SMALL[:, 18:19]),
                 reads=["sd"], writes=[f"str{t}"])
            P.op("vector", lambda e, mcol=mcol, rcol=rcol, bcol=bcol: e.tensor_scalar(
                bcol, mcol, rcol, -1.0, ALU.mult, ALU.mult),
                 reads=[f"stm{t}", f"str{t}"], writes=[f"stb{t}"])
            if fm:
                P.op("scalar", lambda e, xa=xa, rcol=rcol, bcol=bcol: e.activation(
                    PBF[:, 0:1024], xa, AF.Identity, bias=bcol, scale=rcol),
                     reads=[f"xr{t}", f"str{t}", f"stb{t}"], writes=["ps0", "ps1"])
                for c in range(8):
                    P.op("tensor", lambda e, c=c: e.transpose(
                        TRbf[:, c * 128:(c + 1) * 128], PBF[:, c * 128:(c + 1) * 128], IDBF[:]),
                         reads=["ps0", "ps1", "IDBF"], writes=["bkTR"])
                for c in range(8):
                    P.op("vector", lambda e, c=c, t=t: e.tensor_copy(
                        XLNF[c][:, t * 128:(t + 1) * 128], TRbf[:, c * 128:(c + 1) * 128]),
                         reads=["bkTR"], writes=[f"xlnf{c}"])

    ABIGBF = ABIG[:].bitcast(BF16)

    def tr_bf16_to(dst4, src_key, dst_key):
        for g in range(2):
            TB = TRbf if g == 0 else AXbf
            tbk = "bkTR" if g == 0 else "bkAX"
            for b in range(4):
                c = g * 4 + b
                P.op("tensor", lambda e, c=c, b=b, TB=TB: e.transpose(
                    TB[:, b * 128:(b + 1) * 128], ABIGBF[:, c * 128:(c + 1) * 128],
                    IDBF[:]),
                     reads=[src_key, "IDBF"], writes=[tbk])
            P.op("vector", lambda e, g=g, TB=TB: e.tensor_copy(dst4(g), TB[:, 0:512]),
                 reads=[tbk], writes=[dst_key])

    def emit_l0_attn():
        P.dma("gpsimd", lambda e: e.dma_start(ROWS[0:1, 0:3072], b_in.unsqueeze(0)),
              writes=["rows_bin"])
        P.dma("gpsimd", lambda e: e.dma_start(ROWS[0:1, 3072:4096], b_out.unsqueeze(0)),
              writes=["rows_bout"])

        def qt_mm_phase(hp, qt):
            hA = 2 * hp
            lo = max(0, qt * 128 - 128)
            hi = min(T, qt * 128 + 256)
            w = hi - lo
            for h01 in range(2):
                sbank = h01 * 2 + qt % 2
                scol = sbank * 512
                P.op("tensor", lambda e, h01=h01, qt=qt, lo=lo, w=w, scol=scol: e.matmul(
                    S_ps[:, scol:scol + w],
                    QKV[h01 * 64:(h01 + 1) * 64, qt * 128:(qt + 1) * 128],
                    QKV[h01 * 64:(h01 + 1) * 64, T + lo: T + lo + w],
                    start=True, stop=True),
                     reads=["qkvA", "qkvB"], writes=[SKEY[sbank]], tag=f"s{hp}_{qt}_{h01}")

        def qt_tailA(hp, qt):
            hA = 2 * hp
            lo = max(0, qt * 128 - 128)
            w = min(T, qt * 128 + 256) - lo
            mofs = 384 if qt == 0 else (640 if qt == NT - 1 else 0)
            for h01 in range(2):
                h = hA + h01
                sbank = h01 * 2 + qt % 2
                scol = sbank * 512
                pslot = sbank * 512
                P.op("vector", lambda e, scol=scol, w=w, mofs=mofs: e.tensor_tensor(
                    S_ps[:, scol:scol + w], S_ps[:, scol:scol + w], MK[:, mofs:mofs + w],
                    ALU.add),
                     reads=[SKEY[sbank], "MK"], writes=[SKEY[sbank]])
                zc = 24 + sbank
                P.op("scalar", lambda e, scol=scol, w=w, pslot=pslot, qt=qt, h=h, zc=zc:
                     e.activation(PBF[:, pslot:pslot + w], S_ps[:, scol:scol + w], AF.Exp,
                                  bias=BT[:, qt * 16 + h: qt * 16 + h + 1],
                                  accum_out=SMALL[:, zc:zc + 1]),
                     reads=[SKEY[sbank], f"bt{qt}"], writes=[f"ps{sbank}", f"z{sbank}"])

        def qt_tailB(hp, qt):
            lo = max(0, qt * 128 - 128)
            w = min(T, qt * 128 + 256) - lo
            nb = w // 128
            TB = TRbf if qt % 2 == 0 else AXbf
            tbk = "bkTR" if qt % 2 == 0 else "bkAX"
            for h01 in range(2):
                sbank = h01 * 2 + qt % 2
                pslot = sbank * 512
                zc = 24 + sbank
                P.op("vector", lambda e, zc=zc: e.reciprocal(
                    SMALL[:, zc + 4:zc + 5], SMALL[:, zc:zc + 1]),
                     reads=[f"z{sbank}"], writes=[f"zr{sbank}"])
                for bkt in range(nb):
                    P.op("tensor", lambda e, pslot=pslot, bkt=bkt, h01=h01, TB=TB: e.transpose(
                        TB[:, (h01 * 3 + bkt) * 128:(h01 * 3 + bkt + 1) * 128],
                        PBF[:, pslot + bkt * 128: pslot + (bkt + 1) * 128], IDBF[:]),
                         reads=[f"ps{sbank}", "IDBF"], writes=[tbk])
                P.op("vector", lambda e, h01=h01, TB=TB, nb=nb, qt=qt: e.tensor_copy(
                    PTT[:, h01 * 2048 + (qt % 2) * 1024:
                        h01 * 2048 + (qt % 2) * 1024 + nb * 128],
                    TB[:, h01 * 3 * 128:(h01 * 3 + nb) * 128]),
                     reads=[tbk], writes=[f"ptb{h01}_{qt % 2}"])

        def qt_tailC(hp, qt):
            lo = max(0, qt * 128 - 128)
            w = min(T, qt * 128 + 256) - lo
            nb = w // 128
            TB = TRbf if qt % 2 == 0 else AXbf
            tbk = "bkTR" if qt % 2 == 0 else "bkAX"
            for h01 in range(2):
                sbank = h01 * 2 + qt % 2
                zc = 24 + sbank
                avk = "AV" if h01 == 0 else "AY"
                avp = AV_ps if h01 == 0 else AY_ps
                for bkt in range(nb):
                    ktg = lo // 128 + bkt
                    P.op("tensor", lambda e, bkt=bkt, h01=h01, ktg=ktg, avp=avp, nb=nb,
                         qt=qt: e.matmul(
                        avp[:, 0:64],
                        PTT[:, h01 * 2048 + (qt % 2) * 1024 + bkt * 128:
                            h01 * 2048 + (qt % 2) * 1024 + (bkt + 1) * 128],
                        VT[:, ktg * 128 + h01 * 64: ktg * 128 + h01 * 64 + 64],
                        start=(bkt == 0), stop=(bkt == nb - 1)),
                         reads=[f"ptb{h01}_{qt % 2}", "vt"], writes=[avk])
                cslot = sbank * 64
                P.op("vector", lambda e, avp=avp, zc=zc, cslot=cslot: e.tensor_scalar(
                    CTM[:, cslot:cslot + 64], avp[:, 0:64],
                    SMALL[:, zc + 4:zc + 5], None, ALU.mult),
                     reads=[avk, f"zr{sbank}"], writes=[f"ctm{sbank}"])
                tcol = 768 if h01 == 0 else 896
                P.op("tensor", lambda e, cslot=cslot, TB=TB, tcol=tcol: e.transpose(
                    TB[0:64, tcol:tcol + 128], CTM[:, cslot:cslot + 64], IDBF[:]),
                     reads=[f"ctm{sbank}", "IDBF"], writes=[tbk])
                P.op("vector", lambda e, h01=h01, qt=qt, hp=hp, TB=TB, tcol=tcol: e.tensor_copy(
                    CTXF[hp][h01 * 64:(h01 + 1) * 64, qt * 128:(qt + 1) * 128],
                    TB[0:64, tcol:tcol + 128]),
                     reads=[tbk], writes=[f"ctxf{hp}"])

        for hp in range(8):
            hA = 2 * hp
            for si, s in enumerate("qkv"):
                P.dma("gpsimd", lambda e, si=si, hp=hp: e.dma_start(
                    ABIGBF[:, 0:1024],
                    w_in[si * D + hp * 128: si * D + hp * 128 + 128, :]),
                      writes=["abig"])
                tr_bf16_to(lambda g, si=si: WSL[:, si * 1024 + g * 512:
                                                si * 1024 + (g + 1) * 512],
                           "abig", f"wsl{s}")
            for si, s in enumerate("qkv"):
                for tc4 in range(4):
                    for c in range(8):
                        P.op("tensor", lambda e, si=si, c=c, tc4=tc4: e.matmul(
                            AX_ps[:, 0:512],
                            WSL[:, si * 1024 + c * 128: si * 1024 + (c + 1) * 128],
                            XLNF[c][:, tc4 * 512:(tc4 + 1) * 512],
                            start=(c == 0), stop=False),
                             reads=[f"wsl{s}", f"xlnf{c}"], writes=["bkAX"])
                    P.op("tensor", lambda e, si=si, hp=hp: e.matmul(
                        AX_ps[:, 0:512],
                        ROWS[0:1, si * 1024 + hp * 128: si * 1024 + (hp + 1) * 128],
                        ONER[0:1, 0:512], start=False, stop=True),
                         reads=["rows_bin", "ONER"], writes=["bkAX"])
                    P.op("scalar", lambda e, si=si, tc4=tc4, s=s: e.activation(
                        QKV[:, si * T + tc4 * 512: si * T + (tc4 + 1) * 512],
                        AX_ps[:, 0:512], AF.Identity, scale=(0.125 if si == 0 else 1.0)),
                         reads=["bkAX"], writes=[["qkvA", "qkvB", "qkvC"][si]])
            for g in range(4):
                TBv = TRbf if g % 2 == 0 else AXbf
                tbkv = "bkTR" if g % 2 == 0 else "bkAX"
                for b in range(4):
                    kt = g * 4 + b
                    P.op("tensor", lambda e, kt=kt, b=b, TBv=TBv: e.transpose(
                        TBv[:, b * 128:(b + 1) * 128],
                        QKV[:, 2 * T + kt * 128: 2 * T + (kt + 1) * 128], IDBF[:]),
                         reads=["qkvC", "IDBF"], writes=[tbkv])
                P.op("vector", lambda e, g=g, TBv=TBv: e.tensor_copy(
                    VT[:, g * 512:(g + 1) * 512], TBv[:, 0:512]),
                     reads=[tbkv], writes=["vt"])
            P.op("scalar", lambda e: e.activation(PBF[:, 0:2048], QKV[:, 0:T], AF.Square),
                 reads=["qkvA"], writes=["ps0", "ps1", "ps2", "ps3"])
            for tc4 in range(4):
                P.op("tensor", lambda e, tc4=tc4: e.matmul(
                    S_ps[0:2, tc4 * 512:(tc4 + 1) * 512], ES2[:],
                    PBF[:, tc4 * 512:(tc4 + 1) * 512], start=True, stop=True),
                     reads=["ps0", "ps1", "ps2", "ps3", "ES2"], writes=[SKEY[tc4]])
            P.op("scalar", lambda e: e.activation(SCR[0:2, 0:2048], S_ps[0:2, 0:2048],
                                                  AF.Sqrt, bias=CEPS[0:2, 1:2]),
                 reads=SKEY + ["ceps"], writes=["scrA", "scrB"])
            P.op("scalar", lambda e: e.activation(PBF[:, 0:2048], QKV[:, T:2 * T], AF.Square),
                 reads=["qkvB"], writes=["ps0", "ps1", "ps2", "ps3"])
            for tc4 in range(4):
                P.op("tensor", lambda e, tc4=tc4: e.matmul(
                    S_ps[0:2, tc4 * 512:(tc4 + 1) * 512], ES2[:],
                    PBF[:, tc4 * 512:(tc4 + 1) * 512], start=True, stop=True),
                     reads=["ps0", "ps1", "ps2", "ps3", "ES2"], writes=[SKEY[tc4]])
            P.op("vector", lambda e: e.tensor_reduce(SMALL[0:2, 20:21], S_ps[0:2, 0:2048],
                                                     AX, ALU.max),
                 reads=SKEY, writes=["kmax2"])
            P.op("scalar", lambda e: e.activation(SMALL[0:2, 21:22], SMALL[0:2, 20:21],
                                                  AF.Sqrt, bias=CEPS[0:2, 1:2]),
                 reads=["kmax2", "ceps"], writes=["kmax"])
            P.op("vector", lambda e: e.tensor_scalar(
                SCR[0:2, 0:2048], SCR[0:2, 0:2048], SMALL[0:2, 21:22], -1.0,
                ALU.mult, ALU.mult),
                 reads=["scrA", "scrB", "kmax"], writes=["negB", "scrA", "scrB"])
            for qt in range(NT):
                P.op("tensor", lambda e, qt=qt: e.transpose(
                    TR_ps[:, 0:2], SCR[0:2, qt * 128:(qt + 1) * 128], ID32[0:2, 0:2]),
                     reads=["negB", "scrA", "scrB", "ID32"], writes=["bkTR"])
                P.op("vector", lambda e, qt=qt, hA=hA: e.tensor_copy(
                    BT[:, qt * 16 + hA: qt * 16 + hA + 2], TR_ps[:, 0:2]),
                     reads=["bkTR"], writes=[f"bt{qt}"])
            for qt in range(NT + 3):
                if qt < NT:
                    qt_mm_phase(hp, qt)
                if 0 <= qt - 1 < NT:
                    qt_tailA(hp, qt - 1)
                if 0 <= qt - 2 < NT:
                    qt_tailB(hp, qt - 2)
                    qt_tailC(hp, qt - 2)

        def wot_ap(cc):
            if cc < 6:
                return QKV[:, cc * 1024:(cc + 1) * 1024]
            return WSL[:, (cc - 6) * 1024:(cc - 5) * 1024]

        def wot_alias(cc):
            return ["qkvA", "qkvA", "qkvB", "qkvB", "qkvC", "qkvC", "wslq", "wslk"][cc]
        for c in range(8):
            P.dma("gpsimd", lambda e, c=c: e.dma_start(
                ABIGBF[:, 0:1024], w_out[c * 128:(c + 1) * 128, :]), writes=["abig"])
            for cc in range(8):
                TB = TRbf if cc % 2 == 0 else AXbf
                tbk = "bkTR" if cc % 2 == 0 else "bkAX"
                sl = (cc // 2) % 2
                P.op("tensor", lambda e, cc=cc, TB=TB, sl=sl: e.transpose(
                    TB[:, sl * 128:(sl + 1) * 128],
                    ABIGBF[:, cc * 128:(cc + 1) * 128], IDBF[:]),
                     reads=["abig", "IDBF"], writes=[tbk])
                P.op("vector", lambda e, c=c, cc=cc, TB=TB, sl=sl: e.tensor_copy(
                    wot_ap(cc)[:, c * 128:(c + 1) * 128],
                    TB[:, sl * 128:(sl + 1) * 128]),
                     reads=[tbk], writes=["wotx", wot_alias(cc)])
        for t in range(NT):
            pk = ["S0", "S1"]
            for c in range(8):
                for half in range(2):
                    P.op("tensor", lambda e, c=c, t=t, half=half: e.matmul(
                        S_ps[:, half * 512:(half + 1) * 512],
                        CTXF[c][:, t * 128:(t + 1) * 128],
                        wot_ap(c)[:, half * 512:(half + 1) * 512],
                        start=(c == 0), stop=False),
                         reads=[f"ctxf{c}", "wotx"], writes=pk)
            for half in range(2):
                P.op("tensor", lambda e, half=half: e.matmul(
                    S_ps[:, half * 512:(half + 1) * 512], ONER[0:1, 0:128],
                    ROWS[0:1, 3072 + half * 512: 3072 + (half + 1) * 512],
                    start=False, stop=True),
                     reads=["ONER", "rows_bout"], writes=pk)
            P.op("vector", lambda e, t=t: e.tensor_scalar(
                SMALL[:, 62:63], STAT[:, 16 + t:17 + t], 1.0, None, ALU.add),
                 reads=[f"str{t}"], writes=["pr1"])
            P.op("scalar", lambda e, t=t: e.activation(
                SCR[:, 0:1024], XRES[:, t * D:(t + 1) * D], AF.Identity,
                bias=STAT[:, 32 + t:33 + t], scale=SMALL[:, 62:63]),
                 reads=[f"xr{t}", f"stb{t}", "pr1"], writes=["scrA"])
            P.op("vector", lambda e, t=t: e.tensor_tensor(
                XRES[:, t * D:(t + 1) * D], SCR[:, 0:1024], S_ps[:, 0:1024], ALU.add),
                 reads=["scrA"] + pk, writes=[f"xr{t}"])

    def emit_l0_ffn():
        P.dma("gpsimd", lambda e: e.dma_start(ROWS[0:1, 0:4096], f_b1.unsqueeze(0)),
              writes=["rows_bin", "rows_bout"])
        P.dma("gpsimd", lambda e: e.dma_start(ROWS[0:1, 4096:5120], f_b2.unsqueeze(0)),
              writes=["rows_bout"])

        P.op("vector", lambda e: e.memset(SMALL[:, 120:122], 0.0), writes=["fac"])
        P.op("vector", lambda e: e.memset(CEPS[:, 2:3], 1.0), writes=["ceps1"])
        for wi, (wap, nch) in enumerate(((f_w1, 8), (f_w2, 8))):
            for ch in range(nch):
                if wi == 0:
                    src_ap = wap[ch * 512:(ch + 1) * 512, :].rearrange(
                        "(a p) d -> p a d", p=128)
                    dst_ap = ABIG[:].rearrange("p (a d) -> p a d", a=4)
                else:
                    src_ap = wap[ch * 128:(ch + 1) * 128, :]
                    dst_ap = ABIG[:]
                P.dma("sync", lambda e, src_ap=src_ap, dst_ap=dst_ap: e.dma_start(
                    dst_ap, src_ap), writes=["abig"])
                P.op("vector", lambda e: e.tensor_reduce(
                    SMALL[:, 122:123], ABIG[:], AX, ALU.add, apply_absolute_value=True),
                     reads=["abig"], writes=["fred"])
                P.op("vector", lambda e, wi=wi: e.tensor_tensor(
                    SMALL[:, 120 + wi:121 + wi], SMALL[:, 120 + wi:121 + wi],
                    SMALL[:, 122:123], ALU.add),
                     reads=["fred", "fac"], writes=["fac"])
        P.op("tensor", lambda e: e.matmul(TR_ps[0:1, 0:2], CEPS[:, 2:3], SMALL[:, 120:122],
                                          start=True, stop=True),
             reads=["fac", "ceps1"], writes=["bkTR"])
        P.op("vector", lambda e: e.tensor_scalar(
            SCR[0:1, 0:2], TR_ps[0:1, 0:2], 1.0 / (FF * D), 1e-5, ALU.mult, ALU.max),
             reads=["bkTR"], writes=["fc12", "scrA"])
        P.op("vector", lambda e: e.reciprocal(SCR[0:1, 2:4], SCR[0:1, 0:2]),
             reads=["fc12"], writes=["fsw12", "scrA"])
        P.op("vector", lambda e: e.memset(FROW[:], 1.0), writes=["fonesr"])
        P.op("tensor", lambda e: e.matmul(TR_ps[:, 4:8], FROW[:], SCR[0:1, 0:4],
                                          start=True, stop=True),
             reads=["fc12", "fsw12", "fonesr", "scrA"], writes=["bkTR"])
        P.op("vector", lambda e: e.tensor_copy(SMALL[:, 123:127], TR_ps[:, 4:8]),
             reads=["bkTR"], writes=["fswb"])
        C1C, C2C = SMALL[:, 123:124], SMALL[:, 124:125]
        SW1C, SW2C = SMALL[:, 125:126], SMALL[:, 126:127]

        def tern_quarter(load_dma, swc, dst_fn, dst_key_fn):
            load_dma()
            P.op("vector", lambda e, swc=swc: e.tensor_scalar(
                SCR[:, 0:1024], ABIG[:, 0:1024], swc, 0.5, ALU.mult, ALU.is_ge),
                 reads=["abig", "fswb"], writes=["scrA"])
            P.op("vector", lambda e, swc=swc: e.tensor_scalar(
                SCR[:, 1024:2048], ABIG[:, 0:1024], swc, -0.5, ALU.mult, ALU.is_le),
                 reads=["abig", "fswb"], writes=["scrB"])
            P.op("vector", lambda e: e.tensor_tensor(
                PBF[:, 0:1024], SCR[:, 0:1024], SCR[:, 1024:2048], ALU.subtract),
                 reads=["scrA", "scrB"], writes=["ps0", "ps1"])
            for c in range(8):
                P.op("tensor", lambda e, c=c: e.transpose(
                    TRbf[:, c * 128:(c + 1) * 128], PBF[:, c * 128:(c + 1) * 128], IDBF[:]),
                     reads=["ps0", "ps1", "IDBF"], writes=["bkTR"])
            for c in range(8):
                P.op("scalar", lambda e, c=c: e.copy(
                    dst_fn(c), TRbf[:, c * 128:(c + 1) * 128]),
                     reads=["bkTR"], writes=[dst_key_fn(c)])

        for ch in range(32):
            tern_quarter(
                lambda ch=ch: P.dma("sync", lambda e: e.dma_start(
                    ABIG[:, 0:1024], f_w1[ch * 128:(ch + 1) * 128, :]), writes=["abig"]),
                SW1C,
                lambda c, ch=ch: XLNF[c][:].bitcast(FP8)[:, ch * 128:(ch + 1) * 128],
                lambda c: f"xlnf{c}")
        for ch in range(8):
            for qr in range(4):
                def dst(c, ch=ch, qr=qr):
                    fc = qr * 8 + c
                    return CTXF[fc // 4][:].bitcast(FP8)[
                        :, (fc % 4) * 1024 + ch * 128: (fc % 4) * 1024 + (ch + 1) * 128]
                tern_quarter(
                    lambda ch=ch, qr=qr: P.dma("sync", lambda e: e.dma_start(
                        ABIG[:, 0:1024],
                        f_w2[ch * 128:(ch + 1) * 128, qr * 1024:(qr + 1) * 1024]),
                        writes=["abig"]),
                    SW2C, dst, lambda c, qr=qr: f"ctxf{(qr * 8 + c) // 4}")

        def W1T(c):
            return XLNF[c][:].bitcast(FP8)
        def W2T(fc):
            return CTXF[fc // 4][:].bitcast(FP8)[:, (fc % 4) * 1024:(fc % 4 + 1) * 1024]

        for tc in range(NT if upto >= 2 else 0):
            rcol = STAT[:, 16 + tc:17 + tc]
            mcol = STAT[:, tc:tc + 1]
            xa = XRES[:, tc * D:(tc + 1) * D]
            P.op("vector", lambda e, rcol=rcol: e.tensor_tensor(
                SMALL[:, 40:41], rcol, rcol, ALU.mult), reads=[f"str{tc}"], writes=["f40"])
            P.op("vector", lambda e: e.tensor_scalar(
                SMALL[:, 41:42], SMALL[:, 40:41], -1e-5, 1.0 + 1e-5, ALU.mult, ALU.add),
                 reads=["f40"], writes=["f41"])
            P.op("scalar", lambda e: e.activation(SMALL[:, 42:43], SMALL[:, 41:42], AF.Sqrt),
                 reads=["f41"], writes=["f42"])
            P.op("vector", lambda e: e.reciprocal(SMALL[:, 43:44], SMALL[:, 42:43]),
                 reads=["f42"], writes=["f43"])
            P.op("vector", lambda e, xa=xa: e.tensor_reduce(
                SMALL[:, 44:45], xa, AX, ALU.max), reads=[f"xr{tc}"], writes=["f44"])
            P.op("vector", lambda e, xa=xa: e.tensor_reduce(
                SMALL[:, 45:46], xa, AX, ALU.min), reads=[f"xr{tc}"], writes=["f45"])
            P.op("vector", lambda e, mcol=mcol: e.tensor_tensor(
                SMALL[:, 44:45], SMALL[:, 44:45], mcol, ALU.subtract),
                 reads=["f44", f"stm{tc}"], writes=["f44"])
            P.op("vector", lambda e, mcol=mcol: e.tensor_tensor(
                SMALL[:, 45:46], mcol, SMALL[:, 45:46], ALU.subtract),
                 reads=["f45", f"stm{tc}"], writes=["f45"])
            P.op("vector", lambda e: e.tensor_tensor(
                SMALL[:, 44:45], SMALL[:, 44:45], SMALL[:, 45:46], ALU.max),
                 reads=["f44", "f45"], writes=["f44"])
            P.op("vector", lambda e, rcol=rcol: e.tensor_tensor(
                SMALL[:, 45:46], SMALL[:, 44:45], rcol, ALU.mult),
                 reads=["f44", f"str{tc}"], writes=["f45"])
            P.op("vector", lambda e: e.tensor_scalar(
                SMALL[:, 46:47], SMALL[:, 45:46], SMALL[:, 43:44], 1e-5, ALU.mult, ALU.max),
                 reads=["f45", "f43"], writes=["f46"])
            P.op("vector", lambda e: e.reciprocal(SMALL[:, 47:48], SMALL[:, 46:47]),
                 reads=["f46"], writes=["f47"])
            P.op("vector", lambda e: e.tensor_scalar(
                SMALL[:, 47:48], SMALL[:, 47:48], 127.0, None, ALU.mult),
                 reads=["f47"], writes=["f47"])
            P.op("vector", lambda e, rcol=rcol: e.tensor_tensor(
                SMALL[:, 48:49], SMALL[:, 43:44], rcol, ALU.mult),
                 reads=["f43", f"str{tc}"], writes=["f48"])
            P.op("vector", lambda e: e.tensor_tensor(
                SMALL[:, 48:49], SMALL[:, 48:49], SMALL[:, 47:48], ALU.mult),
                 reads=["f48", "f47"], writes=["f48"])
            P.op("vector", lambda e, mcol=mcol: e.tensor_scalar(
                SMALL[:, 49:50], mcol, SMALL[:, 48:49], -1.0, ALU.mult, ALU.mult),
                 reads=[f"stm{tc}", "f48"], writes=["f49"])
            P.op("vector", lambda e: e.tensor_scalar(
                SMALL[:, 50:51], SMALL[:, 46:47], 1.0 / 127.0, C1C, ALU.mult, ALU.mult),
                 reads=["f46", "fswb"], writes=["f50"])
            P.op("vector", lambda e: e.tensor_scalar(
                SMALL[:, 51:52], SMALL[:, 47:48], SW1C, None, ALU.mult),
                 reads=["f47", "fswb"], writes=["f51"])
            P.op("tensor", lambda e: e.transpose(TR_ps[0:1, 16:144], SMALL[:, 51:52], ID32[:]),
                 reads=["f51", "ID32"], writes=["bkTR"])
            P.op("vector", lambda e: e.tensor_copy(U1R[0:1, 0:128], TR_ps[0:1, 16:144]),
                 reads=["bkTR"], writes=["u1r"])
            P.op("scalar", lambda e, xa=xa: e.activation(
                SCR[:, 0:1024], xa, AF.Identity, bias=SMALL[:, 49:50], scale=SMALL[:, 48:49]),
                 reads=[f"xr{tc}", "f48", "f49"], writes=["scrA"])
            P.op("vector", lambda e: e.tensor_scalar(
                PBF[:, 0:1024], SCR[:, 0:1024], MXC[:, 0:1], MXC[:, 0:1],
                ALU.add, ALU.subtract),
                 reads=["scrA", "MXC"], writes=["ps0", "ps1"])
            for c in range(8):
                P.op("tensor", lambda e, c=c: e.transpose(
                    TRbf[:, c * 128:(c + 1) * 128], PBF[:, c * 128:(c + 1) * 128], IDBF[:]),
                     reads=["ps0", "ps1", "IDBF"], writes=["bkTR"])
            P.op("vector", lambda e: e.tensor_copy(
                QKV[:, 0:1024], TRbf[:, 0:1024]),
                 reads=["bkTR"], writes=["qkvA"])
            P.op("vector", lambda e: e.memset(SMALL[:, 114:115], 0.0), writes=["fmx"])
            for fs in range(8):
                bk = fs % 4
                for c in range(8):
                    P.op("tensor", lambda e, c=c, fs=fs, bk=bk: e.matmul(
                        S_ps[:, bk * 512:bk * 512 + 512],
                        QKV[:, c * 128:(c + 1) * 128],
                        W1T(c)[:, fs * 512:(fs + 1) * 512],
                        start=(c == 0), stop=False),
                         reads=["qkvA", f"xlnf{c}"], writes=[SKEY[bk]])
                P.op("tensor", lambda e, fs=fs, bk=bk: e.matmul(
                    S_ps[:, bk * 512:bk * 512 + 512], U1R[0:1, 0:128],
                    ROWS[0:1, fs * 512:(fs + 1) * 512], start=False, stop=True),
                     reads=["u1r", "rows_bin"], writes=[SKEY[bk]])
                P.op("scalar", lambda e, fs=fs, bk=bk: e.activation(
                    ABIG[:, fs * 512:(fs + 1) * 512], S_ps[:, bk * 512:bk * 512 + 512],
                    AF.Gelu, scale=SMALL[:, 50:51]),
                     reads=[SKEY[bk], "f50"], writes=["abig"])
                P.op("vector", lambda e, fs=fs: e.bn_stats(
                    SMALL[:, 64 + 6 * fs:70 + 6 * fs], ABIG[:, fs * 512:(fs + 1) * 512]),
                     reads=["abig"], writes=[f"fbn{fs}"])
                P.op("vector", lambda e, fs=fs: e.tensor_reduce(
                    SMALL[:, 115:116], ABIG[:, fs * 512:(fs + 1) * 512], AX, ALU.max,
                    apply_absolute_value=True),
                     reads=["abig"], writes=["fred2"])
                P.op("vector", lambda e: e.tensor_tensor(
                    SMALL[:, 114:115], SMALL[:, 114:115], SMALL[:, 115:116], ALU.max),
                     reads=["fred2", "fmx"], writes=["fmx"])
            P.op("vector", lambda e: e.bn_aggr(SMALL[:, 112:114], SMALL[:, 64:112]),
                 reads=[f"fbn{fs}" for fs in range(8)], writes=["fmv2"])
            P.op("vector", lambda e: e.tensor_tensor(
                SMALL[:, 52:53], SMALL[:, 112:113], SMALL[:, 112:113], ALU.mult),
                 reads=["fmv2"], writes=["f52"])
            P.op("vector", lambda e: e.tensor_tensor(
                SMALL[:, 52:53], SMALL[:, 52:53], SMALL[:, 113:114], ALU.add),
                 reads=["f52", "fmv2"], writes=["f52"])
            P.op("scalar", lambda e: e.activation(SMALL[:, 53:54], SMALL[:, 52:53],
                                                  AF.Sqrt, bias=CEPS[:, 0:1]),
                 reads=["f52", "ceps"], writes=["f53"])
            P.op("vector", lambda e: e.reciprocal(SMALL[:, 54:55], SMALL[:, 53:54]),
                 reads=["f53"], writes=["f54"])
            P.op("vector", lambda e: e.tensor_tensor(
                SMALL[:, 55:56], SMALL[:, 54:55], SMALL[:, 114:115], ALU.mult),
                 reads=["f54", "fmx"], writes=["f55"])
            P.op("vector", lambda e: e.tensor_scalar(
                SMALL[:, 56:57], SMALL[:, 55:56], 1.0, 1e-5, ALU.mult, ALU.max),
                 reads=["f55"], writes=["f56"])
            P.op("vector", lambda e: e.reciprocal(SMALL[:, 57:58], SMALL[:, 56:57]),
                 reads=["f56"], writes=["f57"])
            P.op("vector", lambda e: e.tensor_scalar(
                SMALL[:, 57:58], SMALL[:, 57:58], 127.0, None, ALU.mult),
                 reads=["f57"], writes=["f57"])
            P.op("vector", lambda e: e.tensor_tensor(
                SMALL[:, 58:59], SMALL[:, 54:55], SMALL[:, 57:58], ALU.mult),
                 reads=["f54", "f57"], writes=["f58"])
            P.op("vector", lambda e: e.tensor_scalar(
                SMALL[:, 59:60], SMALL[:, 56:57], 1.0 / 127.0, C2C, ALU.mult, ALU.mult),
                 reads=["f56", "fswb"], writes=["f59"])
            P.op("vector", lambda e: e.tensor_scalar(
                SMALL[:, 60:61], SMALL[:, 57:58], SW2C, None, ALU.mult),
                 reads=["f57", "fswb"], writes=["f60"])
            P.op("tensor", lambda e: e.transpose(TR_ps[0:1, 16:144], SMALL[:, 60:61], ID32[:]),
                 reads=["f60", "ID32"], writes=["bkTR"])
            P.op("vector", lambda e: e.tensor_copy(U1R[0:1, 128:256], TR_ps[0:1, 16:144]),
                 reads=["bkTR"], writes=["u2r"])
            for fs in range(8):
                P.op("scalar", lambda e, fs=fs: e.activation(
                    SCR[:, 0:512], ABIG[:, fs * 512:(fs + 1) * 512],
                    AF.Identity, scale=SMALL[:, 58:59]),
                     reads=["abig", "f58"], writes=["scrA"])
                P.op("vector", lambda e: e.tensor_scalar(
                    PBF[:, 0:512], SCR[:, 0:512], MXC[:, 0:1], MXC[:, 0:1],
                    ALU.add, ALU.subtract),
                     reads=["scrA", "MXC"], writes=["ps0"])
                for c in range(4):
                    P.op("tensor", lambda e, c=c: e.transpose(
                        TRbf[:, c * 128:(c + 1) * 128], PBF[:, c * 128:(c + 1) * 128],
                        IDBF[:]),
                         reads=["ps0", "IDBF"], writes=["bkTR"])
                P.op("vector", lambda e, fs=fs: e.tensor_copy(
                    QKV[:, 1024 + fs * 512: 1024 + (fs + 1) * 512], TRbf[:, 0:512]),
                     reads=["bkTR"], writes=["qkvA", "qkvB", "qkvC"])
            pk = ["S0", "S1"]
            for half in range(2):
                for fc in range(32):
                    P.op("tensor", lambda e, fc=fc, half=half: e.matmul(
                        S_ps[:, half * 512:(half + 1) * 512],
                        QKV[:, 1024 + fc * 128: 1024 + (fc + 1) * 128],
                        W2T(fc)[:, half * 512:(half + 1) * 512],
                        start=(fc == 0), stop=False),
                         reads=["qkvA", "qkvB", "qkvC", f"ctxf{fc // 4}"], writes=[SKEY[half]])
                P.op("tensor", lambda e, half=half: e.matmul(
                    S_ps[:, half * 512:(half + 1) * 512], U1R[0:1, 128:256],
                    ROWS[0:1, 4096 + half * 512: 4096 + (half + 1) * 512],
                    start=False, stop=True),
                     reads=["u2r", "rows_bout"], writes=[SKEY[half]])
            P.op("vector", lambda e, xa=xa: e.scalar_tensor_tensor(
                xa, S_ps[:, 0:1024], SMALL[:, 59:60], xa, ALU.mult, ALU.add),
                 reads=pk + ["f59", f"xr{tc}"], writes=[f"xr{tc}"])

    STRIPBF = ABIG[:].bitcast(BF16)

    def emit_l1_attn():
        for si, s in enumerate("qkvo"):
            P.dma("gpsimd", lambda e, si=si, s=s: e.dma_start(
                ROWS[0:1, si * 1024:(si + 1) * 1024], a_b[s].unsqueeze(0)),
                writes=["rows_bin", "rows_bout"])
        P.dma("sync", lambda e: e.dma_start(SCR[0:32, 1024:1040], rel_e),
              writes=["scrB"])
        P.op("vector", lambda e: e.memset(SMALL[0:16, 94:95], 0.0), writes=["pbm"])
        for ch in range(8):
            w = 512 if ch < 7 else 511
            P.dma("sync", lambda e, ch=ch, w=w: e.dma_start(
                SCR[0:32, 0:w], c_oh[:, ch * 512:ch * 512 + w]), writes=["scrA"])
            P.op("tensor", lambda e, w=w: e.matmul(
                AX_ps[0:16, 0:w], SCR[0:32, 1024:1040], SCR[0:32, 0:w],
                start=True, stop=True),
                 reads=["scrA", "scrB"], writes=["bkAX"])
            P.op("vector", lambda e, w=w: e.tensor_copy(
                SCR[0:16, 1040:1040 + w],
                bass.AP(AX_ps[0:1, 0:1].tensor, w - 1, [[512, 16], [-1, w]])),
                 reads=["bkAX"], writes=["pbev"])
            P.dma("gpsimd", lambda e, ch=ch, w=w: e.dma_start(
                pb_dram[:, 4223 - ch * 512 - w: 4223 - ch * 512], SCR[0:16, 1040:1040 + w]),
                reads=["pbev"], writes=["pbd"])
            P.op("vector", lambda e, w=w: e.tensor_reduce(
                SMALL[0:16, 95:96], AX_ps[0:16, 0:w], AX, ALU.max,
                apply_absolute_value=True),
                 reads=["bkAX"], writes=["pbr"])
            P.op("vector", lambda e: e.tensor_tensor(
                SMALL[0:16, 94:95], SMALL[0:16, 94:95], SMALL[0:16, 95:96], ALU.max),
                 reads=["pbr", "pbm"], writes=["pbm"])
        P.op("vector", lambda e: e.memset(SCR[0:16, 1040:1553], 0.0), writes=["pbev"])
        P.dma("gpsimd", lambda e: e.dma_start(pb_dram[:, 0:128], SCR[0:16, 1040:1168]),
              reads=["pbev"], writes=["pbd"])
        P.dma("gpsimd", lambda e: e.dma_start(pb_dram[:, 4223:4608], SCR[0:16, 1040:1425]),
              reads=["pbev"], writes=["pbd"])
        P.op("tensor", lambda e: e.transpose(TR_ps[0:1, 0:16], SMALL[0:16, 94:95],
                                             ID32[0:16, 0:16]),
             reads=["pbm", "ID32"], writes=["bkTR"])
        P.op("vector", lambda e: e.tensor_copy(SMALL[0:1, 96:112], TR_ps[0:1, 0:16]),
             reads=["bkTR"], writes=["pbrow"])
        P.op("vector", lambda e: e.tensor_reduce(SMALL[0:1, 113:114], SMALL[0:1, 96:112],
                                                 AX, ALU.max),
             reads=["pbrow"], writes=["pbrow"])
        for j in range(2):
            P.dma("gpsimd", lambda e, j=j: e.dma_start(
                SMALL[j:j + 1, 112:113],
                bass.AP(SMALL[0:1, 0:1].tensor, 113, [[128, 1], [1, 1]])),
                reads=["pbrow"], writes=["pbm2"])
        P.dma("sync", lambda e: e.dma_start(SMALL[0:2, 104:112], gru_c2), writes=["cc2"])
        P.dma("sync", lambda e: e.dma_start(SCR[0:64, 1600:1608], gru_w), writes=["gruw"])
        P.dma("sync", lambda e: e.dma_start(SCR[0:8, 1608:1610], c_p4), writes=["gruw"])
        P.dma("sync", lambda e: e.dma_start(SCR[0:8, 1610:1611], gru_b.unsqueeze(1)),
              writes=["gruw"])
        P.op("tensor", lambda e: e.transpose(TR_ps[0:8, 64:128], SCR[0:64, 1600:1608],
                                             ID32[0:64, 0:64]),
             reads=["gruw", "ID32"], writes=["bkTR"])
        P.op("vector", lambda e: e.tensor_copy(SCR[0:8, 1612:1676], TR_ps[0:8, 64:128]),
             reads=["bkTR"], writes=["gruw2"])
        P.op("tensor", lambda e: e.matmul(TR_ps[0:64, 100:102], SCR[0:8, 1612:1676],
                                          SCR[0:8, 1608:1610], start=True, stop=True),
             reads=["gruw2", "gruw"], writes=["bkTR"])
        P.op("tensor", lambda e: e.matmul(TR_ps[64:128, 102:104], SCR[0:8, 1612:1676],
                                          SCR[0:8, 1608:1610], start=True, stop=True),
             reads=["gruw2", "gruw"], writes=["bkTR"])
        P.op("vector", lambda e: e.memset(GAB[:], 0.0), writes=["gab"])
        P.op("vector", lambda e: e.tensor_copy(GAB[0:64, 0:2], TR_ps[0:64, 100:102]),
             reads=["bkTR", "gab"], writes=["gab"])
        P.op("vector", lambda e: e.tensor_copy(GAB[64:128, 2:4], TR_ps[64:128, 102:104]),
             reads=["bkTR", "gab"], writes=["gab"])
        P.op("tensor", lambda e: e.matmul(TR_ps[0:1, 104:106], SCR[0:8, 1610:1611],
                                          SCR[0:8, 1608:1610], start=True, stop=True),
             reads=["gruw"], writes=["bkTR"])
        P.op("vector", lambda e: e.tensor_copy(SMALL[0:1, 88:90], TR_ps[0:1, 104:106]),
             reads=["bkTR"], writes=["bg2r"])
        for j in range(2):
            P.dma("gpsimd", lambda e, j=j: e.dma_start(
                SMALL[j:j + 1, 90:91], bass.AP(SMALL[0:1, 0:1].tensor, 88, [[128, 1], [1, 1]])),
                reads=["bg2r"], writes=["bga"])
            P.dma("gpsimd", lambda e, j=j: e.dma_start(
                SMALL[j:j + 1, 91:92], bass.AP(SMALL[0:1, 0:1].tensor, 89, [[128, 1], [1, 1]])),
                reads=["bg2r"], writes=["bgb"])

        for hp in range(8):
            hA = 2 * hp
            for si, s in enumerate("qkv"):
                for c in range(8):
                    P.dma("gpsimd", lambda e, s=s, si=si, hp=hp, c=c: e.dma_start(
                        WSL[:, si * 1024 + c * 128: si * 1024 + (c + 1) * 128],
                        a_w[s][c * 128:(c + 1) * 128, hp * 128:(hp + 1) * 128]),
                        writes=[f"wsl{s}"])
            for h01 in range(2):
                h = hA + h01
                P.dma("gpsimd", lambda e, h=h, h01=h01: e.dma_start(
                    STRIPBF[:, h01 * 4096:(h01 + 1) * 4096],
                    bass.AP(pb_dram.tensor, h * 4608, [[1, 128], [1, 4096]])),
                    reads=["pbd"], writes=[f"strip{h01}", "abig"])
            for si, s in enumerate("qkv"):
                for tc4 in range(4):
                    for c in range(8):
                        P.op("tensor", lambda e, si=si, c=c, tc4=tc4: e.matmul(
                            AX_ps[:, 0:512],
                            WSL[:, si * 1024 + c * 128: si * 1024 + (c + 1) * 128],
                            XLNF[c][:, tc4 * 512:(tc4 + 1) * 512],
                            start=(c == 0), stop=False),
                             reads=[f"wsl{s}", f"xlnf{c}"], writes=["bkAX"])
                    P.op("tensor", lambda e, si=si, hp=hp: e.matmul(
                        AX_ps[:, 0:512],
                        ROWS[0:1, si * 1024 + hp * 128: si * 1024 + (hp + 1) * 128],
                        ONER[0:1, 0:512], start=False, stop=True),
                         reads=["rows_bin", "ONER"], writes=["bkAX"])
                    P.op("scalar", lambda e, si=si, tc4=tc4: e.activation(
                        QKV[:, si * T + tc4 * 512: si * T + (tc4 + 1) * 512],
                        AX_ps[:, 0:512], AF.Identity, scale=(0.125 if si == 0 else 1.0)),
                         reads=["bkAX"], writes=[["qkvA", "qkvB", "qkvC"][si]])
            for g in range(4):
                TBv = TRbf if g % 2 == 0 else AXbf
                tbkv = "bkTR" if g % 2 == 0 else "bkAX"
                for b in range(4):
                    kt = g * 4 + b
                    P.op("tensor", lambda e, kt=kt, b=b, TBv=TBv: e.transpose(
                        TBv[:, b * 128:(b + 1) * 128],
                        QKV[:, 2 * T + kt * 128: 2 * T + (kt + 1) * 128], IDBF[:]),
                         reads=["qkvC", "IDBF"], writes=[tbkv])
                P.op("vector", lambda e, g=g, TBv=TBv: e.tensor_copy(
                    VT[:, g * 512:(g + 1) * 512], TBv[:, 0:512]),
                     reads=[tbkv], writes=["vt"])
            P.op("scalar", lambda e: e.activation(PBF[:, 0:2048], QKV[:, 0:T], AF.Square),
                 reads=["qkvA"], writes=["ps0", "ps1", "ps2", "ps3"])
            for tc4 in range(4):
                P.op("tensor", lambda e, tc4=tc4: e.matmul(
                    S_ps[0:2, tc4 * 512:(tc4 + 1) * 512], ES2[:],
                    PBF[:, tc4 * 512:(tc4 + 1) * 512], start=True, stop=True),
                     reads=["ps0", "ps1", "ps2", "ps3", "ES2"], writes=[SKEY[tc4]])
            P.op("scalar", lambda e: e.activation(SCR[0:2, 0:2048], S_ps[0:2, 0:2048],
                                                  AF.Sqrt, bias=CEPS[0:2, 1:2]),
                 reads=SKEY + ["ceps"], writes=["negB", "scrA", "scrB"])
            P.op("scalar", lambda e: e.activation(PBF[:, 0:2048], QKV[:, T:2 * T], AF.Square),
                 reads=["qkvB"], writes=["ps0", "ps1", "ps2", "ps3"])
            for tc4 in range(4):
                P.op("tensor", lambda e, tc4=tc4: e.matmul(
                    S_ps[0:2, tc4 * 512:(tc4 + 1) * 512], ES2[:],
                    PBF[:, tc4 * 512:(tc4 + 1) * 512], start=True, stop=True),
                     reads=["ps0", "ps1", "ps2", "ps3", "ES2"], writes=[SKEY[tc4]])
            P.op("vector", lambda e: e.tensor_reduce(SMALL[0:2, 20:21], S_ps[0:2, 0:2048],
                                                     AX, ALU.max),
                 reads=SKEY, writes=["kmax2"])
            P.op("scalar", lambda e: e.activation(SMALL[0:2, 21:22], SMALL[0:2, 20:21],
                                                  AF.Sqrt, bias=CEPS[0:2, 1:2]),
                 reads=["kmax2", "ceps"], writes=["kmax"])
            P.op("vector", lambda e: e.tensor_scalar(
                SCR[0:2, 0:2048], SCR[0:2, 0:2048], SMALL[0:2, 21:22], None, ALU.mult),
                 reads=["negB", "scrA", "scrB", "kmax"], writes=["negB", "scrA", "scrB"])
            pkeys = ["ps0", "ps1", "ps2", "ps3"]
            for tc4 in range(4):
                P.op("tensor", lambda e, tc4=tc4, hp=hp: e.matmul(
                    S_ps[0:2, tc4 * 512:(tc4 + 1) * 512], GAB[:, 1:4:2],
                    XLNF[hp][:, tc4 * 512:(tc4 + 1) * 512], start=True, stop=True),
                     reads=["gab", f"xlnf{hp}"], writes=[SKEY[tc4]])
            P.op("scalar", lambda e: e.activation(S_ps[0:2, 0:2048], S_ps[0:2, 0:2048],
                                                  AF.Sigmoid, bias=SMALL[0:2, 91:92]),
                 reads=SKEY + ["bgb"], writes=SKEY)
            P.op("vector", lambda e, hp=hp: e.tensor_scalar(
                PBF[0:2, 0:2048], S_ps[0:2, 0:2048], SMALL[0:2, 104 + hp:105 + hp], 1.0,
                ALU.mult, ALU.subtract),
                 reads=SKEY + ["cc2"], writes=pkeys)
            for tc4 in range(4):
                P.op("tensor", lambda e, tc4=tc4, hp=hp: e.matmul(
                    S_ps[0:2, tc4 * 512:(tc4 + 1) * 512], GAB[:, 0:3:2],
                    XLNF[hp][:, tc4 * 512:(tc4 + 1) * 512], start=True, stop=True),
                     reads=["gab", f"xlnf{hp}"], writes=[SKEY[tc4]])
            P.op("scalar", lambda e: e.activation(S_ps[0:2, 0:2048], S_ps[0:2, 0:2048],
                                                  AF.Sigmoid, bias=SMALL[0:2, 90:91]),
                 reads=SKEY + ["bga"], writes=SKEY)
            P.op("vector", lambda e: e.scalar_tensor_tensor(
                PBF[0:2, 0:2048], S_ps[0:2, 0:2048], 1.0, PBF[0:2, 0:2048],
                ALU.mult, ALU.mult),
                 reads=SKEY + pkeys, writes=pkeys)
            P.op("vector", lambda e: e.tensor_scalar(
                PBF[0:2, 0:2048], PBF[0:2, 0:2048], 2.0, None, ALU.add),
                 reads=pkeys, writes=pkeys)
            for qt in range(NT):
                P.op("tensor", lambda e, qt=qt: e.transpose(
                    TRbf[:, 0:2], PBF[0:2, qt * 128:(qt + 1) * 128], IDBF[0:2, 0:2]),
                     reads=pkeys + ["IDBF"], writes=["bkTR"])
                P.op("vector", lambda e, qt=qt, hA=hA: e.tensor_copy(
                    GT[:, qt * 16 + hA: qt * 16 + hA + 2], TRbf[:, 0:2]),
                     reads=["bkTR"], writes=[f"gt{qt}"])
            P.op("scalar", lambda e: e.activation(
                PBF[0:2, 0:2048], PBF[0:2, 0:2048], AF.Abs, scale=SMALL[0:2, 112:113]),
                 reads=pkeys + ["pbm2"], writes=pkeys)
            P.op("vector", lambda e: e.tensor_tensor(
                SCR[0:2, 0:2048], SCR[0:2, 0:2048], PBF[0:2, 0:2048], ALU.add),
                 reads=["negB", "scrA", "scrB"] + pkeys, writes=["negB", "scrA", "scrB"])
            for qt in range(NT):
                P.op("tensor", lambda e, qt=qt: e.transpose(
                    TR_ps[:, 0:2], SCR[0:2, qt * 128:(qt + 1) * 128], ID32[0:2, 0:2]),
                     reads=["negB", "scrA", "scrB", "ID32"], writes=["bkTR"])
                P.op("vector", lambda e, qt=qt, hA=hA: e.tensor_scalar(
                    BT[:, qt * 16 + hA: qt * 16 + hA + 2], TR_ps[:, 0:2], -1.0, None,
                    ALU.mult),
                     reads=["bkTR"], writes=[f"bt{qt}"])

            def l1_mm(qt, kc4):
                for h01 in range(2):
                    sbank = h01 * 2 + kc4 % 2
                    P.op("tensor", lambda e, h01=h01, qt=qt, kc4=kc4, sbank=sbank:
                         e.matmul(
                             S_ps[:, sbank * 512:(sbank + 1) * 512],
                             QKV[h01 * 64:(h01 + 1) * 64, qt * 128:(qt + 1) * 128],
                             QKV[h01 * 64:(h01 + 1) * 64, T + kc4 * 512: T + (kc4 + 1) * 512],
                             start=True, stop=True),
                         reads=["qkvA", "qkvB"], writes=[SKEY[sbank]])

            def l1_tailA(qt, kc4):
                for h01 in range(2):
                    h = hA + h01
                    sbank = h01 * 2 + kc4 % 2
                    scol = sbank * 512
                    pslot = h01 * 1024 + (kc4 % 2) * 512
                    pkey = f"ps{h01 * 2 + kc4 % 2}"
                    stro = h01 * 4096 + 2175 + 128 * qt - kc4 * 512
                    P.op("vector", lambda e, scol=scol, qt=qt, h=h, stro=stro:
                         e.scalar_tensor_tensor(
                             S_ps[:, scol:scol + 512],
                             bass.AP(STRIPBF.tensor, stro, [[8192, 128], [-1, 512]]),
                             GT[:, qt * 16 + h: qt * 16 + h + 1],
                             S_ps[:, scol:scol + 512], ALU.mult, ALU.add),
                         reads=[SKEY[sbank], f"strip{h01}", "abig", f"gt{qt}"],
                         writes=[SKEY[sbank]])
                    zc = 24 + h01 * 4 + kc4
                    P.op("scalar", lambda e, scol=scol, pslot=pslot, qt=qt, h=h, zc=zc:
                         e.activation(
                             PBF[:, pslot:pslot + 512], S_ps[:, scol:scol + 512], AF.Exp,
                             bias=BT[:, qt * 16 + h: qt * 16 + h + 1],
                             accum_out=SMALL[:, zc: zc + 1]),
                         reads=[SKEY[sbank], f"bt{qt}"],
                         writes=[pkey, f"zz{h01}_{kc4}"])

            def l1_tailB(qt, kc4):
                TB = TRbf if qt % 2 == 0 else AVbf
                tbk = "bkTR" if qt % 2 == 0 else "bkAV"
                for h01 in range(2):
                    pslot = h01 * 1024 + (kc4 % 2) * 512
                    pkey = f"ps{h01 * 2 + kc4 % 2}"
                    if kc4 % 2 == 1:
                        zc = 24 + h01 * 4 + kc4
                        dst = 32 + h01 + (kc4 // 2) * 4
                        P.op("vector", lambda e, zc=zc, dst=dst: e.tensor_tensor(
                            SMALL[:, dst:dst + 1],
                            SMALL[:, zc - 1:zc], SMALL[:, zc:zc + 1], ALU.add),
                             reads=[f"zz{h01}_{kc4 - 1}", f"zz{h01}_{kc4}"],
                             writes=[f"zp{h01}_{kc4 // 2}"])
                    for bb in range(4):
                        P.op("tensor", lambda e, pslot=pslot, bb=bb, h01=h01, TB=TB:
                             e.transpose(
                                 TB[:, (h01 * 4 + bb) * 128:(h01 * 4 + bb + 1) * 128],
                                 PBF[:, pslot + bb * 128: pslot + (bb + 1) * 128], IDBF[:]),
                             reads=[pkey, "IDBF"], writes=[tbk])
                    P.op("vector", lambda e, h01=h01, TB=TB, kc4=kc4: e.tensor_copy(
                        bass.AP(PTT[0:1, 0:1].tensor, (kc4 * 4) * 256 + h01 * 128,
                                [[4096, 128], [256, 4], [1, 128]]),
                        TB[:, h01 * 512:(h01 + 1) * 512]),
                         reads=[tbk],
                         writes=[f"pt{kc4 * 4}_{h01}", f"pt{kc4 * 4 + 1}_{h01}",
                                 f"pt{kc4 * 4 + 2}_{h01}", f"pt{kc4 * 4 + 3}_{h01}"])

            def l1_tailC(qt):
                TB = TRbf if qt % 2 == 0 else AVbf
                tbk = "bkTR" if qt % 2 == 0 else "bkAV"
                for h01 in range(2):
                    P.op("vector", lambda e, h01=h01: e.tensor_tensor(
                        SMALL[:, 40 + h01:41 + h01], SMALL[:, 32 + h01:33 + h01],
                        SMALL[:, 36 + h01:37 + h01], ALU.add),
                         reads=[f"zp{h01}_0", f"zp{h01}_1"], writes=[f"zt{h01}"])
                    P.op("vector", lambda e, h01=h01: e.reciprocal(
                        SMALL[:, 42 + h01:43 + h01], SMALL[:, 40 + h01:41 + h01]),
                         reads=[f"zt{h01}"], writes=[f"zr{h01}"])
                    for kcg in range(16):
                        P.op("tensor", lambda e, kcg=kcg, h01=h01: e.matmul(
                            AY_ps[:, 0:64],
                            PTT[:, kcg * 256 + h01 * 128: kcg * 256 + h01 * 128 + 128],
                            VT[:, kcg * 128 + h01 * 64: kcg * 128 + h01 * 64 + 64],
                            start=(kcg == 0), stop=(kcg == 15)),
                             reads=[f"pt{kcg}_{h01}", "vt"], writes=["AY"])
                    P.op("vector", lambda e, h01=h01, qt=qt: e.tensor_scalar(
                        CTM[:, (h01 * 2 + qt % 2) * 64:(h01 * 2 + qt % 2) * 64 + 64],
                        AY_ps[:, 0:64], SMALL[:, 42 + h01:43 + h01], None, ALU.mult),
                         reads=["AY", f"zr{h01}"], writes=[f"ctm{h01}_{qt % 2}"])
                    tcol = 768 if h01 == 0 else 896
                    P.op("tensor", lambda e, h01=h01, qt=qt, TB=TB, tcol=tcol:
                         e.transpose(
                             TB[0:64, tcol:tcol + 128],
                             CTM[:, (h01 * 2 + qt % 2) * 64:(h01 * 2 + qt % 2) * 64 + 64],
                             IDBF[:]),
                         reads=[f"ctm{h01}_{qt % 2}", "IDBF"], writes=[tbk])
                    P.op("vector", lambda e, h01=h01, qt=qt, hp=hp, TB=TB, tcol=tcol:
                         e.tensor_copy(
                             CTXF[hp][h01 * 64:(h01 + 1) * 64, qt * 128:(qt + 1) * 128],
                             TB[0:64, tcol:tcol + 128]),
                         reads=[tbk], writes=[f"ctxf{hp}"])

            stages = [(qt, kc4) for qt in range(NT) for kc4 in range(4)]
            for i in range(len(stages) + 3):
                if i < len(stages):
                    l1_mm(*stages[i])
                if 0 <= i - 1 < len(stages):
                    l1_tailA(*stages[i - 1])
                if 0 <= i - 2 < len(stages):
                    l1_tailB(*stages[i - 2])
                    if stages[i - 2][1] == 3:
                        l1_tailC(stages[i - 2][0])

        def wot_ap(cc):
            if cc < 6:
                return QKV[:, cc * 1024:(cc + 1) * 1024]
            return WSL[:, (cc - 6) * 1024:(cc - 5) * 1024]

        def wot_alias(cc):
            return ["qkvA", "qkvA", "qkvB", "qkvB", "qkvC", "qkvC", "wslq", "wslk"][cc]
        for cc in range(8):
            P.dma("gpsimd", lambda e, cc=cc: e.dma_start(
                wot_ap(cc), a_w["o"][cc * 128:(cc + 1) * 128, :]),
                writes=["wotx", wot_alias(cc)])
        for t in range(NT):
            pk = ["S0", "S1"]
            for c in range(8):
                for half in range(2):
                    P.op("tensor", lambda e, c=c, t=t, half=half: e.matmul(
                        S_ps[:, half * 512:(half + 1) * 512],
                        CTXF[c][:, t * 128:(t + 1) * 128],
                        wot_ap(c)[:, half * 512:(half + 1) * 512],
                        start=(c == 0), stop=False),
                         reads=[f"ctxf{c}", "wotx"], writes=pk)
            for half in range(2):
                P.op("tensor", lambda e, half=half: e.matmul(
                    S_ps[:, half * 512:(half + 1) * 512], ONER[0:1, 0:128],
                    ROWS[0:1, 3072 + half * 512: 3072 + (half + 1) * 512],
                    start=False, stop=True),
                     reads=["ONER", "rows_bout"], writes=pk)
            P.op("vector", lambda e, t=t: e.tensor_tensor(
                XRES[:, t * D:(t + 1) * D], XRES[:, t * D:(t + 1) * D],
                S_ps[:, 0:1024], ALU.add),
                 reads=pk + [f"xr{t}"], writes=[f"xr{t}"])

    SIG7 = 1.0 / (1.0 + float(np.exp(-1.702 * 7.0)))

    def emit_l1_moe(nexp=NE, router_only=False, stage=4):
        ABF32 = ABIG
        DNB = None
        if not router_only and stage >= 4:
            P.dma("gpsimd", lambda e: e.dma_start(QKV[0:16, 5120:6144], m_dnb),
                  writes=["dnball"])
        for c in range(8):
            P.dma("gpsimd", lambda e, c=c: e.dma_start(
                WSL[:, c * 16:(c + 1) * 16], r_w[c * 128:(c + 1) * 128, :]),
                writes=["rw"])
        for tc4 in range(4):
            for c in range(8):
                P.op("tensor", lambda e, c=c, tc4=tc4: e.matmul(
                    AX_ps[0:16, 0:512], WSL[:, c * 16:(c + 1) * 16],
                    XLNF[c][:, tc4 * 512:(tc4 + 1) * 512],
                    start=(c == 0), stop=(c == 7)),
                     reads=["rw", f"xlnf{c}"], writes=["bkAX"])
            P.op("vector", lambda e, tc4=tc4: e.tensor_copy(
                ABIG[0:16, tc4 * 512:(tc4 + 1) * 512], AX_ps[0:16, 0:512]),
                 reads=["bkAX"], writes=["logit"])
        for tc in range(NT):
            P.op("tensor", lambda e, tc=tc: e.transpose(
                TR_ps[:, 0:16], ABIG[0:16, tc * 128:(tc + 1) * 128], ID32[0:16, 0:16]),
                 reads=["logit", "ID32"], writes=["bkTR"])
            P.op("vector", lambda e: e.tensor_copy(SCR[:, 0:16], TR_ps[:, 0:16]),
                 reads=["bkTR"], writes=["lt"])
            P.op("vector", lambda e: e.tensor_reduce(SMALL[:, 44:45], SCR[:, 0:16],
                                                     AX, ALU.max),
                 reads=["lt"], writes=["m1"])
            P.op("vector", lambda e: e.tensor_scalar(
                SCR[:, 16:32], SCR[:, 0:16], SMALL[:, 44:45], None, ALU.is_ge),
                 reads=["lt", "m1"], writes=["eq1"])
            P.op("vector", lambda e: e.scalar_tensor_tensor(
                SCR[:, 32:48], SCR[:, 16:32], -1.0e30, SCR[:, 0:16], ALU.mult, ALU.add),
                 reads=["eq1", "lt"], writes=["l2"])
            P.op("vector", lambda e: e.tensor_reduce(SMALL[:, 45:46], SCR[:, 32:48],
                                                     AX, ALU.max),
                 reads=["l2"], writes=["m2"])
            P.op("vector", lambda e: e.tensor_scalar(
                SCR[:, 48:64], SCR[:, 32:48], SMALL[:, 45:46], None, ALU.is_ge),
                 reads=["l2", "m2"], writes=["eq2"])
            P.op("vector", lambda e: e.tensor_tensor(
                SMALL[:, 46:47], SMALL[:, 45:46], SMALL[:, 44:45], ALU.subtract),
                 reads=["m1", "m2"], writes=["md"])
            P.op("scalar", lambda e: e.activation(SMALL[:, 47:48], SMALL[:, 46:47], AF.Exp),
                 reads=["md"], writes=["dd"])
            P.op("vector", lambda e: e.tensor_scalar(
                SMALL[:, 48:49], SMALL[:, 47:48], 1.0, None, ALU.add),
                 reads=["dd"], writes=["dn"])
            P.op("vector", lambda e: e.reciprocal(SMALL[:, 49:50], SMALL[:, 48:49]),
                 reads=["dn"], writes=["w1"])
            P.op("vector", lambda e: e.tensor_scalar(
                SMALL[:, 50:51], SMALL[:, 49:50], -1.0, 1.0, ALU.mult, ALU.add),
                 reads=["w1"], writes=["w2"])
            P.op("vector", lambda e: e.tensor_scalar(
                SCR[:, 48:64], SCR[:, 48:64], SMALL[:, 50:51], None, ALU.mult),
                 reads=["eq2", "w2"], writes=["eq2"])
            P.op("vector", lambda e, tc=tc: e.scalar_tensor_tensor(
                GT[:, tc * 16:(tc + 1) * 16], SCR[:, 16:32], SMALL[:, 49:50],
                SCR[:, 48:64], ALU.mult, ALU.add),
                 reads=["eq1", "w1", "eq2"], writes=[f"sc{tc}"])
            P.op("tensor", lambda e, tc=tc: e.transpose(
                TR_ps[0:16, 0:128], GT[:, tc * 16:(tc + 1) * 16], ID32[:]),
                 reads=[f"sc{tc}", "ID32"], writes=["bkTR"])
            P.op("vector", lambda e, tc=tc: e.tensor_copy(
                VT[0:16, tc * 128:(tc + 1) * 128], TR_ps[0:16, 0:128]),
                 reads=["bkTR"], writes=["sbt"])

        def gu_ap(par, c):
            return CTXF[par * 4 + c // 2][:, (c % 2) * 1024:(c % 2 + 1) * 1024]

        if router_only:
            return
        if stage >= 4:
            P.op("vector", lambda e: e.tensor_scalar(
                GT[:, 0:256], GT[:, 0:256], 1.0 / 1.702, None, ALU.mult),
                 reads=[f"sc{t}" for t in range(NT)],
                 writes=[f"sc{t}" for t in range(NT)])

            def gup(it):
                return S_ps[:, (it % 2) * 1024:(it % 2) * 1024 + 1024]

            def gupk(it):
                return ["S0", "S1"] if it % 2 == 0 else ["S2", "S3"]

            def dn_ap(ei, c4):
                if ei % 2 == 0:
                    return QKV[:, c4 * 1024:(c4 + 1) * 1024]
                return WSL[:, c4 * 1024:(c4 + 1) * 1024] if c4 < 3 else PBF[:, 1024:2048]

            def emit_head(it):
                ei, tc = divmod(it, NT)
                par = ei % 2
                if tc == 0:
                    for c in range(8):
                        P.dma("gpsimd", lambda e, ei=ei, c=c, par=par: e.dma_start(
                            gu_ap(par, c), m_gu[ei, c * 128:(c + 1) * 128, :]),
                            writes=[f"guw{par}_{c}"])
                    for c4 in range(4):
                        P.dma("gpsimd", lambda e, ei=ei, c4=c4: e.dma_start(
                            dn_ap(ei, c4), m_dn[ei, c4 * 128:(c4 + 1) * 128, :]),
                            writes=[f"dnw{par}_{c4}"])
                    P.dma("gpsimd", lambda e, ei=ei, par=par: e.dma_start(
                        ROWS[0:1, par * 1024:par * 1024 + 1024], m_gub[ei:ei + 1, :]),
                        writes=[f"gub{par}"])
                gk = gupk(it)
                gp = gup(it)
                for c in range(8):
                    for half in range(2):
                        P.op("tensor", lambda e, c=c, tc=tc, half=half, par=par,
                             gp=gp: e.matmul(
                            gp[:, half * 512:(half + 1) * 512],
                            XLNF[c][:, tc * 128:(tc + 1) * 128],
                            gu_ap(par, c)[:, half * 512:(half + 1) * 512],
                            start=(c == 0), stop=False),
                             reads=[f"guw{par}_{c}", f"xlnf{c}"], writes=[gk[half]])
                for half in range(2):
                    P.op("tensor", lambda e, half=half, par=par, gp=gp: e.matmul(
                        gp[:, half * 512:(half + 1) * 512], ONER[0:1, 0:128],
                        ROWS[0:1, par * 1024 + half * 512: par * 1024 + (half + 1) * 512],
                        start=False, stop=True),
                         reads=[f"gub{par}", "ONER"], writes=[gk[half]])

            def emit_tail(it):
                ei, tc = divmod(it, NT)
                par = ei % 2
                sl = it % 2
                gk = gupk(it)
                GUp = gup(it)
                P.op("vector", lambda e, GUp=GUp: e.tensor_scalar(
                    SCR[:, 0:512], GUp[:, 0:1024:2], 7.0, None, ALU.min),
                     reads=gk, writes=["gtw"])
                P.op("scalar", lambda e: e.activation(
                    SCR[:, 512:1024], SCR[:, 0:512], AF.Silu, scale=1.702),
                     reads=["gtw"], writes=["sig"])
                P.op("vector", lambda e, GUp=GUp: e.tensor_scalar(
                    SCR[:, 1024:1536], GUp[:, 1:1024:2], -7.0, 7.0, ALU.max, ALU.min),
                     reads=gk, writes=["upw"])
                P.op("vector", lambda e, tc=tc, ei=ei: e.tensor_scalar(
                    SCR[:, 1024:1536], SCR[:, 1024:1536],
                    GT[:, tc * 16 + ei: tc * 16 + ei + 1],
                    GT[:, tc * 16 + ei: tc * 16 + ei + 1], ALU.mult, ALU.add),
                     reads=["upw", f"sc{tc}"], writes=["upw"])
                P.op("vector", lambda e: e.tensor_tensor(
                    PBF[:, 0:512], SCR[:, 1024:1536], SCR[:, 512:1024], ALU.mult),
                     reads=["upw", "sig"], writes=["ps0"])
                for c in range(4):
                    P.op("tensor", lambda e, c=c: e.transpose(
                        TRbf[:, c * 128:(c + 1) * 128], PBF[:, c * 128:(c + 1) * 128],
                        IDBF[:]),
                         reads=["ps0", "IDBF"], writes=["bkTR"])
                P.op("vector", lambda e, sl=sl: e.tensor_copy(
                    QKV[:, 4096 + sl * 512: 4608 + sl * 512], TRbf[:, 0:512]),
                     reads=["bkTR"], writes=[f"actw{sl}"])
                for half in range(2):
                    dst = AX_ps if half == 0 else AV_ps
                    dk = "bkAX" if half == 0 else "bkAV"
                    for c4 in range(4):
                        P.op("tensor", lambda e, c4=c4, half=half, sl=sl, ei=ei,
                             dst=dst: e.matmul(
                            dst[:, 0:512],
                            QKV[:, 4096 + sl * 512 + c4 * 128:
                                 4096 + sl * 512 + (c4 + 1) * 128],
                            dn_ap(ei, c4)[:, half * 512:(half + 1) * 512],
                            start=(c4 == 0), stop=(c4 == 3 and ei != 0)),
                             reads=[f"actw{sl}", f"dnw{par}_{c4}"], writes=[dk])
                    if ei == 0:
                        P.op("tensor", lambda e, half=half, tc=tc, dst=dst: e.matmul(
                            dst[:, 0:512], VT[0:16, tc * 128:(tc + 1) * 128],
                            QKV[0:16, 5120 + half * 512: 5120 + (half + 1) * 512],
                            start=False, stop=True),
                             reads=["sbt", "dnball"], writes=[dk])
                P.op("vector", lambda e, tc=tc: e.tensor_tensor(
                    XRES[:, tc * D:tc * D + 512], XRES[:, tc * D:tc * D + 512],
                    AX_ps[:, 0:512], ALU.add),
                     reads=["bkAX", f"xr{tc}"], writes=[f"xr{tc}"])
                P.op("vector", lambda e, tc=tc: e.tensor_tensor(
                    XRES[:, tc * D + 512:tc * D + 1024],
                    XRES[:, tc * D + 512:tc * D + 1024],
                    AV_ps[:, 0:512], ALU.add),
                     reads=["bkAV", f"xr{tc}"], writes=[f"xr{tc}"])

            for it in range(nexp * NT):
                emit_head(it)
                if it >= 1:
                    emit_tail(it - 1)
            emit_tail(nexp * NT - 1)
            return
        for ei in range(nexp):
            par = ei % 2
            rbase = par * 1024
            for c in range(8):
                P.dma("gpsimd", lambda e, ei=ei, c=c, par=par: e.dma_start(
                    gu_ap(par, c), m_gu[ei, c * 128:(c + 1) * 128, :]),
                    writes=[f"guw{par}_{c}"])
            if stage >= 4:
                for c4 in range(4):
                    P.dma("gpsimd", lambda e, ei=ei, c4=c4: e.dma_start(
                        QKV[:, c4 * 1024:(c4 + 1) * 1024],
                        m_dn[ei, c4 * 128:(c4 + 1) * 128, :]),
                        writes=[f"dnw{c4}"])
            P.dma("gpsimd", lambda e, ei=ei, rbase=rbase: e.dma_start(
                ROWS[0:1, rbase:rbase + 1024], m_gub[ei:ei + 1, :]),
                writes=[f"gub{par}"])
            for tc in range(NT):
                gk = ["S0", "S1"]
                for half in range(2):
                    for c in range(8):
                        P.op("tensor", lambda e, c=c, tc=tc, half=half, par=par: e.matmul(
                            S_ps[:, half * 512:(half + 1) * 512],
                            XLNF[c][:, tc * 128:(tc + 1) * 128],
                            gu_ap(par, c)[:, half * 512:(half + 1) * 512],
                            start=(c == 0), stop=False),
                             reads=[f"guw{par}_{c}", f"xlnf{c}"], writes=[gk[half]])
                    P.op("tensor", lambda e, half=half, rbase=rbase: e.matmul(
                        S_ps[:, half * 512:(half + 1) * 512], ONER[0:1, 0:128],
                        ROWS[0:1, rbase + half * 512: rbase + (half + 1) * 512],
                        start=False, stop=True),
                         reads=[f"gub{par}", "ONER"], writes=[gk[half]])
                if stage < 2:
                    P.op("vector", lambda e, tc=tc: e.tensor_tensor(
                        XRES[:, tc * D:(tc + 1) * D], XRES[:, tc * D:(tc + 1) * D],
                        S_ps[:, 0:1024], ALU.add),
                         reads=gk + [f"xr{tc}"], writes=[f"xr{tc}"])
                    continue
                GUp = S_ps[:, 0:1024]
                P.op("vector", lambda e: e.tensor_scalar(
                    SCR[:, 0:512], GUp[:, 0:1024:2], 7.0, None, ALU.min),
                     reads=gk, writes=["gtw"])
                P.op("scalar", lambda e: e.activation(
                    SCR[:, 512:1024], SCR[:, 0:512], AF.Sigmoid, scale=1.702),
                     reads=["gtw"], writes=["sig"])
                P.op("vector", lambda e, tc=tc, ei=ei: e.scalar_tensor_tensor(
                    SCR[:, 0:512], SCR[:, 0:512],
                    GT[:, tc * 16 + ei: tc * 16 + ei + 1], SCR[:, 512:1024],
                    ALU.mult, ALU.mult),
                     reads=["sig", "gtw", f"sc{tc}"], writes=["gtw"])
                P.op("vector", lambda e: e.tensor_scalar(
                    SCR[:, 512:1024], GUp[:, 1:1024:2], -7.0, 7.0, ALU.max, ALU.min),
                     reads=gk, writes=["sig"])
                P.op("vector", lambda e: e.scalar_tensor_tensor(
                    PBF[:, 0:512], SCR[:, 512:1024], 1.0, SCR[:, 0:512],
                    ALU.add, ALU.mult),
                     reads=["sig", "gtw"], writes=["ps0"])
                if 2 <= stage < 3:
                    P.op("vector", lambda e, tc=tc: e.tensor_tensor(
                        XRES[:, tc * D:(tc + 1) * D][:, 0:512],
                        XRES[:, tc * D:(tc + 1) * D][:, 0:512],
                        PBF[:, 0:512], ALU.add),
                         reads=["ps0", f"xr{tc}"], writes=[f"xr{tc}"])
                    continue
                for c in range(4):
                    P.op("tensor", lambda e, c=c: e.transpose(
                        TRbf[:, c * 128:(c + 1) * 128], PBF[:, c * 128:(c + 1) * 128],
                        IDBF[:]),
                         reads=["ps0", "IDBF"], writes=["bkTR"])
                for c in range(4):
                    P.op("vector", lambda e, c=c, par=par: e.tensor_copy(
                        QKV[:, 4096 + par * 512 + c * 128: 4096 + par * 512 + (c + 1) * 128],
                        TRbf[:, c * 128:(c + 1) * 128]),
                         reads=["bkTR"], writes=[f"actw{par}"])
                if stage == 3:
                    P.op("vector", lambda e, tc=tc, par=par: e.tensor_tensor(
                        XRES[:, tc * D:(tc + 1) * D][:, 0:512],
                        XRES[:, tc * D:(tc + 1) * D][:, 0:512],
                        QKV[:, 4096 + par * 512: 4608 + par * 512], ALU.add),
                         reads=[f"actw{par}", f"xr{tc}"], writes=[f"xr{tc}"])
                    continue
                dk = ["S2", "S3"]
                for half in range(2):
                    for c4 in range(4):
                        P.op("tensor", lambda e, c4=c4, half=half, par=par, ei=ei: e.matmul(
                            S_ps[:, 1024 + half * 512: 1024 + (half + 1) * 512],
                            QKV[:, 4096 + par * 512 + c4 * 128: 4096 + par * 512 + (c4 + 1) * 128],
                            QKV[:, c4 * 1024 + half * 512: c4 * 1024 + (half + 1) * 512],
                            start=(c4 == 0), stop=(c4 == 3 and ei != 0)),
                             reads=[f"actw{par}", f"dnw{c4}"], writes=[dk[half]])
                    if ei == 0:
                        P.op("tensor", lambda e, half=half, tc=tc: e.matmul(
                            S_ps[:, 1024 + half * 512: 1024 + (half + 1) * 512],
                            VT[0:16, tc * 128:(tc + 1) * 128],
                            QKV[0:16, 5120 + half * 512: 5120 + (half + 1) * 512],
                            start=False, stop=True),
                             reads=["sbt", "dnball"], writes=[dk[half]])
                P.op("vector", lambda e, tc=tc: e.tensor_tensor(
                    XRES[:, tc * D:(tc + 1) * D], XRES[:, tc * D:(tc + 1) * D],
                    S_ps[:, 1024:2048], ALU.add),
                     reads=dk + [f"xr{tc}"], writes=[f"xr{tc}"])

    def emit_out():
        outs = []
        for t in range(NT):
            outs.append(P.dma("gpsimd", lambda e, t=t: e.dma_start(
                y_out[t * 128:(t + 1) * 128, :], XRES[:, t * D:(t + 1) * D]),
                reads=[f"xr{t}"]))
        P.op("gpsimd", lambda e: e.engine_nop(), extra_deps=outs)

    emit_ln(0, fm=True)
    if upto >= 1:
        emit_l0_attn()
    if upto >= 1.5:
        emit_ln(1, fm=False)
        if debug:
            _x1d = nc.dram_tensor("dbg_x1", [T, D], F32, kind="ExternalOutput").ap()
            _x1deps = []
            for t in range(NT):
                _x1deps.append(P.dma("gpsimd", lambda e, t=t: e.dma_start(
                    _x1d[t * 128:(t + 1) * 128, :], XRES[:, t * D:(t + 1) * D]),
                    reads=[f"xr{t}"]))
            P.op("gpsimd", lambda e: e.engine_nop(), extra_deps=_x1deps)
            _std = nc.dram_tensor("dbg_stat1", [128, 48], F32, kind="ExternalOutput").ap()
            _i = P.dma("gpsimd", lambda e: e.dma_start(_std, STAT[:, 0:48]),
                       reads=[f"stb{t}" for t in range(NT)] + [f"str{t}" for t in range(NT)])
            P.op("gpsimd", lambda e: e.engine_nop(), extra_deps=[_i])
        if upto >= 1.7:
            emit_l0_ffn()
    if upto >= 3:
        emit_ln(2, fm=True)
        emit_l1_attn()
    if upto >= 3.5:
        emit_ln(3, fm=True)
        _stage = {3.71: 1, 3.72: 2, 3.721: 2.1, 3.722: 2.2, 3.7225: 2.25, 3.723: 2.3,
                  3.724: 2.4, 3.73: 3}.get(upto, 4)
        emit_l1_moe(nexp=(16 if upto >= 4 else (2 if upto >= 3.7 else 0)),
                    router_only=(upto < 3.7), stage=_stage)
    emit_out()

    if debug:
        def dump(name, ap, reads):
            dt_ = nc.dram_tensor(name, list(ap.shape), F32, kind="ExternalOutput").ap()
            i = P.dma("gpsimd", lambda e: e.dma_start(dt_, ap), reads=reads)
            P.op("gpsimd", lambda e: e.engine_nop(), extra_deps=[i])
        dump("dbg_xlnf0", XLNF[0][:], ["xlnf0"])
        dump("dbg_stat", STAT[:, 0:48], ["stb15"])
        if upto == 1:
            dump("dbg_ctxf0", CTXF[0][:], ["ctxf0"])
            dump("dbg_qkv", QKV[:], ["qkvA", "qkvB", "qkvC"])
            dump("dbg_bt", BT[:], ["bt15"])
            dump("dbg_vt", VT[:], ["vt"])
        if upto == 2:
            dump("dbg_hg", ABIG[:], ["abig" for fs in range(8)])
            dump("dbg_fsc", SMALL[:, 40:61], ["f59", "f58", "f50", "f48"])
            dump("dbg_w1t", XLNF[0][:].bitcast(FP8), ["xlnf0"])
            dump("dbg_w2t", CTXF[0][:].bitcast(FP8), ["ctxf0"])
            dump("dbg_hqf", QKV[:, 1024:5120], ["qkvA", "qkvB", "qkvC"])
            dump("dbg_sw", SMALL[:, 120:127], ["fswb", "fac"])

    P.materialize(nc, ctx)
    ctx.close()
    return nc


def pack_inputs(inputs):
    import ml_dtypes
    ins = {
        "w_in": inputs["L0_attn_w_in"], "b_in": inputs["L0_attn_b_in"],
        "w_out": inputs["L0_attn_w_out"], "b_out": inputs["L0_attn_b_out"],
        "f_w1": inputs["L0_ffn_w1"], "f_b1": inputs["L0_ffn_b1"],
        "f_w2": inputs["L0_ffn_w2"], "f_b2": inputs["L0_ffn_b2"],
        "a_wq": inputs["L1_attn_Wq"], "a_bq": inputs["L1_attn_bq"],
        "a_wk": inputs["L1_attn_Wk"], "a_bk": inputs["L1_attn_bk"],
        "a_wv": inputs["L1_attn_Wv"], "a_bv": inputs["L1_attn_bv"],
        "a_wo": inputs["L1_attn_Wo"], "a_bo": inputs["L1_attn_bo"],
        "rel_e": inputs["L1_attn_rel_embed"],
        "gru_w": inputs["L1_attn_gru_w"], "gru_b": inputs["L1_attn_gru_b"],
        "gru_c": np.asarray(inputs["L1_attn_gru_const"]).reshape(16),
        "gru_c2": np.asarray(inputs["L1_attn_gru_const"]).reshape(8, 2).T,
        "r_w": inputs["L1_ffn_router_weight"],
        "m_gu": inputs["L1_ffn_gate_up_proj"], "m_gub": inputs["L1_ffn_gate_up_proj_bias"],
        "m_dn": inputs["L1_ffn_down_proj"], "m_dnb": inputs["L1_ffn_down_proj_bias"],
        "c_oh": _bucket_onehot(),
        "c_id32": np.eye(128, dtype=np.float32),
        "c_idbf": np.eye(128, dtype=ml_dtypes.bfloat16),
        "c_p4": np.kron(np.eye(2, dtype=np.float32), np.ones((4, 1), np.float32)),
    }
    es2 = np.zeros((128, 2), dtype=ml_dtypes.bfloat16)
    es2[0:64, 0] = 1; es2[64:128, 1] = 1
    ins["c_es2"] = es2
    mi, mf, ml = _l0_masks()
    ins["c_mi"] = mi.astype(ml_dtypes.bfloat16)
    ins["c_mf"] = mf.astype(ml_dtypes.bfloat16)
    ins["c_ml"] = ml.astype(ml_dtypes.bfloat16)
    return {k: np.ascontiguousarray(v) for k, v in ins.items()}


def kernel(**inputs):
    ins = pack_inputs(inputs)
    x = np.ascontiguousarray(inputs["L0_attn_x"], dtype=np.float32)
    nc = build_program()
    in_maps = []
    for c in range(8):
        m = dict(ins)
        m["x"] = np.ascontiguousarray(x[c])
        in_maps.append(m)
    res = run_bass_kernel_spmd(nc, in_maps, core_ids=list(range(8)))
    out = np.stack([np.asarray(res.results[c]["y"]) for c in range(8)], axis=0)
    return out.astype(np.float32)
